# version 06
# baseline (speedup 1.0000x reference)
import numpy as np
import ml_dtypes
from contextlib import ExitStack, contextmanager
import concourse.bass as bass
import concourse.mybir as mybir
from concourse.bass_utils import run_bass_kernel_spmd
from concourse.bass import IndirectOffsetOnAxis

F32 = mybir.dt.float32
BF16 = mybir.dt.bfloat16
I32 = mybir.dt.int32
AF = mybir.ActivationFunctionType
ALU = mybir.AluOpType
AX = mybir.AxisListType
bf16 = ml_dtypes.bfloat16

NCORES = 8
T = 2048
D = 1024
NT = T // 128


class Prog:

    ENGINES = ("sync", "vector", "scalar", "tensor", "gpsimd")
    NH = 8
    NG = 4
    SEMS = ("v", "s", "t", "p") + tuple(f"h{i}" for i in range(8)) \
        + tuple(f"g{i}" for i in range(4))
    ESEM = {"vector": "v", "scalar": "s", "tensor": "t", "gpsimd": "p"}
    SPECIAL = {"max", "max_index", "max_with_indices", "match_replace"}

    def __init__(self, name="prog"):
        self.nc = bass.Bass("TRN2", target_bir_lowering=False, name=name)
        self.ctx = ExitStack()
        self.root_ctx = self.ctx
        self.ops = []
        self.cum = {k: 0 for k in self.SEMS}
        self._consts = {}
        self._fconsts = {}
        self._names = {}
        self.finalized = False
        self._writes = {}
        self._reads = {}
        self._special_ops = set()

    def _uname(self, base):
        i = self._names.get(base, 0)
        self._names[base] = i + 1
        return f"{base}_{i}" if i else base

    def sb(self, name, shape, dtype):
        return self.ctx.enter_context(
            self.nc.sbuf_tensor(self._uname(name), shape, dtype))

    def ps(self, name, shape, dtype=F32):
        return self.ctx.enter_context(
            self.nc.psum_tensor(self._uname(name), shape, dtype))

    def dram(self, name, shape, dtype, kind="Internal"):
        return self.nc.dram_tensor(self._uname(name), shape, dtype, kind=kind)

    def const(self, key, arr):
        if key not in self._consts:
            self._consts[key] = self.nc.inline_tensor(
                np.ascontiguousarray(arr), name=self._uname("c_" + key))
        return self._consts[key]

    def fconst(self, value):
        v = float(value)
        if v not in self._fconsts:
            tile = self.root_ctx.enter_context(
                self.nc.sbuf_tensor(self._uname("fc"), [128, 1], F32,
                                    side="right"))
            self._fconsts[v] = tile
            self._rec("vector", (lambda e, t=tile, vv=v: e.memset(t[:], vv)),
                      [], [tile[:]], False)
        return self._fconsts[v]

    @contextmanager
    def scope(self):
        es = ExitStack()
        old = self.ctx
        self.ctx = es
        try:
            yield
        finally:
            self.ctx = old
            es.close()
            self._fence = dict(self.cum)

    @staticmethod
    def _interval(ap):
        t = ap.tensor
        pairs = [tuple(x) for x in ap.ap]
        off = ap.offset
        lo = hi = 0
        for (step, cnt) in pairs[1:] if len(pairs) > 1 else pairs:
            d = step * (cnt - 1)
            if d >= 0:
                hi += d
            else:
                lo += d
        tname = type(t).__name__
        if tname == "DRamTensorHandle":
            if len(pairs) > 1:
                d0 = pairs[0][0] * (pairs[0][1] - 1)
                if d0 >= 0:
                    hi += d0
                else:
                    lo += d0
            return (t.name, off + lo, off + hi + 1)
        pstep = pairs[0][0] if pairs else 1
        if pstep <= 0:
            return (t.name, 0, 1 << 60)
        foff = off % pstep if pstep > 0 else off
        return (t.name, foff + lo, foff + hi + 1)

    @classmethod
    def _collect_aps(cls, args, kw):
        aps = []
        for a in list(args) + list(kw.values()):
            if isinstance(a, bass.AP):
                aps.append(a)
            elif isinstance(a, IndirectOffsetOnAxis):
                aps.append(a.ap)
            elif hasattr(a, "ap") and callable(getattr(a, "ap", None)) \
                    and type(a).__name__ in ("SBTensorHandle",
                                             "PSumTensorHandle",
                                             "DRamTensorHandle"):
                aps.append(a.ap())
        return aps

    def _rec(self, engine, fn, in_aps, out_aps, is_dma, grp=None,
             special=False):
        if is_dma:
            if engine in ("sync", "scalar"):
                self._hrot = (self._hrot + 1) % self.NH
                semkey = f"h{self._hrot}"
            else:
                self._grot = (self._grot + 1) % self.NG
                semkey = f"g{self._grot}"
            inc = 16
        else:
            semkey = self.ESEM[engine]
            inc = 1
        idx = len(self.ops)
        waits = {k: v for k, v in self._fence.items() if v > 0}
        if is_dma and self.cum[semkey] > 0:
            if waits.get(semkey, -1) < self.cum[semkey]:
                waits[semkey] = self.cum[semkey]

        def add_dep(j):
            if j is None or j == idx:
                return
            (jeng, jsem, jinc, jcum, jdma, jspecial) = self._opinfo[j]
            if (jeng == engine == "tensor" and not jdma and not is_dma):
                return
            if waits.get(jsem, -1) < jcum:
                waits[jsem] = jcum

        ivs_in = [self._interval(a) for a in in_aps]
        ivs_out = [self._interval(a) for a in out_aps]
        for (nm, lo, hi) in ivs_in:
            for (wlo, whi, j, _g) in self._writes.get(nm, ()):
                if wlo < hi and lo < whi:
                    add_dep(j)
        for (nm, lo, hi) in ivs_out:
            for (wlo, whi, j, g) in self._writes.get(nm, ()):
                if wlo < hi and lo < whi and not (grp is not None and g == grp):
                    add_dep(j)
            for (rlo, rhi, j) in self._reads.get(nm, ()):
                if rlo < hi and lo < rhi:
                    add_dep(j)
        cum = self.cum[semkey] + inc
        self.cum[semkey] = cum
        self._opinfo.append((engine, semkey, inc, cum, is_dma, special))
        self.ops.append((engine, fn, semkey, inc, waits))
        for (nm, lo, hi) in ivs_in:
            self._reads.setdefault(nm, []).append((lo, hi, idx))
            if len(self._reads[nm]) > 192:
                self._reads[nm] = self._merge_tail(self._reads[nm])
        for (nm, lo, hi) in ivs_out:
            self._writes.setdefault(nm, []).append((lo, hi, idx, grp))
            if len(self._writes[nm]) > 192:
                self._writes[nm] = self._merge_tailw(self._writes[nm])
        return idx

    @staticmethod
    def _merge_tail(lst):
        old = lst[:-96]
        lo = min(x[0] for x in old)
        hi = max(x[1] for x in old)
        j = max(x[2] for x in old)
        return [(lo, hi, j)] + lst[-96:]

    @staticmethod
    def _merge_tailw(lst):
        old = lst[:-96]
        lo = min(x[0] for x in old)
        hi = max(x[1] for x in old)
        j = max(x[2] for x in old)
        return [(lo, hi, j, None)] + lst[-96:]

    OUT_KW = {"accum_out"}

    def _classify(self, meth, args, kw):
        n_out = 1
        outs, ins = [], []
        pos = list(args)
        if meth == "memset":
            if pos and isinstance(pos[0], bass.AP):
                outs.append(pos[0])
            return ins, outs
        if meth == "indirect_dma_start":
            o, ooff, i, ioff = pos[0], pos[1], pos[2], pos[3]
            outs.append(o)
            ins.append(i)
            if ooff is not None:
                ins.append(ooff.ap)
            if ioff is not None:
                ins.append(ioff.ap)
            return ins, outs
        if meth == "max":
            outs.append(kw.get("out", pos[0] if pos else None))
            iv = kw.get("in_", pos[1] if len(pos) > 1 else None)
            if iv is not None:
                ins.append(iv)
            return ins, outs
        if meth == "dma_start":
            o = kw.get("out", pos[0] if pos else None)
            i = kw.get("in_", pos[1] if len(pos) > 1 else None)
            outs.append(o)
            if i is not None:
                ins.append(i)
            return ins, outs
        first = True
        for a in pos:
            if isinstance(a, bass.AP):
                if first:
                    outs.append(a)
                    first = False
                else:
                    ins.append(a)
            elif isinstance(a, IndirectOffsetOnAxis):
                ins.append(a.ap)
        for k, v in kw.items():
            if isinstance(v, bass.AP):
                if k in self.OUT_KW:
                    outs.append(v)
                else:
                    ins.append(v)
        if meth == "matmul" and not kw.get("start", True):
            ins.append(outs[0])
        return ins, outs

    def _emit(self, engine, meth, args, kw, is_dma, grp=None):
        args = tuple(args)
        kw = dict(kw)
        ins, outs = self._classify(meth, args, kw)
        special = meth in self.SPECIAL

        def fn(e, meth=meth, args=args, kw=kw):
            return getattr(e, meth)(*args, **kw)
        self._rec(engine, fn, ins, outs, is_dma, grp=grp, special=special)

    def op(self, engine, fn, dma=False):
        raise RuntimeError("raw op() unsupported in dep scheduler")

    def dma(self, engine, out, in_, grp=None, **kw):
        self._emit(engine, "dma_start", (), {"out": out, "in_": in_, **kw},
                   True, grp=grp)

    def v(self, meth, *args, **kw):
        self._emit("vector", meth, args, kw, False)

    def s(self, meth, *args, **kw):
        self._emit("scalar", meth, args, kw, False)

    def g(self, meth, *args, **kw):
        grp = kw.pop("grp", None)
        self._emit("gpsimd", meth, args, kw, "dma" in meth, grp=grp)

    def t(self, meth, *args, **kw):
        self._emit("tensor", meth, args, kw, False)

    def step(self):
        pass

    _opinfo = None

    def finalize(self):
        assert not self.finalized
        self.finalized = True
        nc = self.nc
        referenced = [False] * len(self.ops)
        valmap = {}
        for i, (engine, fn, semkey, inc, waits) in enumerate(self.ops):
            valmap[(semkey, self._opinfo[i][3])] = i
        for (engine, fn, semkey, inc, waits) in self.ops:
            for k, vv in waits.items():
                j = valmap.get((k, vv))
                if j is not None:
                    referenced[j] = True
        newcum = {k: 0 for k in self.SEMS}
        remap = {}
        op_inc = [0] * len(self.ops)
        for i, (engine, fn, semkey, inc, waits) in enumerate(self.ops):
            is_dma = self._opinfo[i][4]
            if referenced[i] or is_dma:
                newcum[semkey] += inc
                op_inc[i] = inc
            remap[(semkey, self._opinfo[i][3])] = newcum[semkey]
        sems = {k: self.ctx.enter_context(nc.semaphore("SEM_" + k))
                for k in self.SEMS}
        streams = {e: [] for e in self.ENGINES}
        for i, (engine, fn, semkey, inc, waits) in enumerate(self.ops):
            w2 = {}
            for k, vv in waits.items():
                nv = remap[(k, vv)]
                if nv > 0 and w2.get(k, -1) < nv:
                    w2[k] = nv
            streams[engine].append((w2, fn, semkey, op_inc[i]))
        self.final_cum = newcum
        with nc.Block() as block:
            for ename in self.ENGINES:
                stream = streams[ename]
                if not stream and ename != "sync":
                    continue

                def make(stream=stream, ename=ename):
                    def body(eng):
                        last = {k: 0 for k in self.SEMS}
                        for waits, fn, semkey, inc in stream:
                            for k, vv in waits.items():
                                if vv > last[k]:
                                    eng.wait_ge(sems[k], vv)
                                    last[k] = vv
                            ins = fn(eng)
                            if inc:
                                ins.then_inc(sems[semkey], inc)
                        if ename == "sync":
                            for k in self.SEMS:
                                if self.final_cum[k] > last[k]:
                                    eng.wait_ge(sems[k], self.final_cum[k])
                    return body
                getattr(block, ename)(make())
        return nc

    def run(self, in_maps, trace=False):
        nc = self.finalize()
        res = run_bass_kernel_spmd(nc, in_maps,
                                   core_ids=list(range(len(in_maps))),
                                   trace=trace)
        self.ctx.close()
        return res.results


Prog._opinfo = None


def _progpatch_init(orig_init):
    def init(self, *a, **k):
        orig_init(self, *a, **k)
        self._opinfo = []
        self._fence = {}
        self._hrot = 0
        self._grot = 0
    return init


Prog.__init__ = _progpatch_init(Prog.__init__)



def ident_bf16(p):
    return p.const("ident128", np.eye(128, dtype=bf16))


def load_weight_bf16_T(p, src_ap, rows, cols, name):
    stage = p.sb("wprep_stage", [128, max(cols, rows)], F32)
    stageb = p.sb("wprep_stageb", [128, max(cols, rows)], BF16)
    out_dram = p.dram(name + "_T", [cols, rows], BF16)
    tmp_dram = p.dram(name + "_bf", [rows, cols], BF16)
    for r in range(rows // 128):
        p.dma("sync", stage[:, 0:cols], src_ap[r * 128:(r + 1) * 128, :])
        p.step()
        p.v("tensor_copy", stageb[:, 0:cols], stage[:, 0:cols])
        p.step()
        p.dma("sync", tmp_dram.ap()[r * 128:(r + 1) * 128, :], stageb[:, 0:cols])
        p.step()
    for c in range(cols // 128):
        p.dma("sync", stageb[:, 0:rows],
              tmp_dram.ap()[:, c * 128:(c + 1) * 128], transpose=True)
        p.step()
        p.dma("sync", out_dram.ap()[c * 128:(c + 1) * 128, :], stageb[:, 0:rows])
        p.step()
    return out_dram


def load_weight_bf16(p, src_ap, rows, cols, name):
    out_dram = p.dram(name + "_bf", [rows, cols], BF16)
    stage = p.sb("wcast_stage", [128, cols], F32)
    stageb = p.sb("wcast_stageb", [128, cols], BF16)
    assert rows % 128 == 0
    for r in range(rows // 128):
        p.dma("sync", stage[:], src_ap[r * 128:(r + 1) * 128, :])
        p.step()
        p.v("tensor_copy", stageb[:], stage[:])
        p.step()
        p.dma("sync", out_dram.ap()[r * 128:(r + 1) * 128, :], stageb[:])
        p.step()
    return out_dram


def layernorm_tile(p, out_bf, x_sb, scratch, eps=1e-5):
    m = scratch["m"]; sq = scratch["sq"]; var = scratch["var"]
    rstd = scratch["rstd"]; t1 = scratch["t1"]
    p.v("reduce_sum", m[:], x_sb[:], AX.X)
    p.step()
    p.v("tensor_scalar_mul", m[:], m[:], 1.0 / D)
    p.step()
    p.s("activation", t1[:], x_sb[:], AF.Square, accum_out=sq[:])
    p.step()
    p.v("tensor_scalar_mul", sq[:], sq[:], 1.0 / D)
    p.step()
    p.s("activation", var[:], m[:], AF.Square)
    p.step()
    p.v("tensor_sub", var[:], sq[:], var[:])
    p.step()
    p.s("activation", rstd[:], var[:], AF.Ln, bias=p.fconst(eps)[:])
    p.step()
    p.s("activation", rstd[:], rstd[:], AF.Exp, scale=p.fconst(-0.5)[:])
    p.step()
    p.v("tensor_scalar", out_bf[:], x_sb[:], m[:], rstd[:], ALU.subtract, ALU.mult)
    p.step()


def ln_scratch(p):
    return {
        "m": p.sb("ln_m", [128, 1], F32),
        "sq": p.sb("ln_sq", [128, 1], F32),
        "var": p.sb("ln_var", [128, 1], F32),
        "rstd": p.sb("ln_rstd", [128, 1], F32),
        "t1": p.sb("ln_t1", [128, D], F32),
    }


def transpose_via_dram(p, src_sb_tiles, n_tiles, cols, dst_sb, dram_scratch):
    rows = n_tiles * 128
    for r in range(n_tiles):
        p.dma("sync", dram_scratch.ap()[r * 128:(r + 1) * 128, :], src_sb_tiles[r])
        p.step()
    for c in range(cols // 128):
        p.dma("sync", dst_sb[c][:, 0:rows],
              dram_scratch.ap()[:, c * 128:(c + 1) * 128], transpose=True)
        p.step()


MAXN = 512


def emit_matmul(p, out_ps, lhsT_ap, rhs_ap, start, stop):
    p.t("matmul", out_ps, lhsT_ap, rhs_ap, start=start, stop=stop)





def dma_wload(p, dst_sb_ap, src2d_ap, R, C, queue="sync"):
    a = R // 128
    p.dma(queue,
          dst_sb_ap.rearrange("p (a c) -> p a c", a=a),
          src2d_ap.rearrange("(a p) c -> p a c", p=128))



def build_l1_router(p, xlnT_hi, xlnT_lo, router_ap, w_full):
    rstg = p.sb("r_stg", [128, 8], F32)
    rhi = p.sb("r_hi", [128, 8 * 8], BF16)
    rlo = p.sb("r_lo", [128, 8 * 8], BF16)
    rtmp = p.sb("r_tmp", [128, 8], F32)
    for k in range(8):
        ks = slice(k * 8, (k + 1) * 8)
        p.dma("sync", rstg[:], router_ap[k * 128:(k + 1) * 128, :])
        p.step()
        p.v("tensor_copy", rhi[:, ks], rstg[:])
        p.step()
        p.v("tensor_copy", rtmp[:], rhi[:, ks])
        p.step()
        p.v("tensor_sub", rtmp[:], rstg[:], rtmp[:])
        p.step()
        p.v("tensor_copy", rlo[:, ks], rtmp[:])
        p.step()
    lg_ps = p.ps("lg_ps", [128, 8], F32)
    lg = p.sb("lg", [128, 8], F32)
    ex = p.sb("ex", [128, 8], F32)
    m8 = p.sb("m8t", [128, 8], F32)
    Hgt = p.sb("HgtT", [128, 8], F32)
    Tt = p.sb("TtT", [128, 8], F32)
    scn = p.sb("scnT", [128, 8], F32)
    cnt = p.sb("cntc", [128, 1], F32)
    mx = p.sb("mxc", [128, 1], F32)
    Z = p.sb("Zc", [128, 1], F32)
    p1 = p.sb("p1c", [128, 1], F32)
    p2 = p.sb("p2c", [128, 1], F32)
    nw2 = p.sb("nw2c", [128, 1], F32)
    nw1 = p.sb("nw1c", [128, 1], F32)
    uu = p.sb("uuT", [128, 8], F32)
    oh = p.sb("ohT", [128, 8], F32)
    for t in range(NT):
        ts = slice(t * 128, (t + 1) * 128)
        ws = slice(t * 8, (t + 1) * 8)
        for k in range(8):
            ks = slice(k * 8, (k + 1) * 8)
            p.t("matmul", lg_ps[:], xlnT_hi[:, k * T + t * 128: k * T + t * 128 + 128],
                rhi[:, ks], start=(k == 0), stop=False)
            p.t("matmul", lg_ps[:], xlnT_hi[:, k * T + t * 128: k * T + t * 128 + 128],
                rlo[:, ks], start=False, stop=False)
            p.t("matmul", lg_ps[:], xlnT_lo[:, k * T + t * 128: k * T + t * 128 + 128],
                rhi[:, ks], start=False, stop=(k == 7))
        p.step()
        p.v("tensor_copy", lg[:], lg_ps[:])
        p.step()
        p.v("reduce_max", mx[:], lg[:], AX.X)
        p.step()
        p.v("tensor_scalar_mul", mx[:], mx[:], -1.0)
        p.step()
        p.s("activation", ex[:], lg[:], AF.Exp, bias=mx[:], accum_out=Z[:])
        p.step()
        p.v("reciprocal", Z[:], Z[:])
        p.step()
        p.v("tensor_scalar_mul", ex[:], ex[:], Z[:])
        p.step()
        p.v("max", out=m8[:], in_=lg[:])
        p.step()
        p.step()
        p.v("tensor_scalar", Hgt[:], lg[:], m8[:, 1:2], None, ALU.is_gt)
        p.step()
        p.v("tensor_scalar", Tt[:], lg[:], m8[:, 1:2], None, ALU.is_equal)
        p.step()
        p.v("tensor_reduce", cnt[:], Hgt[:], AX.X, ALU.add)
        p.step()
        p.v("tensor_tensor_scan", scn[:], Tt[:], Tt[:], 0.0, ALU.add, ALU.bypass)
        p.step()
        p.v("scalar_tensor_tensor", uu[:], scn[:], 1.0, Tt[:], ALU.mult, ALU.mult)
        p.step()
        p.v("tensor_scalar", uu[:], uu[:], cnt[:], None, ALU.add)
        p.step()
        p.v("tensor_mul", uu[:], uu[:], Tt[:])
        p.step()
        p.v("tensor_scalar", oh[:], uu[:], 0.5, 2.5, ALU.is_ge, ALU.bypass)
        p.step()
        p.v("tensor_scalar", scn[:], uu[:], 2.5, None, ALU.is_le)
        p.step()
        p.v("tensor_mul", oh[:], oh[:], scn[:])
        p.step()
        p.v("tensor_add", oh[:], oh[:], Hgt[:])
        p.step()
        p.v("tensor_mul", scn[:], ex[:], oh[:])
        p.step()
        p.v("reduce_max", p1[:], scn[:], AX.X)
        p.step()
        p.v("tensor_reduce", p2[:], scn[:], AX.X, ALU.add)
        p.step()
        p.v("tensor_sub", p2[:], p2[:], p1[:])
        p.step()
        p.v("tensor_sub", nw2[:], p2[:], p1[:])
        p.step()
        p.s("activation", nw2[:], nw2[:], AF.Sigmoid)
        p.step()
        p.v("tensor_scalar", nw1[:], nw2[:], -1.0, 1.0, ALU.mult, ALU.add)
        p.step()
        p.v("tensor_scalar", oh[:], uu[:], 1.0, None, ALU.is_equal)
        p.step()
        p.v("tensor_add", oh[:], oh[:], Hgt[:])
        p.step()
        p.v("tensor_scalar", scn[:], uu[:], 2.0, None, ALU.is_equal)
        p.step()
        p.v("tensor_scalar_mul", oh[:], oh[:], nw1[:])
        p.step()
        p.v("tensor_scalar_mul", scn[:], scn[:], nw2[:])
        p.step()
        p.v("tensor_add", w_full[:, ws], oh[:], scn[:])
        p.step()


def build_l1_ffn_experts(p, xlnT_hi, w_full, gate_bf, up_bf, down_bf,
                         x_in_dram, x_out_dram, cast_srcs=None):
    TH = 1024
    wg2 = [p.sb("wg", [128, 8 * 2048], BF16) for _ in range(2)]
    gs = p.sb("gs", [128, 16 * TH], BF16)
    wd = p.sb("wd", [128, 16 * 1024], BF16)
    wcst = p.sb("wcst", [128, 2048], F32) if cast_srcs else None

    def wload(dst_sb, bf_dram, f32_dram, e, R, C, first):
        if not (cast_srcs and first):
            dma_wload(p, dst_sb.ap(), bf_dram.ap()[e], R, C)
            return
        per = (2048 // C) * 128
        ar = per // 128
        for hh in range(R // per):
            rs = slice(hh * per, (hh + 1) * per)
            cs = slice(hh * ar * C, (hh + 1) * ar * C)
            dma_wload(p, wcst.ap()[:, 0:ar * C], f32_dram.ap()[e][rs, :],
                      per, C)
            p.s("copy", dst_sb[:, cs], wcst[:, 0:ar * C])
            p.dma("sync",
                  bf_dram.ap()[e][rs, :].rearrange("(a p) c -> p a c", p=128),
                  dst_sb[:, cs].rearrange("p (a c) -> p a c", a=ar))
    acc = p.sb("accT", [128, 8 * 1024], F32)
    xres = p.sb("xres", [128, 1024], F32)
    ps = [p.ps("pmm", [128, 512], F32) for _ in range(2)]
    pbank = [0]

    def nextps():
        pbank[0] ^= 1
        return ps[pbank[0]]

    for h in range(2):
        for tt in range(8):
            p.v("memset", acc[:, tt * 1024:(tt + 1) * 1024], 0.0)
        p.step()
        for e in range(8):
            for which, wsrc, fsrc in (("g", gate_bf, 0), ("u", up_bf, 1)):
                wg = wg2[0 if which == "g" else 1]
                wload(wg, wsrc, cast_srcs[fsrc] if cast_srcs else None,
                      e, 1024, 2048, h == 0)
                for ft in range(16):
                    for nh in range(2):
                        bank = nextps()
                        for k in range(8):
                            p.t("matmul", bank[:],
                                wg[:, k * 2048 + ft * 128: k * 2048 + ft * 128 + 128],
                                xlnT_hi[:, k * T + h * TH + nh * 512:
                                        k * T + h * TH + nh * 512 + 512],
                                start=(k == 0), stop=(k == 7))
                        p.step()
                        dst = gs[:, ft * TH + nh * 512: ft * TH + nh * 512 + 512]
                        if which == "g":
                            p.s("activation", dst, bank[:], AF.Silu)
                        else:
                            p.v("tensor_mul", dst, bank[:], dst)
                        p.step()
            wload(wd, down_bf, cast_srcs[2] if cast_srcs else None,
                  e, 2048, 1024, h == 0)
            for tt in range(8):
                wcol = w_full[:, (h * 8 + tt) * 8 + e:(h * 8 + tt) * 8 + e + 1]
                for nh2 in range(2):
                    bank = nextps()
                    for k in range(16):
                        p.t("matmul", bank[:],
                            gs[:, k * TH + tt * 128: k * TH + tt * 128 + 128],
                            wd[:, k * 1024 + nh2 * 512: k * 1024 + nh2 * 512 + 512],
                            start=(k == 0), stop=(k == 15))
                    p.step()
                    a = acc[:, tt * 1024 + nh2 * 512: tt * 1024 + nh2 * 512 + 512]
                    p.v("scalar_tensor_tensor", a, bank[:], wcol, a,
                        ALU.mult, ALU.add)
                    p.step()
        for tt in range(8):
            row = (h * 8 + tt) * 128
            p.dma("sync", xres[:], x_in_dram.ap()[row:row + 128, :])
            p.step()
            p.v("scalar_tensor_tensor", xres[:],
                acc[:, tt * 1024:(tt + 1) * 1024], 0.5, xres[:],
                ALU.mult, ALU.add)
            p.step()
            p.dma("sync", x_out_dram.ap()[row:row + 128, :], xres[:])
            p.step()


def build_l1_ffn_call(p, x_in_dram, x_out_dram, router_ap, gate_bf, up_bf,
                      down_bf, tag="l1f", cast_srcs=None):
    xlnT_hi = p.sb(tag + "_Thi", [128, 8 * T], BF16)
    w_full = p.sb(tag + "_wfull", [128, NT * 8], F32)
    with p.scope():
        xlnT_lo = p.sb(tag + "_Tlo", [128, 8 * T], BF16)
        with p.scope():
            build_ln_and_transpose_into(p, x_in_dram, xlnT_hi, xlnT_lo, tag)
        build_l1_router(p, xlnT_hi, xlnT_lo, router_ap, w_full)
    build_l1_ffn_experts(p, xlnT_hi, w_full, gate_bf, up_bf, down_bf,
                         x_in_dram, x_out_dram, cast_srcs=cast_srcs)


def build_ln_and_transpose_into(p, x_src_dram, xlnT_hi, xlnT_lo, tag="ln",
                                tok_out_dram=None):
    xt = p.sb(tag + "_xt", [128, D], F32)
    lnf = p.sb(tag + "_f", [128, D], F32)
    hib = p.sb(tag + "_hib", [128, D], BF16)
    lob = p.sb(tag + "_lob", [128, D], BF16)
    sc = ln_scratch(p)
    tmp = sc["t1"]
    dsc = p.dram(tag + "_dsc", [T, D], BF16)
    dsclo = p.dram(tag + "_dsclo", [T, D], BF16) if xlnT_lo is not None else None
    for t in range(NT):
        p.dma("sync", xt[:], x_src_dram.ap()[t * 128:(t + 1) * 128, :])
        p.step()
        p.v("reduce_sum", sc["m"][:], xt[:], AX.X)
        p.step()
        p.v("tensor_scalar_mul", sc["m"][:], sc["m"][:], 1.0 / D)
        p.step()
        p.s("activation", tmp[:], xt[:], AF.Square, accum_out=sc["sq"][:])
        p.step()
        p.v("tensor_scalar_mul", sc["sq"][:], sc["sq"][:], 1.0 / D)
        p.step()
        p.s("activation", sc["var"][:], sc["m"][:], AF.Square)
        p.step()
        p.v("tensor_sub", sc["var"][:], sc["sq"][:], sc["var"][:])
        p.step()
        p.s("activation", sc["rstd"][:], sc["var"][:], AF.Ln, bias=p.fconst(1e-5)[:])
        p.step()
        p.s("activation", sc["rstd"][:], sc["rstd"][:], AF.Exp,
            scale=p.fconst(-0.5)[:])
        p.step()
        p.v("tensor_scalar", lnf[:], xt[:], sc["m"][:], sc["rstd"][:],
            ALU.subtract, ALU.mult)
        p.step()
        p.v("tensor_copy", hib[:], lnf[:])
        p.step()
        p.dma("sync", dsc.ap()[t * 128:(t + 1) * 128, :], hib[:])
        p.step()
        if tok_out_dram is not None:
            p.dma("sync", tok_out_dram.ap()[t * 128:(t + 1) * 128, :], hib[:])
            p.step()
        if xlnT_lo is not None:
            p.v("tensor_copy", tmp[:], hib[:])
            p.step()
            p.v("tensor_sub", tmp[:], lnf[:], tmp[:])
            p.step()
            p.v("tensor_copy", lob[:], tmp[:])
            p.step()
            p.dma("sync", dsclo.ap()[t * 128:(t + 1) * 128, :], lob[:])
            p.step()
    for src_d, dstT in ((dsc, xlnT_hi), (dsclo, xlnT_lo)):
        if dstT is None:
            continue
        for d8 in range(8):
            p.dma("sync", dstT[:, d8 * T:(d8 + 1) * T],
                  src_d.ap()[:, d8 * 128:(d8 + 1) * 128], transpose=True)
            p.step()



FD = 16
NH = 16
HD = 64
DF = 153
DFP = 160
RRD = 2.0
CHUNK = 128
NCH = T // CHUNK


def build_l1_attn_call(p, x_in_dram, x_out_dram, wqT_ap, wkT_ap, wvT_ap,
                       woT_ap, tag="l1a"):
    ident = p.sb(tag + "_ident", [128, 128], BF16)
    p.dma("sync", ident[:], p.const("ident128", np.eye(128, dtype=bf16)).ap()[:])
    p.step()
    mk = np.zeros((128, 128), np.float32)
    for i in range(128):
        mk[i, i:] = 1.0
    mask = p.sb(tag + "_mask", [128, 128], F32)
    p.dma("sync", mask[:], p.const("maskcausal", mk).ap()[:])
    p.step()
    onesb = p.sb(tag + "_ones", [128, 1], BF16)
    p.v("memset", onesb[:], 1.0)
    p.step()
    ones_row = p.sb(tag + "_onesr", [1, 64], BF16)
    p.v("memset", ones_row[:], 1.0)
    p.step()

    v_tok = p.sb(tag + "_vtok", [128, NT * 1024], BF16)
    q_tok = p.sb(tag + "_qtok", [128, NT * 512], BF16)
    k_tok = p.sb(tag + "_ktok", [128, NT * 512], BF16)
    y_all = p.sb(tag + "_yall", [128, 8 * T], BF16)

    ps0 = p.ps(tag + "_ps0", [128, 512], F32)
    ps1 = p.ps(tag + "_ps1", [128, 512], F32)
    ps2 = p.ps(tag + "_ps2", [128, 512], F32)
    ps3 = p.ps(tag + "_ps3", [128, 512], F32)
    psT = p.ps(tag + "_psT", [128, 256], BF16)

    with p.scope():
        xlnT = p.sb(tag + "_Thi", [128, 8 * T], BF16)
        with p.scope():
            build_ln_and_transpose_into(p, x_in_dram, xlnT, None, tag + "ln")
        wqk = p.sb(tag + "_wqk", [128, 8 * 512], BF16)
        wv = p.sb(tag + "_wv", [128, 8 * 1024], BF16)
        wstg = p.sb(tag + "_wstg", [128, 8 * 1024], F32)
        for wap, tdst in ((wqT_ap, q_tok), (wkT_ap, k_tok)):
            dma_wload(p, wstg.ap()[:, 0:8 * 512], wap, 1024, 512)
            p.v("tensor_copy", wqk[:], wstg[:, 0:8 * 512])
            for tt in range(NT):
                bank = ps0 if tt % 2 == 0 else ps1
                for k in range(8):
                    p.t("matmul", bank[:],
                        xlnT[:, k * T + tt * 128: k * T + tt * 128 + 128],
                        wqk[:, k * 512:(k + 1) * 512],
                        start=(k == 0), stop=(k == 7))
                p.step()
                p.v("tensor_copy", tdst[:, tt * 512:(tt + 1) * 512], bank[:])
                p.step()
        dma_wload(p, wstg.ap(), wvT_ap, 1024, 1024)
        p.v("tensor_copy", wv[:], wstg[:])
        for tt in range(NT):
            for n in range(2):
                bank = ps0 if n == 0 else ps1
                for k in range(8):
                    p.t("matmul", bank[:],
                        xlnT[:, k * T + tt * 128: k * T + tt * 128 + 128],
                        wv[:, k * 1024 + n * 512: k * 1024 + n * 512 + 512],
                        start=(k == 0), stop=(k == 7))
                p.step()
                p.v("tensor_copy",
                    v_tok[:, tt * 1024 + n * 512: tt * 1024 + n * 512 + 512],
                    bank[:])
                p.step()

    qf = p.sb(tag + "_qf", [128, NH * DFP], BF16)
    kf = p.sb(tag + "_kf", [128, NH * DFP], BF16)
    qfDa = p.sb(tag + "_qfDa", [128, 128], BF16)
    qfDb = p.sb(tag + "_qfDb", [32, 128], BF16)
    kfDa = p.sb(tag + "_kfDa", [128, 128], BF16)
    kfDb = p.sb(tag + "_kfDb", [32, 128], BF16)
    st_a = p.sb(tag + "_sta", [128, NH * 65], F32)
    st_b = p.sb(tag + "_stb", [32, NH * 65], F32)
    stba = p.sb(tag + "_stba", [128, 65], BF16)
    stbb = p.sb(tag + "_stbb", [32, 65], BF16)
    S_m = p.sb(tag + "_Sm", [128, 128], BF16)
    den = p.sb(tag + "_den", [1, 128], F32)
    denb = p.sb(tag + "_denb", [1, 128], BF16)
    rb = p.sb(tag + "_rb", [64, 128], F32)

    s2 = float(1.0 / np.sqrt(4.0 * np.sqrt(2.0)))
    p.v("memset", qf[:], 0.0)
    p.v("memset", kf[:], 0.0)
    p.v("memset", st_a[:], 0.0)
    p.v("memset", st_b[:], 0.0)
    p.step()
    for fbuf in (qf, kf):
        f3 = fbuf.ap().rearrange("p (h c) -> p h c", h=NH)
        p.v("memset", f3[:, :, 0:1], 1.0)
    p.step()

    def band_off(j):
        return 33 + 15 * j - (j * (j - 1)) // 2

    for c in range(NCH):
        for fbuf, tok in ((qf, q_tok), (kf, k_tok)):
            f3 = fbuf.ap().rearrange("p (h c) -> p h c", h=NH)
            t3 = tok.ap()[:, c * 512:(c + 1) * 512].rearrange(
                "p (h c) -> p h c", h=NH)[:, :, 0:16]
            p.v("tensor_scalar_mul", f3[:, :, 1:17], t3, 1.0 / RRD)
            p.step()
            p.s("activation", f3[:, :, 17:33], t3, AF.Square,
                scale=p.fconst(s2)[:])
            p.step()
            for j in range(15):
                w = 15 - j
                o = band_off(j)
                p.v("tensor_mul", f3[:, :, o:o + w],
                    f3[:, :, 1 + j:2 + j].to_broadcast([128, NH, w]),
                    f3[:, :, 2 + j:2 + j + w])
            p.step()
        for h in range(NH):
            B = h * DFP
            vsl = v_tok[:, c * 1024 + h * 64: c * 1024 + h * 64 + 64]
            sta = st_a[:, h * 65:(h + 1) * 65]
            stb = st_b[:, h * 65:(h + 1) * 65]
            ysl = y_all.ap()[64 * (h % 2):64 * (h % 2) + 64,
                             (h // 2) * T + c * 128:(h // 2) * T + c * 128 + 128]
            p.t("transpose", psT[:, 0:128], qf[:, B:B + 128], ident[:])
            p.step()
            p.v("tensor_copy", qfDa[:], psT[:, 0:128])
            p.step()
            p.t("transpose", psT[0:25, 0:128], qf[:, B + 128:B + 153], ident[:])
            p.step()
            p.v("tensor_copy", qfDb[0:25, :], psT[0:25, 0:128])
            p.step()
            p.t("transpose", psT[:, 0:128], kf[:, B:B + 128], ident[:])
            p.step()
            p.v("tensor_copy", kfDa[:], psT[:, 0:128])
            p.step()
            p.t("transpose", psT[0:25, 0:128], kf[:, B + 128:B + 153], ident[:])
            p.step()
            p.v("tensor_copy", kfDb[0:25, :], psT[0:25, 0:128])
            p.step()
            p.v("tensor_copy", stba[:], sta)
            p.v("tensor_copy", stbb[0:25, :], stb[0:25, :])
            p.step()
            p.t("matmul", ps0[:, 0:128], kfDa[:], qfDa[:], start=True, stop=False)
            p.t("matmul", ps0[:, 0:128], kfDb[0:25, :], qfDb[0:25, :],
                start=False, stop=True)
            p.step()
            p.v("tensor_mul", S_m[:], ps0[:, 0:128], mask[:])
            p.step()
            p.t("matmul", ps1[0:64, 0:128], vsl, S_m[:], start=True, stop=False)
            p.t("matmul", ps1[0:64, 0:128], stba[:, 0:64], qfDa[:],
                start=False, stop=False)
            p.t("matmul", ps1[0:64, 0:128], stbb[0:25, 0:64], qfDb[0:25, :],
                start=False, stop=True)
            p.step()
            p.t("matmul", ps2[0:1, 0:128], onesb[:], S_m[:], start=True, stop=False)
            p.t("matmul", ps2[0:1, 0:128], stba[:, 64:65], qfDa[:],
                start=False, stop=False)
            p.t("matmul", ps2[0:1, 0:128], stbb[0:25, 64:65], qfDb[0:25, :],
                start=False, stop=True)
            p.step()
            p.v("tensor_scalar", den[:], ps2[0:1, 0:128], 1e-12, None, ALU.add)
            p.step()
            p.v("reciprocal", den[:], den[:])
            p.step()
            p.v("tensor_copy", denb[:], den[:])
            p.step()
            p.t("matmul", ps3[0:64, 0:128], ones_row[:], denb[:],
                start=True, stop=True)
            p.step()
            p.v("tensor_copy", rb[:], ps3[0:64, 0:128])
            p.step()
            p.v("tensor_mul", ysl, ps1[0:64, 0:128], rb[:])
            p.step()
            p.t("matmul", ps0[:, 0:64], kf[:, B:B + 128], vsl, start=True, stop=True)
            p.t("matmul", ps0[:, 64:65], kf[:, B:B + 128], onesb[:],
                start=True, stop=True)
            p.step()
            p.v("tensor_add", sta, sta, ps0[:, 0:65])
            p.step()
            p.t("matmul", ps1[0:25, 0:64], kf[:, B + 128:B + 153], vsl,
                start=True, stop=True)
            p.t("matmul", ps1[0:25, 64:65], kf[:, B + 128:B + 153], onesb[:],
                start=True, stop=True)
            p.step()
            p.v("tensor_add", stb[0:25, :], stb[0:25, :], ps1[0:25, 0:65])
            p.step()

    with p.scope():
        wo = p.sb(tag + "_wo", [128, 8 * 1024], BF16)
        wstg2 = p.sb(tag + "_wstg2", [128, 8 * 1024], F32)
        xres = p.sb(tag + "_xres", [128, 1024], F32)
        dma_wload(p, wstg2.ap(), woT_ap, 1024, 1024)
        p.v("tensor_copy", wo[:], wstg2[:])
        for tt in range(NT):
            p.dma("sync", xres[:], x_in_dram.ap()[tt * 128:(tt + 1) * 128, :])
            p.step()
            for n in range(2):
                bank = ps0 if n == 0 else ps1
                for k in range(8):
                    p.t("matmul", bank[:],
                        y_all[:, k * T + tt * 128: k * T + tt * 128 + 128],
                        wo[:, k * 1024 + n * 512: k * 1024 + n * 512 + 512],
                        start=(k == 0), stop=(k == 7))
                p.step()
                p.v("tensor_add", xres[:, n * 512:(n + 1) * 512],
                    xres[:, n * 512:(n + 1) * 512], bank[:])
                p.step()
            p.dma("sync", x_out_dram.ap()[tt * 128:(tt + 1) * 128, :], xres[:])
            p.step()



DI = 2048
NDT = DI // 128
DSTATE = 16


def build_mamba_dir(p, src_dram, w, y_dram, tag, reverse=False):
    yraw_dram = p.dram(tag + "_yraw", [DI, T], BF16)
    z_dram = p.dram(tag + "_z", [DI, T], BF16)
    xc = p.sb(tag + "_xc", [128, NDT * T], BF16)
    ps0 = p.ps(tag + "_ps0", [128, 512], F32)
    ps1 = p.ps(tag + "_ps1", [128, 512], F32)

    with p.scope():
        xlnT = p.sb(tag + "_xlnT", [128, 8 * T], BF16)
        for d8 in range(8):
            p.dma("sync", xlnT[:, d8 * T:(d8 + 1) * T],
                  src_dram.ap()[:, d8 * 128:(d8 + 1) * 128], transpose=True)
            p.step()
        if reverse:
            rtmp = p.sb(tag + "_rtmp", [128, T], BF16)
            for d8 in range(8):
                p.v("tensor_copy", rtmp[:], xlnT[:, d8 * T:(d8 + 1) * T][:, ::-1])
                p.step()
                p.v("tensor_copy", xlnT[:, d8 * T:(d8 + 1) * T], rtmp[:])
                p.step()
        w_in = p.sb(tag + "_win", [128, 8 * 4096], BF16)
        wstg = p.sb(tag + "_wstg", [128, 2 * 4096], F32)
        zstg = p.sb(tag + "_zstg", [128, 512], BF16)
        for hh in range(4):
            dma_wload(p, wstg.ap(), w["in_wT"][hh * 256:(hh + 1) * 256, :],
                      256, 4096)
            p.v("tensor_copy", w_in[:, hh * 2 * 4096:(hh + 1) * 2 * 4096],
                wstg[:])
        for et in range(32):
            for n in range(4):
                bank = ps0 if (et * 4 + n) % 2 == 0 else ps1
                for k in range(8):
                    p.t("matmul", bank[:],
                        w_in[:, k * 4096 + et * 128: k * 4096 + et * 128 + 128],
                        xlnT[:, k * T + n * 512: k * T + n * 512 + 512],
                        start=(k == 0), stop=(k == 7))
                p.step()
                if et < 16:
                    p.v("tensor_copy",
                        xc[:, et * T + n * 512: et * T + n * 512 + 512], bank[:])
                    p.step()
                else:
                    p.v("tensor_copy", zstg[:], bank[:])
                    p.step()
                    p.dma("sync",
                          z_dram.ap()[(et - 16) * 128:(et - 15) * 128,
                                      n * 512:(n + 1) * 512], zstg[:])
                    p.step()

    with p.scope():
        convw = p.sb(tag + "_convw", [128, 64], F32)
        convb = p.sb(tag + "_convb", [128, 16], F32)
        dtb = p.sb(tag + "_dtb", [128, 16], F32)
        Dl = p.sb(tag + "_Dl", [128, 16], F32)
        Al = p.sb(tag + "_Al", [128, 256], F32)
        p.dma("sync", convw[:], w["conv_w_l"])
        p.dma("sync", convb[:], w["conv_b_l"])
        p.dma("sync", dtb[:], w["dt_b_l"])
        p.dma("sync", Dl[:], w["D_l"])
        p.dma("sync", Al[:], w["A_log_l"])
        p.step()
        p.s("activation", Al[:], Al[:], AF.Exp)
        p.step()
        p.v("tensor_scalar_mul", Al[:], Al[:], -1.0)
        p.step()

        dtt = p.sb(tag + "_dtt", [128, T], F32)
        du_bf = p.sb(tag + "_dubf", [128, T], BF16)
        at2 = [p.sb(tag + "_at", [128, T], BF16) for _ in range(2)]
        bt_bf = p.sb(tag + "_btbf", [128, T], BF16)
        ht_bf = p.sb(tag + "_htbf", [128, T], BF16)
        t1b2 = [p.sb(tag + "_t1b", [128, T], BF16) for _ in range(2)]
        Bsb2 = [p.sb(tag + "_Bsb", [128, T], BF16) for _ in range(2)]
        Csb2 = [p.sb(tag + "_Csb", [128, T], BF16) for _ in range(2)]
        psB2 = [p.ps(tag + "_psB2", [128, 1024], F32) for _ in range(2)]
        psC1 = p.ps(tag + "_psC1", [128, 1024], F32)
        yac = p.sb(tag + "_yac", [128, T], F32)
        t1 = p.sb(tag + "_t1", [128, T], F32)
        ybf = p.sb(tag + "_ybf", [128, T], BF16)
        xdbl_bf = p.sb(tag + "_xdbl", [96, T], BF16)
        dtw = p.sb(tag + "_dtw", [64, 2048], BF16)
        ones_t = p.sb(tag + "_onest", [128, 128], BF16)
        p.v("memset", ones_t[:], 1.0)
        p.step()

        for dt_ in range(NDT):
            xi_t = xc[:, dt_ * T:(dt_ + 1) * T]
            p.v("tensor_scalar_mul", t1[:], xi_t, convw[:, dt_ * 4 + 3:dt_ * 4 + 4])
            p.step()
            for j in range(1, 4):
                p.v("scalar_tensor_tensor", t1[:, j:T], xi_t[:, 0:T - j],
                    convw[:, dt_ * 4 + 3 - j:dt_ * 4 + 4 - j], t1[:, j:T],
                    ALU.mult, ALU.add)
                p.step()
            p.s("activation", xi_t, t1[:], AF.Silu, bias=convb[:, dt_:dt_ + 1])
            p.step()

        with p.scope():
            xw = p.sb(tag + "_xw", [128, 16 * 96], BF16)
            wstg2 = p.sb(tag + "_wstg2", [128, 96], F32)
            for k in range(16):
                p.dma("sync", wstg2[:], w["x_wT"][k * 128:(k + 1) * 128, :])
                p.step()
                p.v("tensor_copy", xw[:, k * 96:(k + 1) * 96], wstg2[:])
                p.step()
            for n in range(4):
                bank = ps0 if n % 2 == 0 else ps1
                for k in range(16):
                    p.t("matmul", bank[0:96, :],
                        xw[:, k * 96:(k + 1) * 96],
                        xc[:, k * T + n * 512: k * T + n * 512 + 512],
                        start=(k == 0), stop=(k == 15))
                p.step()
                p.v("tensor_copy", t1[0:96, n * 512:(n + 1) * 512], bank[0:96, :])
                p.step()
            p.v("tensor_copy", xdbl_bf[:], t1[0:96, :])
            p.step()
            wstg3 = p.sb(tag + "_wstg3", [64, 2048], F32)
            p.dma("sync", wstg3[:], w["dt_wT"][:, :])
            p.step()
            p.v("tensor_copy", dtw[:], wstg3[:])
            p.step()

        bct = [p.sb(tag + f"_bc{i}", [128, T], BF16) for i in range(11)]

        def bc_row(s):
            return bct[s // 3][32 * (s % 3):32 * (s % 3) + 1, :]

        for s in range(32):
            p.dma("gpsimd", bc_row(s), xdbl_bf.ap()[64 + s:65 + s, :])
        p.step()

        for dt_ in range(NDT):
            for n in range(4):
                bank = ps0 if n % 2 == 0 else ps1
                p.t("matmul", bank[:], dtw[:, dt_ * 128:(dt_ + 1) * 128],
                    xdbl_bf.ap()[0:64, n * 512:(n + 1) * 512],
                    start=True, stop=True)
                p.step()
                p.v("tensor_scalar", dtt[:, n * 512:(n + 1) * 512], bank[:],
                    dtb[:, dt_:dt_ + 1], None, ALU.add)
                p.step()
            p.s("activation", t1[:], dtt[:], AF.Abs)
            p.step()
            p.s("activation", t1[:], t1[:], AF.Exp, scale=p.fconst(-1.0)[:])
            p.step()
            p.s("activation", t1[:], t1[:], AF.Ln, bias=p.fconst(1.0)[:])
            p.step()
            p.v("tensor_scalar_max", dtt[:], dtt[:], 0.0)
            p.step()
            p.v("tensor_add", dtt[:], dtt[:], t1[:])
            p.step()
            p.v("tensor_mul", du_bf[:], dtt[:], xc[:, dt_ * T:(dt_ + 1) * T])
            p.step()
            p.v("memset", yac[:], 0.0)
            p.step()
            for s in range(DSTATE):
                par = s % 2
                atp = at2[par]
                psBp = psB2[par]
                Bsb, Csb, t1b = Bsb2[par], Csb2[par], t1b2[par]
                p.s("activation", atp[:], dtt[:], AF.Exp,
                    scale=Al[:, dt_ * 16 + s:dt_ * 16 + s + 1])
                bb = 32 * (s % 3)
                cb = 32 * ((16 + s) % 3)
                for half in range(2):
                    for n in range(2):
                        c0 = half * 1024 + n * 512
                        p.t("matmul", psBp[:, n * 512:(n + 1) * 512],
                            ones_t[bb:bb + 1, :],
                            bc_row(s)[:, c0:c0 + 512], start=True, stop=True)
                    p.s("copy", Bsb[:, half * 1024:(half + 1) * 1024], psBp[:])
                p.v("tensor_mul", bt_bf[:], du_bf[:], Bsb[:])
                p.v("tensor_tensor_scan", ht_bf[:], atp[:], bt_bf[:], 0.0,
                    ALU.mult, ALU.add)
                for half in range(2):
                    for n in range(2):
                        c0 = half * 1024 + n * 512
                        p.t("matmul", psC1[:, n * 512:(n + 1) * 512],
                            ones_t[cb:cb + 1, :],
                            bc_row(16 + s)[:, c0:c0 + 512], start=True, stop=True)
                    p.s("copy", Csb[:, half * 1024:(half + 1) * 1024], psC1[:])
                p.v("tensor_mul", t1b[:], ht_bf[:], Csb[:])
                if s % 2 == 0:
                    p.g("tensor_add", yac[:], yac[:], t1b[:])
                else:
                    p.v("tensor_add", yac[:], yac[:], t1b[:])
            p.v("tensor_copy", ybf[:], yac[:])
            p.step()
            p.dma("sync", yraw_dram.ap()[dt_ * 128:(dt_ + 1) * 128, :], ybf[:])
            p.step()

        for dt_ in range(NDT):
            p.dma("sync", ybf[:], z_dram.ap()[dt_ * 128:(dt_ + 1) * 128, :])
            p.step()
            p.s("activation", t1[:], ybf[:], AF.Silu)
            p.step()
            p.dma("sync", ybf[:], yraw_dram.ap()[dt_ * 128:(dt_ + 1) * 128, :])
            p.step()
            p.v("scalar_tensor_tensor", yac[:], xc[:, dt_ * T:(dt_ + 1) * T],
                Dl[:, dt_:dt_ + 1], ybf[:], ALU.mult, ALU.add)
            p.step()
            p.v("tensor_mul", yac[:], yac[:], t1[:])
            p.step()
            p.v("tensor_copy", ybf[:], yac[:])
            p.step()
            p.dma("sync", yraw_dram.ap()[dt_ * 128:(dt_ + 1) * 128, :], ybf[:])
            p.step()

    with p.scope():
        wo = p.sb(tag + "_wo", [128, 16 * 1024], BF16)
        wo_f = p.sb(tag + "_wo_f", [128, 16 * 1024], F32)
        ycols = p.sb(tag + "_ycols", [128, 16 * 128], BF16)
        outf = p.sb(tag + "_outf", [128, 1024], F32)
        dma_wload(p, wo_f.ap(), w["out_wT"], 2048, 1024)
        p.v("tensor_copy", wo[:], wo_f[:])
        ycols2 = p.sb(tag + "_ycols2", [128, 16 * 128], BF16)
        for tt in range(NT):
            srcv = yraw_dram.ap().rearrange("(a p) t -> p a t", p=128)
            c0 = (T - (tt + 1) * 128) if reverse else tt * 128
            p.dma("sync",
                  ycols.ap().rearrange("p (a t) -> p a t", a=16),
                  srcv[:, :, c0:c0 + 128])
            p.step()
            if reverse:
                for a in range(16):
                    p.v("tensor_copy", ycols2[:, a * 128:(a + 1) * 128],
                        ycols[:, a * 128:(a + 1) * 128][:, ::-1])
                p.step()
                ycols, ycols2 = ycols2, ycols
            for n in range(2):
                bank = ps0 if n == 0 else ps1
                for k in range(16):
                    p.t("matmul", bank[:],
                        ycols[:, k * 128:(k + 1) * 128],
                        wo[:, k * 1024 + n * 512: k * 1024 + n * 512 + 512],
                        start=(k == 0), stop=(k == 15))
                p.step()
                p.v("tensor_copy", outf[:, n * 512:(n + 1) * 512], bank[:])
                p.step()
            p.dma("sync", y_dram.ap()[tt * 128:(tt + 1) * 128, :], outf[:])
            p.step()


def bcast_row_f32(p, row_ap, dst_sb, ps, ones1f, cols=1024):
    stg = p.sb("bcr_stg", [1, cols], F32)
    p.dma("sync", stg[:], row_ap)
    p.step()
    for n in range((cols + 511) // 512):
        c0, c1 = n * 512, min((n + 1) * 512, cols)
        p.t("matmul", ps[:, 0:c1 - c0], ones1f[:], stg[:, c0:c1],
            start=True, stop=True)
        p.step()
        p.v("tensor_copy", dst_sb[:, c0:c1], ps[:, 0:c1 - c0])
        p.step()


def build_l0_attn_call(p, x_in_dram, x_out_dram, wf, wb, ff, tag="l0a"):
    xln_dram = p.dram(tag + "_xln", [T, D], F32)
    yf_dram = p.dram(tag + "_yf", [T, D], F32)
    yb_dram = p.dram(tag + "_yb", [T, D], F32)
    xblk_dram = p.dram(tag + "_xblk", [T, D], F32)

    dsc = p.dram(tag + "_dsc", [T, D], BF16)
    with p.scope():
        xt = p.sb(tag + "_xt", [128, D], F32)
        lnf = p.sb(tag + "_lnf", [128, D], F32)
        hib = p.sb(tag + "_hib", [128, D], BF16)
        sc = ln_scratch(p)
        for t in range(NT):
            p.dma("sync", xt[:], x_in_dram.ap()[t * 128:(t + 1) * 128, :])
            p.step()
            p.v("reduce_sum", sc["m"][:], xt[:], AX.X)
            p.step()
            p.v("tensor_scalar_mul", sc["m"][:], sc["m"][:], 1.0 / D)
            p.step()
            p.s("activation", sc["t1"][:], xt[:], AF.Square, accum_out=sc["sq"][:])
            p.step()
            p.v("tensor_scalar_mul", sc["sq"][:], sc["sq"][:], 1.0 / D)
            p.step()
            p.s("activation", sc["var"][:], sc["m"][:], AF.Square)
            p.step()
            p.v("tensor_sub", sc["var"][:], sc["sq"][:], sc["var"][:])
            p.step()
            p.s("activation", sc["rstd"][:], sc["var"][:], AF.Ln,
                bias=p.fconst(1e-5)[:])
            p.step()
            p.s("activation", sc["rstd"][:], sc["rstd"][:], AF.Exp,
                scale=p.fconst(-0.5)[:])
            p.step()
            p.v("tensor_scalar", lnf[:], xt[:], sc["m"][:], sc["rstd"][:],
                ALU.subtract, ALU.mult)
            p.step()
            p.dma("sync", xln_dram.ap()[t * 128:(t + 1) * 128, :], lnf[:])
            p.step()
            p.v("tensor_copy", hib[:], lnf[:])
            p.step()
            p.dma("sync", dsc.ap()[t * 128:(t + 1) * 128, :], hib[:])
            p.step()

    for dirname, rev, wdir, ydir in (("mf", False, wf, yf_dram),
                                     ("mb", True, wb, yb_dram)):
        with p.scope():
            build_mamba_dir(p, dsc, wdir, ydir, tag + dirname, reverse=rev)

    xffT = p.sb(tag + "_xffT", [128, 8 * T], BF16)
    ps0 = p.ps(tag + "_cps0", [128, 512], F32)
    ps1 = p.ps(tag + "_cps1", [128, 512], F32)
    ones1f = p.sb(tag + "_ones1f", [1, 128], F32)
    p.v("memset", ones1f[:], 1.0)
    p.step()
    g_bc = p.sb(tag + "_gbc", [128, D], F32)
    b_bc = p.sb(tag + "_bbc", [128, D], F32)
    b2_bc = p.sb(tag + "_b2bc", [128, D], F32)
    with p.scope():
        bcast_row_f32(p, ff["ln_g"], g_bc, ps0, ones1f)
        bcast_row_f32(p, ff["ln_b"], b_bc, ps0, ones1f)
        bcast_row_f32(p, ff["b2"], b2_bc, ps0, ones1f)
    with p.scope():
        xt = p.sb(tag + "_cxt", [128, D], F32)
        t2 = p.sb(tag + "_ct2", [128, D], F32)
        hib = p.sb(tag + "_chib", [128, D], BF16)
        sc = ln_scratch(p)
        dsc2 = p.dram(tag + "_dsc2", [T, D], BF16)
        for t in range(NT):
            r0, r1 = t * 128, (t + 1) * 128
            p.dma("sync", xt[:], yf_dram.ap()[r0:r1, :])
            p.step()
            p.dma("sync", t2[:], yb_dram.ap()[r0:r1, :])
            p.step()
            p.v("tensor_add", xt[:], xt[:], t2[:])
            p.step()
            p.dma("sync", t2[:], xln_dram.ap()[r0:r1, :])
            p.step()
            p.v("scalar_tensor_tensor", xt[:], xt[:], 0.5, t2[:],
                ALU.mult, ALU.add)
            p.step()
            p.dma("sync", xblk_dram.ap()[r0:r1, :], xt[:])
            p.step()
            p.v("reduce_sum", sc["m"][:], xt[:], AX.X)
            p.step()
            p.v("tensor_scalar_mul", sc["m"][:], sc["m"][:], 1.0 / D)
            p.step()
            p.s("activation", sc["t1"][:], xt[:], AF.Square, accum_out=sc["sq"][:])
            p.step()
            p.v("tensor_scalar_mul", sc["sq"][:], sc["sq"][:], 1.0 / D)
            p.step()
            p.s("activation", sc["var"][:], sc["m"][:], AF.Square)
            p.step()
            p.v("tensor_sub", sc["var"][:], sc["sq"][:], sc["var"][:])
            p.step()
            p.s("activation", sc["rstd"][:], sc["var"][:], AF.Ln,
                bias=p.fconst(1e-5)[:])
            p.step()
            p.s("activation", sc["rstd"][:], sc["rstd"][:], AF.Exp,
                scale=p.fconst(-0.5)[:])
            p.step()
            p.v("tensor_scalar", t2[:], xt[:], sc["m"][:], sc["rstd"][:],
                ALU.subtract, ALU.mult)
            p.step()
            p.v("tensor_mul", t2[:], t2[:], g_bc[:])
            p.step()
            p.v("tensor_add", t2[:], t2[:], b_bc[:])
            p.step()
            p.v("tensor_copy", hib[:], t2[:])
            p.step()
            p.dma("sync", dsc2.ap()[t * 128:(t + 1) * 128, :], hib[:])
            p.step()
        for d8 in range(8):
            p.dma("sync", xffT[:, d8 * T:(d8 + 1) * T],
                  dsc2.ap()[:, d8 * 128:(d8 + 1) * 128], transpose=True)
            p.step()

    TH = 1024
    h1 = p.sb(tag + "_h1", [128, 32 * TH], BF16)
    b1l = p.sb(tag + "_b1l", [128, 32], F32)
    p.dma("sync", b1l[:], ff["b1_l"])
    p.step()
    with p.scope():
        w1c = p.sb(tag + "_w1c", [128, 8 * 128], BF16)
        wstg1k = p.sb(tag + "_wstg5", [128, 8 * 128], F32)
        w2h = p.sb(tag + "_w2h", [128, 32 * 512], BF16)
        w2hf = p.sb(tag + "_w2hf", [128, 16 * 512], F32)
        wstg6 = p.sb(tag + "_wstg6", [128, 512], F32)
        xres = p.sb(tag + "_xres", [128, 512], F32)
        for h in range(2):
            tsl = slice(h * TH, (h + 1) * TH)
            for ft in range(32):
                dma_wload(p, wstg1k.ap(), ff["w1T"][:, ft * 128:(ft + 1) * 128],
                          1024, 128)
                p.v("tensor_copy", w1c[:], wstg1k[:])
                for n in range(2):
                    bank = ps0 if n == 0 else ps1
                    for k in range(8):
                        p.t("matmul", bank[:],
                            w1c[:, k * 128:(k + 1) * 128],
                            xffT[:, k * T + h * TH + n * 512:
                                 k * T + h * TH + n * 512 + 512],
                            start=(k == 0), stop=(k == 7))
                    p.step()
                    p.s("activation",
                        h1[:, ft * TH + n * 512: ft * TH + n * 512 + 512],
                        bank[:], AF.Silu, bias=b1l[:, ft:ft + 1])
                    p.step()
            for nh in range(2):
                for hh in range(2):
                    dma_wload(p, w2hf.ap(),
                              ff["w2T"][hh * 2048:(hh + 1) * 2048,
                                        nh * 512:(nh + 1) * 512], 2048, 512)
                    p.v("tensor_copy",
                        w2h[:, hh * 16 * 512:(hh + 1) * 16 * 512], w2hf[:])
                for tt in range(8):
                    r0 = (h * 8 + tt) * 128
                    bank = ps0 if tt % 2 == 0 else ps1
                    for k in range(32):
                        p.t("matmul", bank[:],
                            h1[:, k * TH + tt * 128: k * TH + tt * 128 + 128],
                            w2h[:, k * 512:(k + 1) * 512],
                            start=(k == 0), stop=(k == 31))
                    p.step()
                    p.dma("sync", xres[:],
                          x_in_dram.ap()[r0:r0 + 128, nh * 512:(nh + 1) * 512])
                    p.step()
                    p.v("tensor_add", xres[:], xres[:], bank[:])
                    p.step()
                    p.v("tensor_add", xres[:], xres[:],
                        b2_bc[:, nh * 512:(nh + 1) * 512])
                    p.step()
                    p.dma("sync", wstg6[:],
                          xblk_dram.ap()[r0:r0 + 128, nh * 512:(nh + 1) * 512])
                    p.step()
                    p.v("tensor_add", xres[:], xres[:], wstg6[:])
                    p.step()
                    p.dma("sync",
                          x_out_dram.ap()[r0:r0 + 128, nh * 512:(nh + 1) * 512],
                          xres[:])
                    p.step()



NE = 32
CAP = 1024
NSLOT = NE * CAP
TRASH = NSLOT


def build_l0_router(p, x_in_dram, gwT_ap, xln_out, hloc_out,
                    expv_out, tag):
    xlnT_hi = p.sb(tag + "_Thi", [128, 8 * T], BF16)
    xlnT_lo = p.sb(tag + "_Tlo", [128, 8 * T], BF16)
    with p.scope():
        build_ln_and_transpose_into(p, x_in_dram, xlnT_hi, xlnT_lo, tag + "ln",
                                    tok_out_dram=xln_out)
    rhi = p.sb(tag + "_rhi", [128, 8 * 32], BF16)
    rlo = p.sb(tag + "_rlo", [128, 8 * 32], BF16)
    with p.scope():
        rstg = p.sb(tag + "_rstg", [128, 32], F32)
        rtmp = p.sb(tag + "_rtmp", [128, 32], F32)
        for k in range(8):
            ks = slice(k * 32, (k + 1) * 32)
            p.dma("sync", rstg[:], gwT_ap[k * 128:(k + 1) * 128, :])
            p.step()
            p.v("tensor_copy", rhi[:, ks], rstg[:])
            p.step()
            p.v("tensor_copy", rtmp[:], rhi[:, ks])
            p.step()
            p.v("tensor_sub", rtmp[:], rstg[:], rtmp[:])
            p.step()
            p.v("tensor_copy", rlo[:, ks], rtmp[:])
            p.step()
    lg_ps = p.ps(tag + "_lgps", [128, 32], F32)
    lg = p.sb(tag + "_lg", [128, 32], F32)
    ex = p.sb(tag + "_ex", [128, 32], F32)
    m8 = p.sb(tag + "_m8", [128, 8], F32)
    Hgt = p.sb(tag + "_Hgt", [128, 32], F32)
    Tt = p.sb(tag + "_Tt", [128, 32], F32)
    scn = p.sb(tag + "_scn", [128, 32], F32)
    cnt = p.sb(tag + "_cnt", [128, 1], F32)
    mx = p.sb(tag + "_mx", [128, 1], F32)
    lim = p.sb(tag + "_lim", [128, 1], F32)
    Hb = p.sb(tag + "_Hb", [128, 32], BF16)
    for t in range(NT):
        ts = slice(t * 128, (t + 1) * 128)
        for k in range(8):
            ks = slice(k * 32, (k + 1) * 32)
            xs_ = slice(k * T + t * 128, k * T + t * 128 + 128)
            p.t("matmul", lg_ps[:], xlnT_hi[:, xs_], rhi[:, ks],
                start=(k == 0), stop=False)
            p.t("matmul", lg_ps[:], xlnT_hi[:, xs_], rlo[:, ks],
                start=False, stop=False)
            p.t("matmul", lg_ps[:], xlnT_lo[:, xs_], rhi[:, ks],
                start=False, stop=(k == 7))
        p.step()
        p.v("tensor_copy", lg[:], lg_ps[:])
        p.step()
        p.v("reduce_max", mx[:], lg[:], AX.X)
        p.step()
        p.v("tensor_scalar_mul", mx[:], mx[:], -1.0)
        p.step()
        p.s("activation", ex[:], lg[:], AF.Exp, bias=mx[:])
        p.step()
        p.dma("sync", expv_out.ap()[ts, :], ex[:])
        p.step()
        p.v("max", out=m8[:], in_=lg[:])
        p.step()
        p.v("tensor_scalar", Hgt[:], lg[:], m8[:, 7:8], None, ALU.is_gt)
        p.step()
        p.v("tensor_scalar", Tt[:], lg[:], m8[:, 7:8], None, ALU.is_equal)
        p.step()
        p.v("tensor_reduce", cnt[:], Hgt[:], AX.X, ALU.add)
        p.step()
        p.v("tensor_tensor_scan", scn[:], Tt[:], Tt[:], 0.0, ALU.add, ALU.bypass)
        p.step()
        p.v("tensor_scalar", lim[:], cnt[:], -1.0, 8.0, ALU.mult, ALU.add)
        p.step()
        p.v("tensor_scalar", scn[:], scn[:], lim[:], None, ALU.is_le)
        p.step()
        p.v("tensor_mul", scn[:], scn[:], Tt[:])
        p.step()
        p.v("tensor_add", Hgt[:], Hgt[:], scn[:])
        p.step()
        p.v("tensor_copy", Hb[:], Hgt[:])
        p.step()
        p.dma("sync", hloc_out.ap()[ts, :], Hb[:])
        p.step()


def build_l0_s2(p, hfull_ap, hloc_ap, expv_ap, tokid_ap, rows_ap,
                ptable_out, gidx_out, w_out, tag):
    N = 8 * T
    with p.scope():
        zt = p.sb(tag + "_zt", [128, 257], I32)
        p.v("memset", zt[:], 0)
        p.step()
        p.dma("sync",
              ptable_out.ap().rearrange("(p a) w -> p (a w)", p=128), zt[:])
        p.step()
    pos_all = p.dram(tag + "_posall", [N, NE], F32)
    with p.scope():
        incl = p.sb(tag + "_incl", [32, N], F32)
        hf = p.sb(tag + "_hf", [32, N], BF16)
        p.dma("sync", hf[:], hfull_ap)
        p.step()
        p.v("tensor_tensor_scan", incl[:], hf[:], hf[:], 0.0, ALU.add,
            ALU.bypass)
        p.step()
        p.v("tensor_sub", incl[:], incl[:], hf[:])
        p.step()
        identf = p.sb(tag + "_identf", [128, 128], F32)
        p.dma("sync", identf[:],
              p.const("identf128", np.eye(128, dtype=np.float32)).ap()[:])
        p.step()
        psT = p.ps(tag + "_psT", [128, 128], F32)
        ptk = p.sb(tag + "_ptk", [128, 32], F32)
        for t in range(N // 128):
            p.t("transpose", psT[:, 0:32], incl[:, t * 128:(t + 1) * 128],
                identf[0:32, 0:32])
            p.step()
            p.v("tensor_copy", ptk[:], psT[:, 0:32])
            p.step()
            p.dma("sync", pos_all.ap()[t * 128:(t + 1) * 128, :], ptk[:])
            p.step()
    pos_tok = p.sb(tag + "_postok", [128, 32], F32)
    htok = p.sb(tag + "_htok", [128, 32], F32)
    htokb = p.sb(tag + "_htokb", [128, 32], BF16)
    expv = p.sb(tag + "_expv", [128, 32], F32)
    hkept = p.sb(tag + "_hkept", [128, 32], F32)
    r2 = p.sb(tag + "_r2", [128, 32], F32)
    A = p.sb(tag + "_A", [128, 32], F32)
    big_t = p.sb(tag + "_big", [128, 32], F32)
    e1024 = p.sb(tag + "_e1024", [128, 32], F32)
    trash32 = p.sb(tag + "_trash", [128, 32], F32)
    p.v("memset", big_t[:], 2.0e8)
    p.step()
    p.dma("sync", e1024[:],
          p.const("e1024", np.broadcast_to(
              (np.arange(32, dtype=np.float32) * CAP)[None, :],
              (128, 32)).copy()).ap()[:])
    p.step()
    gidx_f = p.sb(tag + "_gidxf", [128, 8], F32)
    praw = p.sb(tag + "_praw", [128, 8], F32)
    keptk = p.sb(tag + "_keptk", [128, 8], F32)
    wsum = p.sb(tag + "_wsum", [128, 1], F32)
    wt = p.sb(tag + "_wt", [128, 8], F32)
    gidx_i = p.sb(tag + "_gidxi", [128, 8], I32)
    sidx_f = p.sb(tag + "_sidxf", [128, 8], F32)
    sidx_i = p.sb(tag + "_sidxi", [128, 8], I32)
    tki = p.sb(tag + "_tki", [128, 1], I32)
    rws = p.sb(tag + "_rws", [128, 1], I32)
    for t in range(NT):
        ts = slice(t * 128, (t + 1) * 128)
        p.dma("sync", rws[:], rows_ap[ts, :])
        p.step()
        p.g("indirect_dma_start", pos_tok[:], None, pos_all.ap()[:, :],
            IndirectOffsetOnAxis(rws.ap() if False else rws[:], 0))
        p.step()
        p.dma("sync", htokb[:], hloc_ap[ts, :])
        p.step()
        p.v("tensor_copy", htok[:], htokb[:])
        p.step()
        p.dma("sync", expv[:], expv_ap[ts, :])
        p.step()
        p.dma("sync", tki[:], tokid_ap[ts, :])
        p.step()
        p.v("tensor_scalar", hkept[:], pos_tok[:], float(CAP) - 0.5, None,
            ALU.is_lt)
        p.step()
        p.v("tensor_mul", hkept[:], hkept[:], htok[:])
        p.step()
        p.v("tensor_tensor_scan", r2[:], htok[:], htok[:], 0.0, ALU.add,
            ALU.bypass)
        p.step()
        p.v("tensor_mul", r2[:], r2[:], htok[:])
        p.step()
        p.v("scalar_tensor_tensor", A[:], pos_tok[:], 1.0, e1024[:],
            ALU.add, ALU.add)
        p.step()
        p.v("tensor_mul", A[:], A[:], hkept[:])
        p.step()
        p.v("tensor_scalar", trash32[:], hkept[:], -2.0e8, 2.0e8,
            ALU.mult, ALU.add)
        p.step()
        p.v("tensor_add", A[:], A[:], trash32[:])
        p.step()
        for k in range(8):
            p.v("scalar_tensor_tensor", trash32[:], r2[:],
                float(k + 1), A[:], ALU.is_equal, ALU.mult,
                accum_out=gidx_f[:, k:k + 1])
            p.step()
            p.v("scalar_tensor_tensor", trash32[:], r2[:],
                float(k + 1), expv[:], ALU.is_equal, ALU.mult,
                accum_out=praw[:, k:k + 1])
            p.step()
            p.v("scalar_tensor_tensor", trash32[:], r2[:],
                float(k + 1), hkept[:], ALU.is_equal, ALU.mult,
                accum_out=keptk[:, k:k + 1])
            p.step()
        p.v("tensor_reduce", wsum[:], praw[:], AX.X, ALU.add)
        p.step()
        p.v("reciprocal", wsum[:], wsum[:])
        p.step()
        p.v("tensor_scalar_mul", wt[:], praw[:], wsum[:])
        p.step()
        p.v("tensor_mul", wt[:], wt[:], keptk[:])
        p.step()
        p.dma("sync", w_out.ap()[ts, :], wt[:])
        p.step()
        p.v("tensor_scalar", sidx_f[:], gidx_f[:], -1.0, float(TRASH),
            ALU.add, ALU.min)
        p.step()
        p.v("tensor_copy", sidx_i[:], sidx_f[:])
        p.step()
        p.v("tensor_scalar", gidx_f[:], gidx_f[:], -1.0, 0.0, ALU.add, ALU.max)
        p.step()
        p.v("tensor_scalar_min", gidx_f[:], gidx_f[:], float(NSLOT - 1))
        p.step()
        p.v("tensor_copy", gidx_i[:], gidx_f[:])
        p.step()
        p.dma("sync", gidx_out.ap()[ts, :], gidx_i[:])
        p.step()
        for k in range(8):
            p.g("indirect_dma_start", ptable_out.ap()[:, :],
                IndirectOffsetOnAxis(sidx_i[:, k:k + 1], 0), tki[:], None,
                grp="ptscatter")
        p.step()


def build_l0_s3(p, ptable_my_ap, xlnfull_dram, gupT_ap, downT_ap, eo_out, tag):
    tblf = p.sb(tag + "_tblf", [32, 128], F32)
    tmp = p.sb(tag + "_tmp", [32, 128], F32)
    tmpi = p.sb(tag + "_tmpi", [32, 128], I32)
    identf = p.sb(tag + "_idf", [128, 128], F32)
    p.dma("sync", identf[:],
          p.const("identf128", np.eye(128, dtype=np.float32)).ap()[:])
    p.step()
    psT = p.ps(tag + "_psT", [128, 128], F32)
    p.v("memset", tblf[:], 0.0)
    p.step()
    for r in range(8):
        p.dma("sync", tmpi[:],
              ptable_my_ap[r:r + 1, :].rearrange("w (p a) -> (w p) a", p=32))
        p.step()
        p.v("tensor_copy", tmp[:], tmpi[:])
        p.step()
        p.v("tensor_add", tblf[:], tblf[:], tmp[:])
        p.step()
    p.v("tensor_scalar", tblf[:], tblf[:], -1.0, 0.0, ALU.add, ALU.max)
    p.step()
    p.v("tensor_scalar_min", tblf[:], tblf[:], float(8 * T - 1))
    p.step()
    p.t("transpose", psT[:, 0:32], tblf[:], identf[0:32, 0:32])
    p.step()
    idxf = p.sb(tag + "_idxf", [128, 32], F32)
    idxi = p.sb(tag + "_idxi", [128, 32], I32)
    p.v("tensor_copy", idxf[:], psT[:, 0:32])
    p.step()
    p.v("tensor_copy", idxi[:], idxf[:])
    p.step()

    xrows = [p.sb(tag + "_xrows", [128, D], BF16) for _ in range(2)]
    xstage = [p.dram(tag + "_xstage", [128, D], BF16) for _ in range(2)]
    xeT = [p.sb(tag + "_xeT", [128, 8 * 128], BF16) for _ in range(2)]
    wgup = [p.sb(tag + "_wgup", [128, 8 * 1536], BF16) for _ in range(2)]
    wdn = [p.sb(tag + "_wdn", [128, 6 * 1024], BF16) for _ in range(2)]
    wstg8 = [p.sb(tag + "_wstg8", [128, 8 * 1536], F32) for _ in range(2)]
    gs = p.sb(tag + "_gs", [128, 6 * 128], BF16)
    eob = [p.sb(tag + "_eob", [128, 1024], BF16) for _ in range(2)]
    ps0 = p.ps(tag + "_ps0", [128, 512], F32)
    ps1 = p.ps(tag + "_ps1", [128, 512], F32)
    for e in range(NE):
        b = e % 2
        p.g("indirect_dma_start", xrows[b][:], None, xlnfull_dram.ap()[:, :],
            IndirectOffsetOnAxis(idxi[:, e:e + 1], 0))
        p.dma("sync", xstage[b].ap()[:, :], xrows[b][:])
        for d8 in range(8):
            p.dma("sync", xeT[b][:, d8 * 128:(d8 + 1) * 128],
                  xstage[b].ap()[:, d8 * 128:(d8 + 1) * 128], transpose=True)
        dma_wload(p, wstg8[b].ap(), gupT_ap[e], 1024, 1536)
        p.v("tensor_copy", wgup[b][:], wstg8[b][:])
        for mt in range(12):
            bank = ps0 if mt % 2 == 0 else ps1
            for k in range(8):
                p.t("matmul", bank[:, 0:128],
                    wgup[b][:, k * 1536 + mt * 128: k * 1536 + mt * 128 + 128],
                    xeT[b][:, k * 128:(k + 1) * 128],
                    start=(k == 0), stop=(k == 7))
            if mt < 6:
                p.s("activation", gs[:, mt * 128:(mt + 1) * 128],
                    bank[:, 0:128], AF.Silu)
            else:
                f = mt - 6
                p.v("tensor_mul", gs[:, f * 128:(f + 1) * 128],
                    bank[:, 0:128], gs[:, f * 128:(f + 1) * 128])
        dma_wload(p, wstg8[b].ap()[:, 0:6 * 1024], downT_ap[e], 768, 1024)
        p.v("tensor_copy", wdn[b][:], wstg8[b][:, 0:6 * 1024])
        for n in range(2):
            bank = ps0 if n == 0 else ps1
            for k in range(6):
                p.t("matmul", bank[:],
                    gs[:, k * 128:(k + 1) * 128],
                    wdn[b][:, k * 1024 + n * 512: k * 1024 + n * 512 + 512],
                    start=(k == 0), stop=(k == 5))
            p.v("tensor_copy", eob[b][:, n * 512:(n + 1) * 512], bank[:])
        p.dma("sync", eo_out.ap()[e * 128:(e + 1) * 128, :], eob[b][:])


def build_l0_s4(p, x_in_dram, eofull_dram, gidx_ap, w_ap, x_out_dram, tag):
    acc = p.sb(tag + "_acc", [128, D], F32)
    grows = p.sb(tag + "_grows", [128, D], BF16)
    wt = p.sb(tag + "_wt", [128, 8], F32)
    gix = p.sb(tag + "_gix", [128, 8], I32)
    xres = p.sb(tag + "_xres", [128, D], F32)
    for t in range(NT):
        ts = slice(t * 128, (t + 1) * 128)
        p.dma("sync", wt[:], w_ap[ts, :])
        p.step()
        p.dma("sync", gix[:], gidx_ap[ts, :])
        p.step()
        p.v("memset", acc[:], 0.0)
        p.step()
        for k in range(8):
            p.g("indirect_dma_start", grows[:], None, eofull_dram.ap()[:, :],
                IndirectOffsetOnAxis(gix[:, k:k + 1], 0))
            p.step()
            p.v("scalar_tensor_tensor", acc[:], grows[:], wt[:, k:k + 1],
                acc[:], ALU.mult, ALU.add)
            p.step()
        p.dma("sync", xres[:], x_in_dram.ap()[ts, :])
        p.step()
        p.v("scalar_tensor_tensor", xres[:], acc[:], 0.5, xres[:],
            ALU.mult, ALU.add)
        p.step()
        p.dma("sync", x_out_dram.ap()[ts, :], xres[:])
        p.step()



def _runp(p, in_maps):
    return p.run(in_maps)


def _mamba_host_layout(pref, I):
    def vec_l(v, w_=16):
        return np.ascontiguousarray(v.reshape(w_, 128).T)
    return {
        "in_wT": np.ascontiguousarray(I[pref + "in_w"].T),
        "x_wT": np.ascontiguousarray(I[pref + "x_w"].T),
        "dt_wT": np.ascontiguousarray(I[pref + "dt_w"].T),
        "out_wT": np.ascontiguousarray(I[pref + "out_w"].T),
        "conv_w_l": np.ascontiguousarray(
            I[pref + "conv_w"].reshape(16, 128, 4).transpose(1, 0, 2)
            .reshape(128, 64)),
        "conv_b_l": vec_l(I[pref + "conv_b"]),
        "dt_b_l": vec_l(I[pref + "dt_b"]),
        "D_l": vec_l(I[pref + "D"]),
        "A_log_l": np.ascontiguousarray(
            I[pref + "A_log"].reshape(16, 128, 16).transpose(1, 0, 2)
            .reshape(128, 256)),
    }


def _pad_qk(W):
    WT = np.ascontiguousarray(W.T)
    out = np.zeros((1024, 512), np.float32)
    for h in range(16):
        out[:, 32 * h:32 * h + 16] = WT[:, 16 * h:16 * h + 16]
    return out


MAMBA_W_NAMES = ["in_wT", "x_wT", "dt_wT", "out_wT", "conv_w_l", "conv_b_l",
                 "dt_b_l", "D_l", "A_log_l"]
MAMBA_W_SHAPES = {"in_wT": [1024, 4096], "x_wT": [2048, 96],
                  "dt_wT": [64, 2048], "out_wT": [2048, 1024],
                  "conv_w_l": [128, 64], "conv_b_l": [128, 16],
                  "dt_b_l": [128, 16], "D_l": [128, 16],
                  "A_log_l": [128, 256]}
N_ALL = 8 * T


def _launch_s1(x_shards, gwT, tagname):
    p = Prog(tagname)
    x_in = p.dram("x", [T, D], F32, kind="ExternalInput")
    gw = p.dram("gwT", [1024, 32], F32, kind="ExternalInput")
    xln_o = p.dram("xln", [T, D], BF16, kind="ExternalOutput")
    hloc_o = p.dram("hloc", [T, 32], BF16, kind="ExternalOutput")
    expv_o = p.dram("expv", [T, 32], F32, kind="ExternalOutput")
    build_l0_router(p, x_in, gw.ap(), xln_o, hloc_o, expv_o, "r")
    return p.run([{"x": x_shards[c], "gwT": gwT} for c in range(8)])


def _launch_s2(r1, tagname):
    hfull = np.concatenate([r1[c]["hloc"] for c in range(8)], axis=0)
    hfullT = np.ascontiguousarray(hfull.T)
    p = Prog(tagname)
    hf_in = p.dram("hfull", [32, N_ALL], BF16, kind="ExternalInput")
    hl_in = p.dram("hloc", [T, 32], BF16, kind="ExternalInput")
    ev_in = p.dram("expv", [T, 32], F32, kind="ExternalInput")
    tk_in = p.dram("tokid", [T, 1], I32, kind="ExternalInput")
    rw_in = p.dram("rows", [T, 1], I32, kind="ExternalInput")
    pt_o = p.dram("ptable", [NSLOT + 128, 1], I32, kind="ExternalOutput")
    gi_o = p.dram("gidx", [T, 8], I32, kind="ExternalOutput")
    w_o = p.dram("w", [T, 8], F32, kind="ExternalOutput")
    build_l0_s2(p, hf_in.ap(), hl_in.ap(), ev_in.ap(), tk_in.ap(), rw_in.ap(),
                pt_o, gi_o, w_o, "s2")
    return p.run([{
        "hfull": hfullT, "hloc": r1[c]["hloc"], "expv": r1[c]["expv"],
        "tokid": (np.arange(T) + c * T + 1).reshape(-1, 1).astype(np.int32),
        "rows": (np.arange(T) + c * T).reshape(-1, 1).astype(np.int32),
    } for c in range(8)])


def _launch_s3(r1, r2, gupT, downT, tagname):
    ptables = np.stack([r2[c]["ptable"][:, 0] for c in range(8)])
    xlnfull = np.concatenate([r1[c]["xln"] for c in range(8)], axis=0)
    pt_my = []
    for c in range(8):
        sel = np.zeros((8, 32, 128), np.int32)
        for e in range(NE):
            sel[:, e, :] = ptables[:, e * CAP + c * 128: e * CAP + (c + 1) * 128]
        pt_my.append(sel.reshape(8, 4096))
    p = Prog(tagname)
    ptm_in = p.dram("ptm", [8, 4096], I32, kind="ExternalInput")
    xlf_in = p.dram("xlnfull", [N_ALL, D], BF16, kind="ExternalInput")
    gu_in = p.dram("gupT", [32, 1024, 1536], F32, kind="ExternalInput")
    dn_in = p.dram("downT", [32, 768, 1024], F32, kind="ExternalInput")
    eo_o = p.dram("eo", [4096, D], BF16, kind="ExternalOutput")
    build_l0_s3(p, ptm_in.ap(), xlf_in, gu_in.ap(), dn_in.ap(), eo_o, "s3")
    return p.run([{"ptm": pt_my[c], "xlnfull": xlnfull, "gupT": gupT,
                   "downT": downT} for c in range(8)])


def _eo_full(r3):
    eof = np.zeros((NSLOT, D), bf16)
    for c in range(8):
        part = r3[c]["eo"].reshape(NE, 128, D)
        for e in range(NE):
            eof[e * CAP + c * 128: e * CAP + (c + 1) * 128] = part[e]
    return eof


def kernel(**I):
    x = np.ascontiguousarray(I["L0_attn_x"]).astype(np.float32, copy=False)
    x_shards = [np.ascontiguousarray(x[c]) for c in range(8)]
    gwT = np.ascontiguousarray(I["L0_ffn_gate_weight"].T)
    gupT = np.ascontiguousarray(np.asarray(I["L0_ffn_gate_up_proj"])
                                .transpose(0, 2, 1))
    downT = np.ascontiguousarray(np.asarray(I["L0_ffn_down_proj"])
                                 .transpose(0, 2, 1))
    wf_h = _mamba_host_layout("L0_attn_fwd_", I)
    wb_h = _mamba_host_layout("L0_attn_bwd_", I)
    ff_h = {
        "w1T": np.ascontiguousarray(I["L0_attn_ff_w1"].T),
        "w2T": np.ascontiguousarray(I["L0_attn_ff_w2"].T),
        "b1_l": np.ascontiguousarray(I["L0_attn_ff_b1"].reshape(32, 128).T),
        "ln_g": np.ascontiguousarray(I["L0_attn_ff_ln_g"].reshape(1, -1)),
        "ln_b": np.ascontiguousarray(I["L0_attn_ff_ln_b"].reshape(1, -1)),
        "b2": np.ascontiguousarray(I["L0_attn_ff_b2"].reshape(1, -1)),
    }
    FF_SHAPES = {"w1T": [1024, 4096], "w2T": [4096, 1024], "b1_l": [128, 32],
                 "ln_g": [1, 1024], "ln_b": [1, 1024], "b2": [1, 1024]}

    r1 = _launch_s1(x_shards, gwT, "L1s1")
    r2 = _launch_s2(r1, "L2s2")
    r3 = _launch_s3(r1, r2, gupT, downT, "L3s3")
    eof1 = _eo_full(r3)

    p = Prog("L4big")
    x_in = p.dram("x", [T, D], F32, kind="ExternalInput")
    eo_in = p.dram("eofull", [NSLOT, D], BF16, kind="ExternalInput")
    gi_in = p.dram("gidx", [T, 8], I32, kind="ExternalInput")
    w_in = p.dram("w", [T, 8], F32, kind="ExternalInput")
    gw_in = p.dram("gwT", [1024, 32], F32, kind="ExternalInput")
    wf = {n: p.dram("f_" + n, MAMBA_W_SHAPES[n], F32,
                    kind="ExternalInput").ap() for n in MAMBA_W_NAMES}
    wb = {n: p.dram("b_" + n, MAMBA_W_SHAPES[n], F32,
                    kind="ExternalInput").ap() for n in MAMBA_W_NAMES}
    ff = {n: p.dram("ff_" + n, FF_SHAPES[n], F32,
                    kind="ExternalInput").ap() for n in FF_SHAPES}
    x2_o = p.dram("x2", [T, D], F32, kind="ExternalOutput")
    xln_o = p.dram("xln", [T, D], BF16, kind="ExternalOutput")
    hloc_o = p.dram("hloc", [T, 32], BF16, kind="ExternalOutput")
    expv_o = p.dram("expv", [T, 32], F32, kind="ExternalOutput")
    x1_d = p.dram("x1scr", [T, D], F32)
    with p.scope():
        build_l0_s4(p, x_in, eo_in, gi_in.ap(), w_in.ap(), x1_d, "s4")
    with p.scope():
        build_l0_attn_call(p, x1_d, x2_o, wf, wb, ff)
    with p.scope():
        build_l0_router(p, x2_o, gw_in.ap(), xln_o, hloc_o, expv_o, "r2")
    im4 = []
    for c in range(8):
        m = {"x": x_shards[c], "eofull": eof1, "gidx": r2[c]["gidx"],
             "w": r2[c]["w"], "gwT": gwT}
        for n in MAMBA_W_NAMES:
            m["f_" + n] = wf_h[n]
            m["b_" + n] = wb_h[n]
        for n in FF_SHAPES:
            m["ff_" + n] = ff_h[n]
        im4.append(m)
    r4 = _runp(p, im4)

    r5 = _launch_s2(r4, "L5s2")
    r6 = _launch_s3(r4, r5, gupT, downT, "L6s3")
    eof2 = _eo_full(r6)

    p = Prog("L7big")
    x_in = p.dram("x", [T, D], F32, kind="ExternalInput")
    eo_in = p.dram("eofull", [NSLOT, D], BF16, kind="ExternalInput")
    gi_in = p.dram("gidx", [T, 8], I32, kind="ExternalInput")
    w_in = p.dram("w", [T, 8], F32, kind="ExternalInput")
    rt_in = p.dram("routerDE", [1024, 8], F32, kind="ExternalInput")
    ga_in = p.dram("gating", [8, 1024, 2048], F32, kind="ExternalInput")
    up_in = p.dram("upw", [8, 1024, 2048], F32, kind="ExternalInput")
    dn_in = p.dram("downw", [8, 2048, 1024], F32, kind="ExternalInput")
    wq_in = p.dram("wqTp", [1024, 512], F32, kind="ExternalInput")
    wk_in = p.dram("wkTp", [1024, 512], F32, kind="ExternalInput")
    wv_in = p.dram("wvT", [1024, 1024], F32, kind="ExternalInput")
    wo_in = p.dram("woT", [1024, 1024], F32, kind="ExternalInput")
    out_o = p.dram("out", [T, D], F32, kind="ExternalOutput")
    x3_d = p.dram("x3scr", [T, D], F32)
    x4_d = p.dram("x4scr", [T, D], F32)
    x5_d = p.dram("x5scr", [T, D], F32)
    ga_bf = p.dram("ga_bf", [8, 1024, 2048], BF16)
    up_bf = p.dram("up_bf", [8, 1024, 2048], BF16)
    dn_bf = p.dram("dn_bf", [8, 2048, 1024], BF16)
    with p.scope():
        build_l0_s4(p, x_in, eo_in, gi_in.ap(), w_in.ap(), x3_d, "s4b")
    with p.scope():
        build_l1_ffn_call(p, x3_d, x4_d, rt_in.ap(), ga_bf, up_bf, dn_bf,
                          tag="l1fa", cast_srcs=(ga_in, up_in, dn_in))
    with p.scope():
        build_l1_attn_call(p, x4_d, x5_d, wq_in.ap(), wk_in.ap(), wv_in.ap(),
                           wo_in.ap(), tag="l1at")
    with p.scope():
        build_l1_ffn_call(p, x5_d, out_o, rt_in.ap(), ga_bf, up_bf, dn_bf,
                          tag="l1fb")
    im7 = []
    l1w = {
        "routerDE": np.ascontiguousarray(I["L1_ffn_kernel_router_DE"]),
        "gating": np.ascontiguousarray(I["L1_ffn_kernel_gating_EDF"]),
        "upw": np.ascontiguousarray(I["L1_ffn_kernel_up_EDF"]),
        "downw": np.ascontiguousarray(I["L1_ffn_kernel_down_EFD"]),
        "wqTp": _pad_qk(np.asarray(I["L1_attn_Wq"])),
        "wkTp": _pad_qk(np.asarray(I["L1_attn_Wk"])),
        "wvT": np.ascontiguousarray(I["L1_attn_Wv"].T),
        "woT": np.ascontiguousarray(I["L1_attn_Wo"].T),
    }
    for c in range(8):
        m = {"x": r4[c]["x2"], "eofull": eof2, "gidx": r5[c]["gidx"],
             "w": r5[c]["w"]}
        m.update(l1w)
        im7.append(m)
    r7 = _runp(p, im7)
    return np.stack([r7[c]["out"] for c in range(8)]).astype(np.float32)


# version 07
# speedup vs baseline: 1.1351x; 1.1351x over previous
import numpy as np
import ml_dtypes
from contextlib import ExitStack, contextmanager
import concourse.bass as bass
import concourse.mybir as mybir
from concourse.bass_utils import run_bass_kernel_spmd
from concourse.bass import IndirectOffsetOnAxis

F32 = mybir.dt.float32
BF16 = mybir.dt.bfloat16
I32 = mybir.dt.int32
AF = mybir.ActivationFunctionType
ALU = mybir.AluOpType
AX = mybir.AxisListType
bf16 = ml_dtypes.bfloat16

NCORES = 8
T = 2048
D = 1024
NT = T // 128


class Prog:

    ENGINES = ("sync", "vector", "scalar", "tensor", "gpsimd")
    NH = 8
    NG = 4
    SEMS = ("v", "s", "t", "p") + tuple(f"h{i}" for i in range(8)) \
        + tuple(f"g{i}" for i in range(4))
    ESEM = {"vector": "v", "scalar": "s", "tensor": "t", "gpsimd": "p"}
    SPECIAL = {"max", "max_index", "max_with_indices", "match_replace"}

    def __init__(self, name="prog"):
        self.nc = bass.Bass("TRN2", target_bir_lowering=False, name=name)
        self.ctx = ExitStack()
        self.root_ctx = self.ctx
        self.ops = []
        self.cum = {k: 0 for k in self.SEMS}
        self._consts = {}
        self._fconsts = {}
        self._names = {}
        self.finalized = False
        self._writes = {}
        self._reads = {}
        self._special_ops = set()

    def _uname(self, base):
        i = self._names.get(base, 0)
        self._names[base] = i + 1
        return f"{base}_{i}" if i else base

    def sb(self, name, shape, dtype):
        return self.ctx.enter_context(
            self.nc.sbuf_tensor(self._uname(name), shape, dtype))

    def ps(self, name, shape, dtype=F32):
        return self.ctx.enter_context(
            self.nc.psum_tensor(self._uname(name), shape, dtype))

    def dram(self, name, shape, dtype, kind="Internal"):
        return self.nc.dram_tensor(self._uname(name), shape, dtype, kind=kind)

    def const(self, key, arr):
        if key not in self._consts:
            self._consts[key] = self.nc.inline_tensor(
                np.ascontiguousarray(arr), name=self._uname("c_" + key))
        return self._consts[key]

    def fconst(self, value):
        v = float(value)
        if v not in self._fconsts:
            tile = self.root_ctx.enter_context(
                self.nc.sbuf_tensor(self._uname("fc"), [128, 1], F32,
                                    side="right"))
            self._fconsts[v] = tile
            self._rec("vector", (lambda e, t=tile, vv=v: e.memset(t[:], vv)),
                      [], [tile[:]], False)
        return self._fconsts[v]

    @contextmanager
    def scope(self):
        es = ExitStack()
        old = self.ctx
        self.ctx = es
        try:
            yield
        finally:
            self.ctx = old
            es.close()
            self._fence = dict(self.cum)

    @staticmethod
    def _interval(ap):
        t = ap.tensor
        pairs = [tuple(x) for x in ap.ap]
        off = ap.offset
        lo = hi = 0
        for (step, cnt) in pairs[1:] if len(pairs) > 1 else pairs:
            d = step * (cnt - 1)
            if d >= 0:
                hi += d
            else:
                lo += d
        tname = type(t).__name__
        if tname == "DRamTensorHandle":
            if len(pairs) > 1:
                d0 = pairs[0][0] * (pairs[0][1] - 1)
                if d0 >= 0:
                    hi += d0
                else:
                    lo += d0
            return (t.name, off + lo, off + hi + 1)
        pstep = pairs[0][0] if pairs else 1
        if pstep <= 0:
            return (t.name, 0, 1 << 60)
        foff = off % pstep if pstep > 0 else off
        return (t.name, foff + lo, foff + hi + 1)

    @classmethod
    def _collect_aps(cls, args, kw):
        aps = []
        for a in list(args) + list(kw.values()):
            if isinstance(a, bass.AP):
                aps.append(a)
            elif isinstance(a, IndirectOffsetOnAxis):
                aps.append(a.ap)
            elif hasattr(a, "ap") and callable(getattr(a, "ap", None)) \
                    and type(a).__name__ in ("SBTensorHandle",
                                             "PSumTensorHandle",
                                             "DRamTensorHandle"):
                aps.append(a.ap())
        return aps

    def _rec(self, engine, fn, in_aps, out_aps, is_dma, grp=None,
             special=False):
        if is_dma:
            if engine in ("sync", "scalar"):
                self._hrot = (self._hrot + 1) % self.NH
                semkey = f"h{self._hrot}"
            else:
                self._grot = (self._grot + 1) % self.NG
                semkey = f"g{self._grot}"
            inc = 16
        else:
            semkey = self.ESEM[engine]
            inc = 1
        idx = len(self.ops)
        waits = {k: v for k, v in self._fence.items() if v > 0}
        if is_dma and self.cum[semkey] > 0:
            if waits.get(semkey, -1) < self.cum[semkey]:
                waits[semkey] = self.cum[semkey]

        def add_dep(j):
            if j is None or j == idx:
                return
            (jeng, jsem, jinc, jcum, jdma, jspecial) = self._opinfo[j]
            if (jeng == engine == "tensor" and not jdma and not is_dma):
                return
            if waits.get(jsem, -1) < jcum:
                waits[jsem] = jcum

        ivs_in = [self._interval(a) for a in in_aps]
        ivs_out = [self._interval(a) for a in out_aps]
        for (nm, lo, hi) in ivs_in:
            for (wlo, whi, j, _g) in self._writes.get(nm, ()):
                if wlo < hi and lo < whi:
                    add_dep(j)
        for (nm, lo, hi) in ivs_out:
            for (wlo, whi, j, g) in self._writes.get(nm, ()):
                if wlo < hi and lo < whi and not (grp is not None and g == grp):
                    add_dep(j)
            for (rlo, rhi, j) in self._reads.get(nm, ()):
                if rlo < hi and lo < rhi:
                    add_dep(j)
        cum = self.cum[semkey] + inc
        self.cum[semkey] = cum
        self._opinfo.append((engine, semkey, inc, cum, is_dma, special))
        self.ops.append((engine, fn, semkey, inc, waits))
        for (nm, lo, hi) in ivs_in:
            self._reads.setdefault(nm, []).append((lo, hi, idx))
            if len(self._reads[nm]) > 192:
                self._reads[nm] = self._merge_tail(self._reads[nm])
        for (nm, lo, hi) in ivs_out:
            self._writes.setdefault(nm, []).append((lo, hi, idx, grp))
            if len(self._writes[nm]) > 192:
                self._writes[nm] = self._merge_tailw(self._writes[nm])
        return idx

    @staticmethod
    def _merge_tail(lst):
        old = lst[:-96]
        lo = min(x[0] for x in old)
        hi = max(x[1] for x in old)
        j = max(x[2] for x in old)
        return [(lo, hi, j)] + lst[-96:]

    @staticmethod
    def _merge_tailw(lst):
        old = lst[:-96]
        lo = min(x[0] for x in old)
        hi = max(x[1] for x in old)
        j = max(x[2] for x in old)
        return [(lo, hi, j, None)] + lst[-96:]

    OUT_KW = {"accum_out"}

    def _classify(self, meth, args, kw):
        n_out = 1
        outs, ins = [], []
        pos = list(args)
        if meth == "memset":
            if pos and isinstance(pos[0], bass.AP):
                outs.append(pos[0])
            return ins, outs
        if meth == "indirect_dma_start":
            o, ooff, i, ioff = pos[0], pos[1], pos[2], pos[3]
            outs.append(o)
            ins.append(i)
            if ooff is not None:
                ins.append(ooff.ap)
            if ioff is not None:
                ins.append(ioff.ap)
            return ins, outs
        if meth == "max":
            outs.append(kw.get("out", pos[0] if pos else None))
            iv = kw.get("in_", pos[1] if len(pos) > 1 else None)
            if iv is not None:
                ins.append(iv)
            return ins, outs
        if meth == "dma_start":
            o = kw.get("out", pos[0] if pos else None)
            i = kw.get("in_", pos[1] if len(pos) > 1 else None)
            outs.append(o)
            if i is not None:
                ins.append(i)
            return ins, outs
        first = True
        for a in pos:
            if isinstance(a, bass.AP):
                if first:
                    outs.append(a)
                    first = False
                else:
                    ins.append(a)
            elif isinstance(a, IndirectOffsetOnAxis):
                ins.append(a.ap)
        for k, v in kw.items():
            if isinstance(v, bass.AP):
                if k in self.OUT_KW:
                    outs.append(v)
                else:
                    ins.append(v)
        if meth == "matmul" and not kw.get("start", True):
            ins.append(outs[0])
        return ins, outs

    def _emit(self, engine, meth, args, kw, is_dma, grp=None):
        args = tuple(args)
        kw = dict(kw)
        ins, outs = self._classify(meth, args, kw)
        special = meth in self.SPECIAL

        def fn(e, meth=meth, args=args, kw=kw):
            return getattr(e, meth)(*args, **kw)
        self._rec(engine, fn, ins, outs, is_dma, grp=grp, special=special)

    def op(self, engine, fn, dma=False):
        raise RuntimeError("raw op() unsupported in dep scheduler")

    def dma(self, engine, out, in_, grp=None, **kw):
        self._emit(engine, "dma_start", (), {"out": out, "in_": in_, **kw},
                   True, grp=grp)

    def v(self, meth, *args, **kw):
        self._emit("vector", meth, args, kw, False)

    def s(self, meth, *args, **kw):
        self._emit("scalar", meth, args, kw, False)

    def g(self, meth, *args, **kw):
        grp = kw.pop("grp", None)
        self._emit("gpsimd", meth, args, kw, "dma" in meth, grp=grp)

    def t(self, meth, *args, **kw):
        self._emit("tensor", meth, args, kw, False)

    def step(self):
        pass

    _opinfo = None

    def finalize(self):
        assert not self.finalized
        self.finalized = True
        nc = self.nc
        referenced = [False] * len(self.ops)
        valmap = {}
        for i, (engine, fn, semkey, inc, waits) in enumerate(self.ops):
            valmap[(semkey, self._opinfo[i][3])] = i
        for (engine, fn, semkey, inc, waits) in self.ops:
            for k, vv in waits.items():
                j = valmap.get((k, vv))
                if j is not None:
                    referenced[j] = True
        newcum = {k: 0 for k in self.SEMS}
        remap = {}
        op_inc = [0] * len(self.ops)
        for i, (engine, fn, semkey, inc, waits) in enumerate(self.ops):
            is_dma = self._opinfo[i][4]
            if referenced[i] or is_dma:
                newcum[semkey] += inc
                op_inc[i] = inc
            remap[(semkey, self._opinfo[i][3])] = newcum[semkey]
        sems = {k: self.ctx.enter_context(nc.semaphore("SEM_" + k))
                for k in self.SEMS}
        streams = {e: [] for e in self.ENGINES}
        for i, (engine, fn, semkey, inc, waits) in enumerate(self.ops):
            w2 = {}
            for k, vv in waits.items():
                nv = remap[(k, vv)]
                if nv > 0 and w2.get(k, -1) < nv:
                    w2[k] = nv
            streams[engine].append((w2, fn, semkey, op_inc[i]))
        self.final_cum = newcum
        with nc.Block() as block:
            for ename in self.ENGINES:
                stream = streams[ename]
                if not stream and ename != "sync":
                    continue

                def make(stream=stream, ename=ename):
                    def body(eng):
                        last = {k: 0 for k in self.SEMS}
                        for waits, fn, semkey, inc in stream:
                            for k, vv in waits.items():
                                if vv > last[k]:
                                    eng.wait_ge(sems[k], vv)
                                    last[k] = vv
                            ins = fn(eng)
                            if inc:
                                ins.then_inc(sems[semkey], inc)
                        if ename == "sync":
                            for k in self.SEMS:
                                if self.final_cum[k] > last[k]:
                                    eng.wait_ge(sems[k], self.final_cum[k])
                    return body
                getattr(block, ename)(make())
        return nc

    def run(self, in_maps, trace=False):
        nc = self.finalize()
        res = run_bass_kernel_spmd(nc, in_maps,
                                   core_ids=list(range(len(in_maps))),
                                   trace=trace)
        self.ctx.close()
        return res.results


Prog._opinfo = None


def _progpatch_init(orig_init):
    def init(self, *a, **k):
        orig_init(self, *a, **k)
        self._opinfo = []
        self._fence = {}
        self._hrot = 0
        self._grot = 0
    return init


Prog.__init__ = _progpatch_init(Prog.__init__)



def ident_bf16(p):
    return p.const("ident128", np.eye(128, dtype=bf16))


def load_weight_bf16_T(p, src_ap, rows, cols, name):
    stage = p.sb("wprep_stage", [128, max(cols, rows)], F32)
    stageb = p.sb("wprep_stageb", [128, max(cols, rows)], BF16)
    out_dram = p.dram(name + "_T", [cols, rows], BF16)
    tmp_dram = p.dram(name + "_bf", [rows, cols], BF16)
    for r in range(rows // 128):
        p.dma("sync", stage[:, 0:cols], src_ap[r * 128:(r + 1) * 128, :])
        p.step()
        p.v("tensor_copy", stageb[:, 0:cols], stage[:, 0:cols])
        p.step()
        p.dma("sync", tmp_dram.ap()[r * 128:(r + 1) * 128, :], stageb[:, 0:cols])
        p.step()
    for c in range(cols // 128):
        p.dma("sync", stageb[:, 0:rows],
              tmp_dram.ap()[:, c * 128:(c + 1) * 128], transpose=True)
        p.step()
        p.dma("sync", out_dram.ap()[c * 128:(c + 1) * 128, :], stageb[:, 0:rows])
        p.step()
    return out_dram


def load_weight_bf16(p, src_ap, rows, cols, name):
    out_dram = p.dram(name + "_bf", [rows, cols], BF16)
    stage = p.sb("wcast_stage", [128, cols], F32)
    stageb = p.sb("wcast_stageb", [128, cols], BF16)
    assert rows % 128 == 0
    for r in range(rows // 128):
        p.dma("sync", stage[:], src_ap[r * 128:(r + 1) * 128, :])
        p.step()
        p.v("tensor_copy", stageb[:], stage[:])
        p.step()
        p.dma("sync", out_dram.ap()[r * 128:(r + 1) * 128, :], stageb[:])
        p.step()
    return out_dram


def layernorm_tile(p, out_bf, x_sb, scratch, eps=1e-5):
    m = scratch["m"]; sq = scratch["sq"]; var = scratch["var"]
    rstd = scratch["rstd"]; t1 = scratch["t1"]
    p.v("reduce_sum", m[:], x_sb[:], AX.X)
    p.step()
    p.v("tensor_scalar_mul", m[:], m[:], 1.0 / D)
    p.step()
    p.s("activation", t1[:], x_sb[:], AF.Square, accum_out=sq[:])
    p.step()
    p.v("tensor_scalar_mul", sq[:], sq[:], 1.0 / D)
    p.step()
    p.s("activation", var[:], m[:], AF.Square)
    p.step()
    p.v("tensor_sub", var[:], sq[:], var[:])
    p.step()
    p.s("activation", rstd[:], var[:], AF.Ln, bias=p.fconst(eps)[:])
    p.step()
    p.s("activation", rstd[:], rstd[:], AF.Exp, scale=p.fconst(-0.5)[:])
    p.step()
    p.v("tensor_scalar", out_bf[:], x_sb[:], m[:], rstd[:], ALU.subtract, ALU.mult)
    p.step()


def ln_scratch(p):
    return {
        "m": p.sb("ln_m", [128, 1], F32),
        "sq": p.sb("ln_sq", [128, 1], F32),
        "var": p.sb("ln_var", [128, 1], F32),
        "rstd": p.sb("ln_rstd", [128, 1], F32),
        "t1": p.sb("ln_t1", [128, D], F32),
    }


def transpose_via_dram(p, src_sb_tiles, n_tiles, cols, dst_sb, dram_scratch):
    rows = n_tiles * 128
    for r in range(n_tiles):
        p.dma("sync", dram_scratch.ap()[r * 128:(r + 1) * 128, :], src_sb_tiles[r])
        p.step()
    for c in range(cols // 128):
        p.dma("sync", dst_sb[c][:, 0:rows],
              dram_scratch.ap()[:, c * 128:(c + 1) * 128], transpose=True)
        p.step()


MAXN = 512


def emit_matmul(p, out_ps, lhsT_ap, rhs_ap, start, stop):
    p.t("matmul", out_ps, lhsT_ap, rhs_ap, start=start, stop=stop)





def dma_wload(p, dst_sb_ap, src2d_ap, R, C, queue="sync"):
    a = R // 128
    p.dma(queue,
          dst_sb_ap.rearrange("p (a c) -> p a c", a=a),
          src2d_ap.rearrange("(a p) c -> p a c", p=128))



def build_l1_router(p, xlnT_hi, xlnT_lo, router_ap, w_full):
    rstg = p.sb("r_stg", [128, 8], F32)
    rhi = p.sb("r_hi", [128, 8 * 8], BF16)
    rlo = p.sb("r_lo", [128, 8 * 8], BF16)
    rtmp = p.sb("r_tmp", [128, 8], F32)
    for k in range(8):
        ks = slice(k * 8, (k + 1) * 8)
        p.dma("sync", rstg[:], router_ap[k * 128:(k + 1) * 128, :])
        p.step()
        p.v("tensor_copy", rhi[:, ks], rstg[:])
        p.step()
        p.v("tensor_copy", rtmp[:], rhi[:, ks])
        p.step()
        p.v("tensor_sub", rtmp[:], rstg[:], rtmp[:])
        p.step()
        p.v("tensor_copy", rlo[:, ks], rtmp[:])
        p.step()
    lg_ps = p.ps("lg_ps", [128, 8], F32)
    lg = p.sb("lg", [128, 8], F32)
    ex = p.sb("ex", [128, 8], F32)
    m8 = p.sb("m8t", [128, 8], F32)
    Hgt = p.sb("HgtT", [128, 8], F32)
    Tt = p.sb("TtT", [128, 8], F32)
    scn = p.sb("scnT", [128, 8], F32)
    cnt = p.sb("cntc", [128, 1], F32)
    mx = p.sb("mxc", [128, 1], F32)
    Z = p.sb("Zc", [128, 1], F32)
    p1 = p.sb("p1c", [128, 1], F32)
    p2 = p.sb("p2c", [128, 1], F32)
    nw2 = p.sb("nw2c", [128, 1], F32)
    nw1 = p.sb("nw1c", [128, 1], F32)
    uu = p.sb("uuT", [128, 8], F32)
    oh = p.sb("ohT", [128, 8], F32)
    for t in range(NT):
        ts = slice(t * 128, (t + 1) * 128)
        ws = slice(t * 8, (t + 1) * 8)
        for k in range(8):
            ks = slice(k * 8, (k + 1) * 8)
            p.t("matmul", lg_ps[:], xlnT_hi[:, k * T + t * 128: k * T + t * 128 + 128],
                rhi[:, ks], start=(k == 0), stop=False)
            p.t("matmul", lg_ps[:], xlnT_hi[:, k * T + t * 128: k * T + t * 128 + 128],
                rlo[:, ks], start=False, stop=False)
            p.t("matmul", lg_ps[:], xlnT_lo[:, k * T + t * 128: k * T + t * 128 + 128],
                rhi[:, ks], start=False, stop=(k == 7))
        p.step()
        p.v("tensor_copy", lg[:], lg_ps[:])
        p.step()
        p.v("reduce_max", mx[:], lg[:], AX.X)
        p.step()
        p.v("tensor_scalar_mul", mx[:], mx[:], -1.0)
        p.step()
        p.s("activation", ex[:], lg[:], AF.Exp, bias=mx[:], accum_out=Z[:])
        p.step()
        p.v("reciprocal", Z[:], Z[:])
        p.step()
        p.v("tensor_scalar_mul", ex[:], ex[:], Z[:])
        p.step()
        p.v("max", out=m8[:], in_=lg[:])
        p.step()
        p.step()
        p.v("tensor_scalar", Hgt[:], lg[:], m8[:, 1:2], None, ALU.is_gt)
        p.step()
        p.v("tensor_scalar", Tt[:], lg[:], m8[:, 1:2], None, ALU.is_equal)
        p.step()
        p.v("tensor_reduce", cnt[:], Hgt[:], AX.X, ALU.add)
        p.step()
        p.v("tensor_tensor_scan", scn[:], Tt[:], Tt[:], 0.0, ALU.add, ALU.bypass)
        p.step()
        p.v("scalar_tensor_tensor", uu[:], scn[:], 1.0, Tt[:], ALU.mult, ALU.mult)
        p.step()
        p.v("tensor_scalar", uu[:], uu[:], cnt[:], None, ALU.add)
        p.step()
        p.v("tensor_mul", uu[:], uu[:], Tt[:])
        p.step()
        p.v("tensor_scalar", oh[:], uu[:], 0.5, 2.5, ALU.is_ge, ALU.bypass)
        p.step()
        p.v("tensor_scalar", scn[:], uu[:], 2.5, None, ALU.is_le)
        p.step()
        p.v("tensor_mul", oh[:], oh[:], scn[:])
        p.step()
        p.v("tensor_add", oh[:], oh[:], Hgt[:])
        p.step()
        p.v("tensor_mul", scn[:], ex[:], oh[:])
        p.step()
        p.v("reduce_max", p1[:], scn[:], AX.X)
        p.step()
        p.v("tensor_reduce", p2[:], scn[:], AX.X, ALU.add)
        p.step()
        p.v("tensor_sub", p2[:], p2[:], p1[:])
        p.step()
        p.v("tensor_sub", nw2[:], p2[:], p1[:])
        p.step()
        p.s("activation", nw2[:], nw2[:], AF.Sigmoid)
        p.step()
        p.v("tensor_scalar", nw1[:], nw2[:], -1.0, 1.0, ALU.mult, ALU.add)
        p.step()
        p.v("tensor_scalar", oh[:], uu[:], 1.0, None, ALU.is_equal)
        p.step()
        p.v("tensor_add", oh[:], oh[:], Hgt[:])
        p.step()
        p.v("tensor_scalar", scn[:], uu[:], 2.0, None, ALU.is_equal)
        p.step()
        p.v("tensor_scalar_mul", oh[:], oh[:], nw1[:])
        p.step()
        p.v("tensor_scalar_mul", scn[:], scn[:], nw2[:])
        p.step()
        p.v("tensor_add", w_full[:, ws], oh[:], scn[:])
        p.step()


def build_l1_ffn_experts(p, xlnT_hi, w_full, gate_bf, up_bf, down_bf,
                         x_in_dram, x_out_dram, cast_srcs=None):
    TH = 1024
    wg2 = [p.sb("wg", [128, 8 * 2048], BF16) for _ in range(2)]
    gs = p.sb("gs", [128, 16 * TH], BF16)
    wd = p.sb("wd", [128, 16 * 1024], BF16)
    wcst = p.sb("wcst", [128, 2048], F32) if cast_srcs else None

    def wload(dst_sb, bf_dram, f32_dram, e, R, C, first):
        if not (cast_srcs and first):
            dma_wload(p, dst_sb.ap(), bf_dram.ap()[e], R, C)
            return
        per = (2048 // C) * 128
        ar = per // 128
        for hh in range(R // per):
            rs = slice(hh * per, (hh + 1) * per)
            cs = slice(hh * ar * C, (hh + 1) * ar * C)
            dma_wload(p, wcst.ap()[:, 0:ar * C], f32_dram.ap()[e][rs, :],
                      per, C)
            p.s("copy", dst_sb[:, cs], wcst[:, 0:ar * C])
            p.dma("sync",
                  bf_dram.ap()[e][rs, :].rearrange("(a p) c -> p a c", p=128),
                  dst_sb[:, cs].rearrange("p (a c) -> p a c", a=ar))
    acc = p.sb("accT", [128, 8 * 1024], F32)
    xres = p.sb("xres", [128, 1024], F32)
    ps = [p.ps("pmm", [128, 512], F32) for _ in range(2)]
    pbank = [0]

    def nextps():
        pbank[0] ^= 1
        return ps[pbank[0]]

    for h in range(2):
        for tt in range(8):
            p.v("memset", acc[:, tt * 1024:(tt + 1) * 1024], 0.0)
        p.step()
        for e in range(8):
            for which, wsrc, fsrc in (("g", gate_bf, 0), ("u", up_bf, 1)):
                wg = wg2[0 if which == "g" else 1]
                wload(wg, wsrc, cast_srcs[fsrc] if cast_srcs else None,
                      e, 1024, 2048, h == 0)
                for ft in range(16):
                    for nh in range(2):
                        bank = nextps()
                        for k in range(8):
                            p.t("matmul", bank[:],
                                wg[:, k * 2048 + ft * 128: k * 2048 + ft * 128 + 128],
                                xlnT_hi[:, k * T + h * TH + nh * 512:
                                        k * T + h * TH + nh * 512 + 512],
                                start=(k == 0), stop=(k == 7))
                        p.step()
                        dst = gs[:, ft * TH + nh * 512: ft * TH + nh * 512 + 512]
                        if which == "g":
                            p.s("activation", dst, bank[:], AF.Silu)
                        else:
                            p.v("tensor_mul", dst, bank[:], dst)
                        p.step()
            wload(wd, down_bf, cast_srcs[2] if cast_srcs else None,
                  e, 2048, 1024, h == 0)
            for tt in range(8):
                wcol = w_full[:, (h * 8 + tt) * 8 + e:(h * 8 + tt) * 8 + e + 1]
                for nh2 in range(2):
                    bank = nextps()
                    for k in range(16):
                        p.t("matmul", bank[:],
                            gs[:, k * TH + tt * 128: k * TH + tt * 128 + 128],
                            wd[:, k * 1024 + nh2 * 512: k * 1024 + nh2 * 512 + 512],
                            start=(k == 0), stop=(k == 15))
                    p.step()
                    a = acc[:, tt * 1024 + nh2 * 512: tt * 1024 + nh2 * 512 + 512]
                    p.v("scalar_tensor_tensor", a, bank[:], wcol, a,
                        ALU.mult, ALU.add)
                    p.step()
        for tt in range(8):
            row = (h * 8 + tt) * 128
            p.dma("sync", xres[:], x_in_dram.ap()[row:row + 128, :])
            p.step()
            p.v("scalar_tensor_tensor", xres[:],
                acc[:, tt * 1024:(tt + 1) * 1024], 0.5, xres[:],
                ALU.mult, ALU.add)
            p.step()
            p.dma("sync", x_out_dram.ap()[row:row + 128, :], xres[:])
            p.step()


def build_l1_ffn_call(p, x_in_dram, x_out_dram, router_ap, gate_bf, up_bf,
                      down_bf, tag="l1f", cast_srcs=None):
    xlnT_hi = p.sb(tag + "_Thi", [128, 8 * T], BF16)
    w_full = p.sb(tag + "_wfull", [128, NT * 8], F32)
    with p.scope():
        xlnT_lo = p.sb(tag + "_Tlo", [128, 8 * T], BF16)
        with p.scope():
            build_ln_and_transpose_into(p, x_in_dram, xlnT_hi, xlnT_lo, tag)
        build_l1_router(p, xlnT_hi, xlnT_lo, router_ap, w_full)
    build_l1_ffn_experts(p, xlnT_hi, w_full, gate_bf, up_bf, down_bf,
                         x_in_dram, x_out_dram, cast_srcs=cast_srcs)


def build_ln_and_transpose_into(p, x_src_dram, xlnT_hi, xlnT_lo, tag="ln",
                                tok_out_dram=None):
    xt = p.sb(tag + "_xt", [128, D], F32)
    lnf = p.sb(tag + "_f", [128, D], F32)
    hib = p.sb(tag + "_hib", [128, D], BF16)
    lob = p.sb(tag + "_lob", [128, D], BF16)
    sc = ln_scratch(p)
    tmp = sc["t1"]
    dsc = p.dram(tag + "_dsc", [T, D], BF16)
    dsclo = p.dram(tag + "_dsclo", [T, D], BF16) if xlnT_lo is not None else None
    for t in range(NT):
        p.dma("sync", xt[:], x_src_dram.ap()[t * 128:(t + 1) * 128, :])
        p.step()
        p.v("reduce_sum", sc["m"][:], xt[:], AX.X)
        p.step()
        p.v("tensor_scalar_mul", sc["m"][:], sc["m"][:], 1.0 / D)
        p.step()
        p.s("activation", tmp[:], xt[:], AF.Square, accum_out=sc["sq"][:])
        p.step()
        p.v("tensor_scalar_mul", sc["sq"][:], sc["sq"][:], 1.0 / D)
        p.step()
        p.s("activation", sc["var"][:], sc["m"][:], AF.Square)
        p.step()
        p.v("tensor_sub", sc["var"][:], sc["sq"][:], sc["var"][:])
        p.step()
        p.s("activation", sc["rstd"][:], sc["var"][:], AF.Ln, bias=p.fconst(1e-5)[:])
        p.step()
        p.s("activation", sc["rstd"][:], sc["rstd"][:], AF.Exp,
            scale=p.fconst(-0.5)[:])
        p.step()
        p.v("tensor_scalar", lnf[:], xt[:], sc["m"][:], sc["rstd"][:],
            ALU.subtract, ALU.mult)
        p.step()
        p.v("tensor_copy", hib[:], lnf[:])
        p.step()
        p.dma("sync", dsc.ap()[t * 128:(t + 1) * 128, :], hib[:])
        p.step()
        if tok_out_dram is not None:
            p.dma("sync", tok_out_dram.ap()[t * 128:(t + 1) * 128, :], hib[:])
            p.step()
        if xlnT_lo is not None:
            p.v("tensor_copy", tmp[:], hib[:])
            p.step()
            p.v("tensor_sub", tmp[:], lnf[:], tmp[:])
            p.step()
            p.v("tensor_copy", lob[:], tmp[:])
            p.step()
            p.dma("sync", dsclo.ap()[t * 128:(t + 1) * 128, :], lob[:])
            p.step()
    for src_d, dstT in ((dsc, xlnT_hi), (dsclo, xlnT_lo)):
        if dstT is None:
            continue
        for d8 in range(8):
            p.dma("sync", dstT[:, d8 * T:(d8 + 1) * T],
                  src_d.ap()[:, d8 * 128:(d8 + 1) * 128], transpose=True)
            p.step()



FD = 16
NH = 16
HD = 64
DF = 153
DFP = 160
RRD = 2.0
CHUNK = 128
NCH = T // CHUNK


def build_l1_attn_call(p, x_in_dram, x_out_dram, wqT_ap, wkT_ap, wvT_ap,
                       woT_ap, tag="l1a"):
    ident = p.sb(tag + "_ident", [128, 128], BF16)
    p.dma("sync", ident[:], p.const("ident128", np.eye(128, dtype=bf16)).ap()[:])
    p.step()
    mk = np.zeros((128, 128), np.float32)
    for i in range(128):
        mk[i, i:] = 1.0
    mask = p.sb(tag + "_mask", [128, 128], F32)
    p.dma("sync", mask[:], p.const("maskcausal", mk).ap()[:])
    p.step()
    onesb = p.sb(tag + "_ones", [128, 1], BF16)
    p.v("memset", onesb[:], 1.0)
    p.step()
    ones_row = p.sb(tag + "_onesr", [1, 64], BF16)
    p.v("memset", ones_row[:], 1.0)
    p.step()

    v_tok = p.sb(tag + "_vtok", [128, NT * 1024], BF16)
    q_tok = p.sb(tag + "_qtok", [128, NT * 512], BF16)
    k_tok = p.sb(tag + "_ktok", [128, NT * 512], BF16)
    y_all = p.sb(tag + "_yall", [128, 8 * T], BF16)

    ps0 = p.ps(tag + "_ps0", [128, 512], F32)
    ps1 = p.ps(tag + "_ps1", [128, 512], F32)
    ps2 = p.ps(tag + "_ps2", [128, 512], F32)
    ps3 = p.ps(tag + "_ps3", [128, 512], F32)
    psT = p.ps(tag + "_psT", [128, 256], BF16)

    with p.scope():
        xlnT = p.sb(tag + "_Thi", [128, 8 * T], BF16)
        with p.scope():
            build_ln_and_transpose_into(p, x_in_dram, xlnT, None, tag + "ln")
        wqk = p.sb(tag + "_wqk", [128, 8 * 512], BF16)
        wv = p.sb(tag + "_wv", [128, 8 * 1024], BF16)
        wstg = p.sb(tag + "_wstg", [128, 8 * 1024], F32)
        for wap, tdst in ((wqT_ap, q_tok), (wkT_ap, k_tok)):
            dma_wload(p, wstg.ap()[:, 0:8 * 512], wap, 1024, 512)
            p.v("tensor_copy", wqk[:], wstg[:, 0:8 * 512])
            for tt in range(NT):
                bank = ps0 if tt % 2 == 0 else ps1
                for k in range(8):
                    p.t("matmul", bank[:],
                        xlnT[:, k * T + tt * 128: k * T + tt * 128 + 128],
                        wqk[:, k * 512:(k + 1) * 512],
                        start=(k == 0), stop=(k == 7))
                p.step()
                p.v("tensor_copy", tdst[:, tt * 512:(tt + 1) * 512], bank[:])
                p.step()
        dma_wload(p, wstg.ap(), wvT_ap, 1024, 1024)
        p.v("tensor_copy", wv[:], wstg[:])
        for tt in range(NT):
            for n in range(2):
                bank = ps0 if n == 0 else ps1
                for k in range(8):
                    p.t("matmul", bank[:],
                        xlnT[:, k * T + tt * 128: k * T + tt * 128 + 128],
                        wv[:, k * 1024 + n * 512: k * 1024 + n * 512 + 512],
                        start=(k == 0), stop=(k == 7))
                p.step()
                p.v("tensor_copy",
                    v_tok[:, tt * 1024 + n * 512: tt * 1024 + n * 512 + 512],
                    bank[:])
                p.step()

    qf = p.sb(tag + "_qf", [128, NH * DFP], BF16)
    kf = p.sb(tag + "_kf", [128, NH * DFP], BF16)
    qfDa = p.sb(tag + "_qfDa", [128, 128], BF16)
    qfDb = p.sb(tag + "_qfDb", [32, 128], BF16)
    kfDa = p.sb(tag + "_kfDa", [128, 128], BF16)
    kfDb = p.sb(tag + "_kfDb", [32, 128], BF16)
    st_a = p.sb(tag + "_sta", [128, NH * 65], F32)
    st_b = p.sb(tag + "_stb", [32, NH * 65], F32)
    stba = p.sb(tag + "_stba", [128, 65], BF16)
    stbb = p.sb(tag + "_stbb", [32, 65], BF16)
    S_m = p.sb(tag + "_Sm", [128, 128], BF16)
    den = p.sb(tag + "_den", [1, 128], F32)
    denb = p.sb(tag + "_denb", [1, 128], BF16)
    rb = p.sb(tag + "_rb", [64, 128], F32)

    s2 = float(1.0 / np.sqrt(4.0 * np.sqrt(2.0)))
    p.v("memset", qf[:], 0.0)
    p.v("memset", kf[:], 0.0)
    p.v("memset", st_a[:], 0.0)
    p.v("memset", st_b[:], 0.0)
    p.step()
    for fbuf in (qf, kf):
        f3 = fbuf.ap().rearrange("p (h c) -> p h c", h=NH)
        p.v("memset", f3[:, :, 0:1], 1.0)
    p.step()

    def band_off(j):
        return 33 + 15 * j - (j * (j - 1)) // 2

    for c in range(NCH):
        for fbuf, tok in ((qf, q_tok), (kf, k_tok)):
            f3 = fbuf.ap().rearrange("p (h c) -> p h c", h=NH)
            t3 = tok.ap()[:, c * 512:(c + 1) * 512].rearrange(
                "p (h c) -> p h c", h=NH)[:, :, 0:16]
            p.v("tensor_scalar_mul", f3[:, :, 1:17], t3, 1.0 / RRD)
            p.step()
            p.s("activation", f3[:, :, 17:33], t3, AF.Square,
                scale=p.fconst(s2)[:])
            p.step()
            for j in range(15):
                w = 15 - j
                o = band_off(j)
                p.v("tensor_mul", f3[:, :, o:o + w],
                    f3[:, :, 1 + j:2 + j].to_broadcast([128, NH, w]),
                    f3[:, :, 2 + j:2 + j + w])
            p.step()
        for h in range(NH):
            B = h * DFP
            vsl = v_tok[:, c * 1024 + h * 64: c * 1024 + h * 64 + 64]
            sta = st_a[:, h * 65:(h + 1) * 65]
            stb = st_b[:, h * 65:(h + 1) * 65]
            ysl = y_all.ap()[64 * (h % 2):64 * (h % 2) + 64,
                             (h // 2) * T + c * 128:(h // 2) * T + c * 128 + 128]
            p.t("transpose", psT[:, 0:128], qf[:, B:B + 128], ident[:])
            p.step()
            p.v("tensor_copy", qfDa[:], psT[:, 0:128])
            p.step()
            p.t("transpose", psT[0:25, 0:128], qf[:, B + 128:B + 153], ident[:])
            p.step()
            p.v("tensor_copy", qfDb[0:25, :], psT[0:25, 0:128])
            p.step()
            p.t("transpose", psT[:, 0:128], kf[:, B:B + 128], ident[:])
            p.step()
            p.v("tensor_copy", kfDa[:], psT[:, 0:128])
            p.step()
            p.t("transpose", psT[0:25, 0:128], kf[:, B + 128:B + 153], ident[:])
            p.step()
            p.v("tensor_copy", kfDb[0:25, :], psT[0:25, 0:128])
            p.step()
            p.v("tensor_copy", stba[:], sta)
            p.v("tensor_copy", stbb[0:25, :], stb[0:25, :])
            p.step()
            p.t("matmul", ps0[:, 0:128], kfDa[:], qfDa[:], start=True, stop=False)
            p.t("matmul", ps0[:, 0:128], kfDb[0:25, :], qfDb[0:25, :],
                start=False, stop=True)
            p.step()
            p.v("tensor_mul", S_m[:], ps0[:, 0:128], mask[:])
            p.step()
            p.t("matmul", ps1[0:64, 0:128], vsl, S_m[:], start=True, stop=False)
            p.t("matmul", ps1[0:64, 0:128], stba[:, 0:64], qfDa[:],
                start=False, stop=False)
            p.t("matmul", ps1[0:64, 0:128], stbb[0:25, 0:64], qfDb[0:25, :],
                start=False, stop=True)
            p.step()
            p.t("matmul", ps2[0:1, 0:128], onesb[:], S_m[:], start=True, stop=False)
            p.t("matmul", ps2[0:1, 0:128], stba[:, 64:65], qfDa[:],
                start=False, stop=False)
            p.t("matmul", ps2[0:1, 0:128], stbb[0:25, 64:65], qfDb[0:25, :],
                start=False, stop=True)
            p.step()
            p.v("tensor_scalar", den[:], ps2[0:1, 0:128], 1e-12, None, ALU.add)
            p.step()
            p.v("reciprocal", den[:], den[:])
            p.step()
            p.v("tensor_copy", denb[:], den[:])
            p.step()
            p.t("matmul", ps3[0:64, 0:128], ones_row[:], denb[:],
                start=True, stop=True)
            p.step()
            p.v("tensor_copy", rb[:], ps3[0:64, 0:128])
            p.step()
            p.v("tensor_mul", ysl, ps1[0:64, 0:128], rb[:])
            p.step()
            p.t("matmul", ps0[:, 0:64], kf[:, B:B + 128], vsl, start=True, stop=True)
            p.t("matmul", ps0[:, 64:65], kf[:, B:B + 128], onesb[:],
                start=True, stop=True)
            p.step()
            p.v("tensor_add", sta, sta, ps0[:, 0:65])
            p.step()
            p.t("matmul", ps1[0:25, 0:64], kf[:, B + 128:B + 153], vsl,
                start=True, stop=True)
            p.t("matmul", ps1[0:25, 64:65], kf[:, B + 128:B + 153], onesb[:],
                start=True, stop=True)
            p.step()
            p.v("tensor_add", stb[0:25, :], stb[0:25, :], ps1[0:25, 0:65])
            p.step()

    with p.scope():
        wo = p.sb(tag + "_wo", [128, 8 * 1024], BF16)
        wstg2 = p.sb(tag + "_wstg2", [128, 8 * 1024], F32)
        xres = p.sb(tag + "_xres", [128, 1024], F32)
        dma_wload(p, wstg2.ap(), woT_ap, 1024, 1024)
        p.v("tensor_copy", wo[:], wstg2[:])
        for tt in range(NT):
            p.dma("sync", xres[:], x_in_dram.ap()[tt * 128:(tt + 1) * 128, :])
            p.step()
            for n in range(2):
                bank = ps0 if n == 0 else ps1
                for k in range(8):
                    p.t("matmul", bank[:],
                        y_all[:, k * T + tt * 128: k * T + tt * 128 + 128],
                        wo[:, k * 1024 + n * 512: k * 1024 + n * 512 + 512],
                        start=(k == 0), stop=(k == 7))
                p.step()
                p.v("tensor_add", xres[:, n * 512:(n + 1) * 512],
                    xres[:, n * 512:(n + 1) * 512], bank[:])
                p.step()
            p.dma("sync", x_out_dram.ap()[tt * 128:(tt + 1) * 128, :], xres[:])
            p.step()



DI = 2048
NDT = DI // 128
DSTATE = 16


def build_mamba_dir(p, src_dram, w, y_dram, tag, reverse=False):
    yraw_dram = p.dram(tag + "_yraw", [DI, T], BF16)
    z_dram = p.dram(tag + "_z", [DI, T], BF16)
    xc = p.sb(tag + "_xc", [128, NDT * T], BF16)
    ps0 = p.ps(tag + "_ps0", [128, 512], F32)
    ps1 = p.ps(tag + "_ps1", [128, 512], F32)

    with p.scope():
        xlnT = p.sb(tag + "_xlnT", [128, 8 * T], BF16)
        for d8 in range(8):
            p.dma("sync", xlnT[:, d8 * T:(d8 + 1) * T],
                  src_dram.ap()[:, d8 * 128:(d8 + 1) * 128], transpose=True)
            p.step()
        if reverse:
            rtmp = p.sb(tag + "_rtmp", [128, T], BF16)
            for d8 in range(8):
                p.v("tensor_copy", rtmp[:], xlnT[:, d8 * T:(d8 + 1) * T][:, ::-1])
                p.step()
                p.v("tensor_copy", xlnT[:, d8 * T:(d8 + 1) * T], rtmp[:])
                p.step()
        w_in = p.sb(tag + "_win", [128, 8 * 4096], BF16)
        wstg = p.sb(tag + "_wstg", [128, 2 * 4096], F32)
        zstg = p.sb(tag + "_zstg", [128, 512], BF16)
        for hh in range(4):
            dma_wload(p, wstg.ap(), w["in_wT"][hh * 256:(hh + 1) * 256, :],
                      256, 4096)
            p.v("tensor_copy", w_in[:, hh * 2 * 4096:(hh + 1) * 2 * 4096],
                wstg[:])
        for et in range(32):
            for n in range(4):
                bank = ps0 if (et * 4 + n) % 2 == 0 else ps1
                for k in range(8):
                    p.t("matmul", bank[:],
                        w_in[:, k * 4096 + et * 128: k * 4096 + et * 128 + 128],
                        xlnT[:, k * T + n * 512: k * T + n * 512 + 512],
                        start=(k == 0), stop=(k == 7))
                p.step()
                if et < 16:
                    p.v("tensor_copy",
                        xc[:, et * T + n * 512: et * T + n * 512 + 512], bank[:])
                    p.step()
                else:
                    p.v("tensor_copy", zstg[:], bank[:])
                    p.step()
                    p.dma("sync",
                          z_dram.ap()[(et - 16) * 128:(et - 15) * 128,
                                      n * 512:(n + 1) * 512], zstg[:])
                    p.step()

    with p.scope():
        convw = p.sb(tag + "_convw", [128, 64], F32)
        convb = p.sb(tag + "_convb", [128, 16], F32)
        dtb = p.sb(tag + "_dtb", [128, 16], F32)
        Dl = p.sb(tag + "_Dl", [128, 16], F32)
        Al = p.sb(tag + "_Al", [128, 256], F32)
        p.dma("sync", convw[:], w["conv_w_l"])
        p.dma("sync", convb[:], w["conv_b_l"])
        p.dma("sync", dtb[:], w["dt_b_l"])
        p.dma("sync", Dl[:], w["D_l"])
        p.dma("sync", Al[:], w["A_log_l"])
        p.step()
        p.s("activation", Al[:], Al[:], AF.Exp)
        p.step()
        p.v("tensor_scalar_mul", Al[:], Al[:], -1.0)
        p.step()

        dtt = p.sb(tag + "_dtt", [128, T], F32)
        du_bf = p.sb(tag + "_dubf", [128, T], BF16)
        at2 = [p.sb(tag + "_at", [128, T], BF16) for _ in range(2)]
        bt_bf = p.sb(tag + "_btbf", [128, T], BF16)
        ht_bf = p.sb(tag + "_htbf", [128, T], BF16)
        t1b2 = [p.sb(tag + "_t1b", [128, T], BF16) for _ in range(2)]
        Bsb2 = [p.sb(tag + "_Bsb", [128, T], BF16) for _ in range(2)]
        Csb2 = [p.sb(tag + "_Csb", [128, T], BF16) for _ in range(2)]
        psB2 = [p.ps(tag + "_psB2", [128, 1024], F32) for _ in range(2)]
        psC1 = p.ps(tag + "_psC1", [128, 1024], F32)
        yac = p.sb(tag + "_yac", [128, T], F32)
        t1 = p.sb(tag + "_t1", [128, T], F32)
        ybf = p.sb(tag + "_ybf", [128, T], BF16)
        xdbl_bf = p.sb(tag + "_xdbl", [96, T], BF16)
        dtw = p.sb(tag + "_dtw", [64, 2048], BF16)
        ones_t = p.sb(tag + "_onest", [128, 128], BF16)
        p.v("memset", ones_t[:], 1.0)
        p.step()

        for dt_ in range(NDT):
            xi_t = xc[:, dt_ * T:(dt_ + 1) * T]
            p.v("tensor_scalar_mul", t1[:], xi_t, convw[:, dt_ * 4 + 3:dt_ * 4 + 4])
            p.step()
            for j in range(1, 4):
                p.v("scalar_tensor_tensor", t1[:, j:T], xi_t[:, 0:T - j],
                    convw[:, dt_ * 4 + 3 - j:dt_ * 4 + 4 - j], t1[:, j:T],
                    ALU.mult, ALU.add)
                p.step()
            p.s("activation", xi_t, t1[:], AF.Silu, bias=convb[:, dt_:dt_ + 1])
            p.step()

        with p.scope():
            xw = p.sb(tag + "_xw", [128, 16 * 96], BF16)
            wstg2 = p.sb(tag + "_wstg2", [128, 96], F32)
            for k in range(16):
                p.dma("sync", wstg2[:], w["x_wT"][k * 128:(k + 1) * 128, :])
                p.step()
                p.v("tensor_copy", xw[:, k * 96:(k + 1) * 96], wstg2[:])
                p.step()
            for n in range(4):
                bank = ps0 if n % 2 == 0 else ps1
                for k in range(16):
                    p.t("matmul", bank[0:96, :],
                        xw[:, k * 96:(k + 1) * 96],
                        xc[:, k * T + n * 512: k * T + n * 512 + 512],
                        start=(k == 0), stop=(k == 15))
                p.step()
                p.v("tensor_copy", t1[0:96, n * 512:(n + 1) * 512], bank[0:96, :])
                p.step()
            p.v("tensor_copy", xdbl_bf[:], t1[0:96, :])
            p.step()
            wstg3 = p.sb(tag + "_wstg3", [64, 2048], F32)
            p.dma("sync", wstg3[:], w["dt_wT"][:, :])
            p.step()
            p.v("tensor_copy", dtw[:], wstg3[:])
            p.step()

        bct = [p.sb(tag + f"_bc{i}", [128, T], BF16) for i in range(11)]

        def bc_row(s):
            return bct[s // 3][32 * (s % 3):32 * (s % 3) + 1, :]

        for s in range(32):
            p.dma("gpsimd", bc_row(s), xdbl_bf.ap()[64 + s:65 + s, :])
        p.step()

        for dt_ in range(NDT):
            for n in range(4):
                bank = ps0 if n % 2 == 0 else ps1
                p.t("matmul", bank[:], dtw[:, dt_ * 128:(dt_ + 1) * 128],
                    xdbl_bf.ap()[0:64, n * 512:(n + 1) * 512],
                    start=True, stop=True)
                p.step()
                p.v("tensor_scalar", dtt[:, n * 512:(n + 1) * 512], bank[:],
                    dtb[:, dt_:dt_ + 1], None, ALU.add)
                p.step()
            p.s("activation", t1[:], dtt[:], AF.Abs)
            p.step()
            p.s("activation", t1[:], t1[:], AF.Exp, scale=p.fconst(-1.0)[:])
            p.step()
            p.s("activation", t1[:], t1[:], AF.Ln, bias=p.fconst(1.0)[:])
            p.step()
            p.v("tensor_scalar_max", dtt[:], dtt[:], 0.0)
            p.step()
            p.v("tensor_add", dtt[:], dtt[:], t1[:])
            p.step()
            p.v("tensor_mul", du_bf[:], dtt[:], xc[:, dt_ * T:(dt_ + 1) * T])
            p.step()
            p.v("memset", yac[:], 0.0)
            p.step()
            for s in range(DSTATE):
                par = s % 2
                atp = at2[par]
                psBp = psB2[par]
                Bsb, Csb, t1b = Bsb2[par], Csb2[par], t1b2[par]
                p.s("activation", atp[:], dtt[:], AF.Exp,
                    scale=Al[:, dt_ * 16 + s:dt_ * 16 + s + 1])
                bb = 32 * (s % 3)
                cb = 32 * ((16 + s) % 3)
                for half in range(2):
                    for n in range(2):
                        c0 = half * 1024 + n * 512
                        p.t("matmul", psBp[:, n * 512:(n + 1) * 512],
                            ones_t[bb:bb + 1, :],
                            bc_row(s)[:, c0:c0 + 512], start=True, stop=True)
                    p.s("copy", Bsb[:, half * 1024:(half + 1) * 1024], psBp[:])
                p.v("tensor_mul", bt_bf[:], du_bf[:], Bsb[:])
                p.v("tensor_tensor_scan", ht_bf[:], atp[:], bt_bf[:], 0.0,
                    ALU.mult, ALU.add)
                for half in range(2):
                    for n in range(2):
                        c0 = half * 1024 + n * 512
                        p.t("matmul", psC1[:, n * 512:(n + 1) * 512],
                            ones_t[cb:cb + 1, :],
                            bc_row(16 + s)[:, c0:c0 + 512], start=True, stop=True)
                    p.s("copy", Csb[:, half * 1024:(half + 1) * 1024], psC1[:])
                p.v("tensor_mul", t1b[:], ht_bf[:], Csb[:])
                if s % 2 == 0:
                    p.g("tensor_add", yac[:], yac[:], t1b[:])
                else:
                    p.v("tensor_add", yac[:], yac[:], t1b[:])
            p.v("tensor_copy", ybf[:], yac[:])
            p.step()
            p.dma("sync", yraw_dram.ap()[dt_ * 128:(dt_ + 1) * 128, :], ybf[:])
            p.step()

        for dt_ in range(NDT):
            p.dma("sync", ybf[:], z_dram.ap()[dt_ * 128:(dt_ + 1) * 128, :])
            p.step()
            p.s("activation", t1[:], ybf[:], AF.Silu)
            p.step()
            p.dma("sync", ybf[:], yraw_dram.ap()[dt_ * 128:(dt_ + 1) * 128, :])
            p.step()
            p.v("scalar_tensor_tensor", yac[:], xc[:, dt_ * T:(dt_ + 1) * T],
                Dl[:, dt_:dt_ + 1], ybf[:], ALU.mult, ALU.add)
            p.step()
            p.v("tensor_mul", yac[:], yac[:], t1[:])
            p.step()
            p.v("tensor_copy", ybf[:], yac[:])
            p.step()
            p.dma("sync", yraw_dram.ap()[dt_ * 128:(dt_ + 1) * 128, :], ybf[:])
            p.step()

    with p.scope():
        wo = p.sb(tag + "_wo", [128, 16 * 1024], BF16)
        wo_f = p.sb(tag + "_wo_f", [128, 16 * 1024], F32)
        ycols = p.sb(tag + "_ycols", [128, 16 * 128], BF16)
        outf = p.sb(tag + "_outf", [128, 1024], F32)
        dma_wload(p, wo_f.ap(), w["out_wT"], 2048, 1024)
        p.v("tensor_copy", wo[:], wo_f[:])
        ycols2 = p.sb(tag + "_ycols2", [128, 16 * 128], BF16)
        for tt in range(NT):
            srcv = yraw_dram.ap().rearrange("(a p) t -> p a t", p=128)
            c0 = (T - (tt + 1) * 128) if reverse else tt * 128
            p.dma("sync",
                  ycols.ap().rearrange("p (a t) -> p a t", a=16),
                  srcv[:, :, c0:c0 + 128])
            p.step()
            if reverse:
                for a in range(16):
                    p.v("tensor_copy", ycols2[:, a * 128:(a + 1) * 128],
                        ycols[:, a * 128:(a + 1) * 128][:, ::-1])
                p.step()
                ycols, ycols2 = ycols2, ycols
            for n in range(2):
                bank = ps0 if n == 0 else ps1
                for k in range(16):
                    p.t("matmul", bank[:],
                        ycols[:, k * 128:(k + 1) * 128],
                        wo[:, k * 1024 + n * 512: k * 1024 + n * 512 + 512],
                        start=(k == 0), stop=(k == 15))
                p.step()
                p.v("tensor_copy", outf[:, n * 512:(n + 1) * 512], bank[:])
                p.step()
            p.dma("sync", y_dram.ap()[tt * 128:(tt + 1) * 128, :], outf[:])
            p.step()


def bcast_row_f32(p, row_ap, dst_sb, ps, ones1f, cols=1024):
    stg = p.sb("bcr_stg", [1, cols], F32)
    p.dma("sync", stg[:], row_ap)
    p.step()
    for n in range((cols + 511) // 512):
        c0, c1 = n * 512, min((n + 1) * 512, cols)
        p.t("matmul", ps[:, 0:c1 - c0], ones1f[:], stg[:, c0:c1],
            start=True, stop=True)
        p.step()
        p.v("tensor_copy", dst_sb[:, c0:c1], ps[:, 0:c1 - c0])
        p.step()


def build_l0_attn_call(p, x_in_dram, x_out_dram, wf, wb, ff, tag="l0a"):
    xln_dram = p.dram(tag + "_xln", [T, D], F32)
    yf_dram = p.dram(tag + "_yf", [T, D], F32)
    yb_dram = p.dram(tag + "_yb", [T, D], F32)
    xblk_dram = p.dram(tag + "_xblk", [T, D], F32)

    dsc = p.dram(tag + "_dsc", [T, D], BF16)
    with p.scope():
        xt = p.sb(tag + "_xt", [128, D], F32)
        lnf = p.sb(tag + "_lnf", [128, D], F32)
        hib = p.sb(tag + "_hib", [128, D], BF16)
        sc = ln_scratch(p)
        for t in range(NT):
            p.dma("sync", xt[:], x_in_dram.ap()[t * 128:(t + 1) * 128, :])
            p.step()
            p.v("reduce_sum", sc["m"][:], xt[:], AX.X)
            p.step()
            p.v("tensor_scalar_mul", sc["m"][:], sc["m"][:], 1.0 / D)
            p.step()
            p.s("activation", sc["t1"][:], xt[:], AF.Square, accum_out=sc["sq"][:])
            p.step()
            p.v("tensor_scalar_mul", sc["sq"][:], sc["sq"][:], 1.0 / D)
            p.step()
            p.s("activation", sc["var"][:], sc["m"][:], AF.Square)
            p.step()
            p.v("tensor_sub", sc["var"][:], sc["sq"][:], sc["var"][:])
            p.step()
            p.s("activation", sc["rstd"][:], sc["var"][:], AF.Ln,
                bias=p.fconst(1e-5)[:])
            p.step()
            p.s("activation", sc["rstd"][:], sc["rstd"][:], AF.Exp,
                scale=p.fconst(-0.5)[:])
            p.step()
            p.v("tensor_scalar", lnf[:], xt[:], sc["m"][:], sc["rstd"][:],
                ALU.subtract, ALU.mult)
            p.step()
            p.dma("sync", xln_dram.ap()[t * 128:(t + 1) * 128, :], lnf[:])
            p.step()
            p.v("tensor_copy", hib[:], lnf[:])
            p.step()
            p.dma("sync", dsc.ap()[t * 128:(t + 1) * 128, :], hib[:])
            p.step()

    for dirname, rev, wdir, ydir in (("mf", False, wf, yf_dram),
                                     ("mb", True, wb, yb_dram)):
        with p.scope():
            build_mamba_dir(p, dsc, wdir, ydir, tag + dirname, reverse=rev)

    xffT = p.sb(tag + "_xffT", [128, 8 * T], BF16)
    ps0 = p.ps(tag + "_cps0", [128, 512], F32)
    ps1 = p.ps(tag + "_cps1", [128, 512], F32)
    ones1f = p.sb(tag + "_ones1f", [1, 128], F32)
    p.v("memset", ones1f[:], 1.0)
    p.step()
    g_bc = p.sb(tag + "_gbc", [128, D], F32)
    b_bc = p.sb(tag + "_bbc", [128, D], F32)
    b2_bc = p.sb(tag + "_b2bc", [128, D], F32)
    with p.scope():
        bcast_row_f32(p, ff["ln_g"], g_bc, ps0, ones1f)
        bcast_row_f32(p, ff["ln_b"], b_bc, ps0, ones1f)
        bcast_row_f32(p, ff["b2"], b2_bc, ps0, ones1f)
    with p.scope():
        xt = p.sb(tag + "_cxt", [128, D], F32)
        t2 = p.sb(tag + "_ct2", [128, D], F32)
        hib = p.sb(tag + "_chib", [128, D], BF16)
        sc = ln_scratch(p)
        dsc2 = p.dram(tag + "_dsc2", [T, D], BF16)
        for t in range(NT):
            r0, r1 = t * 128, (t + 1) * 128
            p.dma("sync", xt[:], yf_dram.ap()[r0:r1, :])
            p.step()
            p.dma("sync", t2[:], yb_dram.ap()[r0:r1, :])
            p.step()
            p.v("tensor_add", xt[:], xt[:], t2[:])
            p.step()
            p.dma("sync", t2[:], xln_dram.ap()[r0:r1, :])
            p.step()
            p.v("scalar_tensor_tensor", xt[:], xt[:], 0.5, t2[:],
                ALU.mult, ALU.add)
            p.step()
            p.dma("sync", xblk_dram.ap()[r0:r1, :], xt[:])
            p.step()
            p.v("reduce_sum", sc["m"][:], xt[:], AX.X)
            p.step()
            p.v("tensor_scalar_mul", sc["m"][:], sc["m"][:], 1.0 / D)
            p.step()
            p.s("activation", sc["t1"][:], xt[:], AF.Square, accum_out=sc["sq"][:])
            p.step()
            p.v("tensor_scalar_mul", sc["sq"][:], sc["sq"][:], 1.0 / D)
            p.step()
            p.s("activation", sc["var"][:], sc["m"][:], AF.Square)
            p.step()
            p.v("tensor_sub", sc["var"][:], sc["sq"][:], sc["var"][:])
            p.step()
            p.s("activation", sc["rstd"][:], sc["var"][:], AF.Ln,
                bias=p.fconst(1e-5)[:])
            p.step()
            p.s("activation", sc["rstd"][:], sc["rstd"][:], AF.Exp,
                scale=p.fconst(-0.5)[:])
            p.step()
            p.v("tensor_scalar", t2[:], xt[:], sc["m"][:], sc["rstd"][:],
                ALU.subtract, ALU.mult)
            p.step()
            p.v("tensor_mul", t2[:], t2[:], g_bc[:])
            p.step()
            p.v("tensor_add", t2[:], t2[:], b_bc[:])
            p.step()
            p.v("tensor_copy", hib[:], t2[:])
            p.step()
            p.dma("sync", dsc2.ap()[t * 128:(t + 1) * 128, :], hib[:])
            p.step()
        for d8 in range(8):
            p.dma("sync", xffT[:, d8 * T:(d8 + 1) * T],
                  dsc2.ap()[:, d8 * 128:(d8 + 1) * 128], transpose=True)
            p.step()

    TH = 1024
    h1 = p.sb(tag + "_h1", [128, 32 * TH], BF16)
    b1l = p.sb(tag + "_b1l", [128, 32], F32)
    p.dma("sync", b1l[:], ff["b1_l"])
    p.step()
    with p.scope():
        w1c = p.sb(tag + "_w1c", [128, 8 * 128], BF16)
        wstg1k = p.sb(tag + "_wstg5", [128, 8 * 128], F32)
        w2h = p.sb(tag + "_w2h", [128, 32 * 512], BF16)
        w2hf = p.sb(tag + "_w2hf", [128, 16 * 512], F32)
        wstg6 = p.sb(tag + "_wstg6", [128, 512], F32)
        xres = p.sb(tag + "_xres", [128, 512], F32)
        for h in range(2):
            tsl = slice(h * TH, (h + 1) * TH)
            for ft in range(32):
                dma_wload(p, wstg1k.ap(), ff["w1T"][:, ft * 128:(ft + 1) * 128],
                          1024, 128)
                p.v("tensor_copy", w1c[:], wstg1k[:])
                for n in range(2):
                    bank = ps0 if n == 0 else ps1
                    for k in range(8):
                        p.t("matmul", bank[:],
                            w1c[:, k * 128:(k + 1) * 128],
                            xffT[:, k * T + h * TH + n * 512:
                                 k * T + h * TH + n * 512 + 512],
                            start=(k == 0), stop=(k == 7))
                    p.step()
                    p.s("activation",
                        h1[:, ft * TH + n * 512: ft * TH + n * 512 + 512],
                        bank[:], AF.Silu, bias=b1l[:, ft:ft + 1])
                    p.step()
            for nh in range(2):
                for hh in range(2):
                    dma_wload(p, w2hf.ap(),
                              ff["w2T"][hh * 2048:(hh + 1) * 2048,
                                        nh * 512:(nh + 1) * 512], 2048, 512)
                    p.v("tensor_copy",
                        w2h[:, hh * 16 * 512:(hh + 1) * 16 * 512], w2hf[:])
                for tt in range(8):
                    r0 = (h * 8 + tt) * 128
                    bank = ps0 if tt % 2 == 0 else ps1
                    for k in range(32):
                        p.t("matmul", bank[:],
                            h1[:, k * TH + tt * 128: k * TH + tt * 128 + 128],
                            w2h[:, k * 512:(k + 1) * 512],
                            start=(k == 0), stop=(k == 31))
                    p.step()
                    p.dma("sync", xres[:],
                          x_in_dram.ap()[r0:r0 + 128, nh * 512:(nh + 1) * 512])
                    p.step()
                    p.v("tensor_add", xres[:], xres[:], bank[:])
                    p.step()
                    p.v("tensor_add", xres[:], xres[:],
                        b2_bc[:, nh * 512:(nh + 1) * 512])
                    p.step()
                    p.dma("sync", wstg6[:],
                          xblk_dram.ap()[r0:r0 + 128, nh * 512:(nh + 1) * 512])
                    p.step()
                    p.v("tensor_add", xres[:], xres[:], wstg6[:])
                    p.step()
                    p.dma("sync",
                          x_out_dram.ap()[r0:r0 + 128, nh * 512:(nh + 1) * 512],
                          xres[:])
                    p.step()



NE = 32
CAP = 1024
NSLOT = NE * CAP
TRASH = NSLOT


def build_l0_router(p, x_in_dram, gwT_ap, xln_out, hloc_out,
                    expv_out, tag):
    xlnT_hi = p.sb(tag + "_Thi", [128, 8 * T], BF16)
    xlnT_lo = p.sb(tag + "_Tlo", [128, 8 * T], BF16)
    with p.scope():
        build_ln_and_transpose_into(p, x_in_dram, xlnT_hi, xlnT_lo, tag + "ln",
                                    tok_out_dram=xln_out)
    rhi = p.sb(tag + "_rhi", [128, 8 * 32], BF16)
    rlo = p.sb(tag + "_rlo", [128, 8 * 32], BF16)
    with p.scope():
        rstg = p.sb(tag + "_rstg", [128, 32], F32)
        rtmp = p.sb(tag + "_rtmp", [128, 32], F32)
        for k in range(8):
            ks = slice(k * 32, (k + 1) * 32)
            p.dma("sync", rstg[:], gwT_ap[k * 128:(k + 1) * 128, :])
            p.step()
            p.v("tensor_copy", rhi[:, ks], rstg[:])
            p.step()
            p.v("tensor_copy", rtmp[:], rhi[:, ks])
            p.step()
            p.v("tensor_sub", rtmp[:], rstg[:], rtmp[:])
            p.step()
            p.v("tensor_copy", rlo[:, ks], rtmp[:])
            p.step()
    lg_ps = p.ps(tag + "_lgps", [128, 32], F32)
    lg = p.sb(tag + "_lg", [128, 32], F32)
    ex = p.sb(tag + "_ex", [128, 32], F32)
    m8 = p.sb(tag + "_m8", [128, 8], F32)
    Hgt = p.sb(tag + "_Hgt", [128, 32], F32)
    Tt = p.sb(tag + "_Tt", [128, 32], F32)
    scn = p.sb(tag + "_scn", [128, 32], F32)
    cnt = p.sb(tag + "_cnt", [128, 1], F32)
    mx = p.sb(tag + "_mx", [128, 1], F32)
    lim = p.sb(tag + "_lim", [128, 1], F32)
    Hb = p.sb(tag + "_Hb", [128, 32], BF16)
    for t in range(NT):
        ts = slice(t * 128, (t + 1) * 128)
        for k in range(8):
            ks = slice(k * 32, (k + 1) * 32)
            xs_ = slice(k * T + t * 128, k * T + t * 128 + 128)
            p.t("matmul", lg_ps[:], xlnT_hi[:, xs_], rhi[:, ks],
                start=(k == 0), stop=False)
            p.t("matmul", lg_ps[:], xlnT_hi[:, xs_], rlo[:, ks],
                start=False, stop=False)
            p.t("matmul", lg_ps[:], xlnT_lo[:, xs_], rhi[:, ks],
                start=False, stop=(k == 7))
        p.step()
        p.v("tensor_copy", lg[:], lg_ps[:])
        p.step()
        p.v("reduce_max", mx[:], lg[:], AX.X)
        p.step()
        p.v("tensor_scalar_mul", mx[:], mx[:], -1.0)
        p.step()
        p.s("activation", ex[:], lg[:], AF.Exp, bias=mx[:])
        p.step()
        p.dma("sync", expv_out.ap()[ts, :], ex[:])
        p.step()
        p.v("max", out=m8[:], in_=lg[:])
        p.step()
        p.v("tensor_scalar", Hgt[:], lg[:], m8[:, 7:8], None, ALU.is_gt)
        p.step()
        p.v("tensor_scalar", Tt[:], lg[:], m8[:, 7:8], None, ALU.is_equal)
        p.step()
        p.v("tensor_reduce", cnt[:], Hgt[:], AX.X, ALU.add)
        p.step()
        p.v("tensor_tensor_scan", scn[:], Tt[:], Tt[:], 0.0, ALU.add, ALU.bypass)
        p.step()
        p.v("tensor_scalar", lim[:], cnt[:], -1.0, 8.0, ALU.mult, ALU.add)
        p.step()
        p.v("tensor_scalar", scn[:], scn[:], lim[:], None, ALU.is_le)
        p.step()
        p.v("tensor_mul", scn[:], scn[:], Tt[:])
        p.step()
        p.v("tensor_add", Hgt[:], Hgt[:], scn[:])
        p.step()
        p.v("tensor_copy", Hb[:], Hgt[:])
        p.step()
        p.dma("sync", hloc_out.ap()[ts, :], Hb[:])
        p.step()


def build_l0_s2(p, hfull_ap, hloc_ap, expv_ap, tokid_ap, rows_ap,
                ptable_out, gidx_out, w_out, tag):
    N = 8 * T
    with p.scope():
        zt = p.sb(tag + "_zt", [128, 257], I32)
        p.v("memset", zt[:], 0)
        p.step()
        p.dma("sync",
              ptable_out.ap().rearrange("(p a) w -> p (a w)", p=128), zt[:])
        p.step()
    pos_all = p.dram(tag + "_posall", [N, NE], F32)
    with p.scope():
        incl = p.sb(tag + "_incl", [32, N], F32)
        hf = p.sb(tag + "_hf", [32, N], BF16)
        p.dma("sync", hf[:], hfull_ap)
        p.step()
        p.v("tensor_tensor_scan", incl[:], hf[:], hf[:], 0.0, ALU.add,
            ALU.bypass)
        p.step()
        p.v("tensor_sub", incl[:], incl[:], hf[:])
        p.step()
        identf = p.sb(tag + "_identf", [128, 128], F32)
        p.dma("sync", identf[:],
              p.const("identf128", np.eye(128, dtype=np.float32)).ap()[:])
        p.step()
        psT = p.ps(tag + "_psT", [128, 128], F32)
        ptk = p.sb(tag + "_ptk", [128, 32], F32)
        for t in range(N // 128):
            p.t("transpose", psT[:, 0:32], incl[:, t * 128:(t + 1) * 128],
                identf[0:32, 0:32])
            p.step()
            p.v("tensor_copy", ptk[:], psT[:, 0:32])
            p.step()
            p.dma("sync", pos_all.ap()[t * 128:(t + 1) * 128, :], ptk[:])
            p.step()
    pos_tok = p.sb(tag + "_postok", [128, 32], F32)
    htok = p.sb(tag + "_htok", [128, 32], F32)
    htokb = p.sb(tag + "_htokb", [128, 32], BF16)
    expv = p.sb(tag + "_expv", [128, 32], F32)
    hkept = p.sb(tag + "_hkept", [128, 32], F32)
    r2 = p.sb(tag + "_r2", [128, 32], F32)
    A = p.sb(tag + "_A", [128, 32], F32)
    big_t = p.sb(tag + "_big", [128, 32], F32)
    e1024 = p.sb(tag + "_e1024", [128, 32], F32)
    trash32 = p.sb(tag + "_trash", [128, 32], F32)
    p.v("memset", big_t[:], 2.0e8)
    p.step()
    p.dma("sync", e1024[:],
          p.const("e1024", np.broadcast_to(
              (np.arange(32, dtype=np.float32) * CAP)[None, :],
              (128, 32)).copy()).ap()[:])
    p.step()
    gidx_f = p.sb(tag + "_gidxf", [128, 8], F32)
    praw = p.sb(tag + "_praw", [128, 8], F32)
    keptk = p.sb(tag + "_keptk", [128, 8], F32)
    wsum = p.sb(tag + "_wsum", [128, 1], F32)
    wt = p.sb(tag + "_wt", [128, 8], F32)
    gidx_i = p.sb(tag + "_gidxi", [128, 8], I32)
    sidx_f = p.sb(tag + "_sidxf", [128, 8], F32)
    sidx_i = p.sb(tag + "_sidxi", [128, 8], I32)
    tki = p.sb(tag + "_tki", [128, 1], I32)
    rws = p.sb(tag + "_rws", [128, 1], I32)
    for t in range(NT):
        ts = slice(t * 128, (t + 1) * 128)
        p.dma("sync", rws[:], rows_ap[ts, :])
        p.step()
        p.g("indirect_dma_start", pos_tok[:], None, pos_all.ap()[:, :],
            IndirectOffsetOnAxis(rws.ap() if False else rws[:], 0))
        p.step()
        p.dma("sync", htokb[:], hloc_ap[ts, :])
        p.step()
        p.v("tensor_copy", htok[:], htokb[:])
        p.step()
        p.dma("sync", expv[:], expv_ap[ts, :])
        p.step()
        p.dma("sync", tki[:], tokid_ap[ts, :])
        p.step()
        p.v("tensor_scalar", hkept[:], pos_tok[:], float(CAP) - 0.5, None,
            ALU.is_lt)
        p.step()
        p.v("tensor_mul", hkept[:], hkept[:], htok[:])
        p.step()
        p.v("tensor_tensor_scan", r2[:], htok[:], htok[:], 0.0, ALU.add,
            ALU.bypass)
        p.step()
        p.v("tensor_mul", r2[:], r2[:], htok[:])
        p.step()
        p.v("scalar_tensor_tensor", A[:], pos_tok[:], 1.0, e1024[:],
            ALU.add, ALU.add)
        p.step()
        p.v("tensor_mul", A[:], A[:], hkept[:])
        p.step()
        p.v("tensor_scalar", trash32[:], hkept[:], -2.0e8, 2.0e8,
            ALU.mult, ALU.add)
        p.step()
        p.v("tensor_add", A[:], A[:], trash32[:])
        p.step()
        for k in range(8):
            p.v("scalar_tensor_tensor", trash32[:], r2[:],
                float(k + 1), A[:], ALU.is_equal, ALU.mult,
                accum_out=gidx_f[:, k:k + 1])
            p.step()
            p.v("scalar_tensor_tensor", trash32[:], r2[:],
                float(k + 1), expv[:], ALU.is_equal, ALU.mult,
                accum_out=praw[:, k:k + 1])
            p.step()
            p.v("scalar_tensor_tensor", trash32[:], r2[:],
                float(k + 1), hkept[:], ALU.is_equal, ALU.mult,
                accum_out=keptk[:, k:k + 1])
            p.step()
        p.v("tensor_reduce", wsum[:], praw[:], AX.X, ALU.add)
        p.step()
        p.v("reciprocal", wsum[:], wsum[:])
        p.step()
        p.v("tensor_scalar_mul", wt[:], praw[:], wsum[:])
        p.step()
        p.v("tensor_mul", wt[:], wt[:], keptk[:])
        p.step()
        p.dma("sync", w_out.ap()[ts, :], wt[:])
        p.step()
        p.v("tensor_scalar", sidx_f[:], gidx_f[:], -1.0, float(TRASH),
            ALU.add, ALU.min)
        p.step()
        p.v("tensor_copy", sidx_i[:], sidx_f[:])
        p.step()
        p.v("tensor_scalar", gidx_f[:], gidx_f[:], -1.0, 0.0, ALU.add, ALU.max)
        p.step()
        p.v("tensor_scalar_min", gidx_f[:], gidx_f[:], float(NSLOT - 1))
        p.step()
        p.v("tensor_copy", gidx_i[:], gidx_f[:])
        p.step()
        p.dma("sync", gidx_out.ap()[ts, :], gidx_i[:])
        p.step()
        for k in range(8):
            p.g("indirect_dma_start", ptable_out.ap()[:, :],
                IndirectOffsetOnAxis(sidx_i[:, k:k + 1], 0), tki[:], None,
                grp="ptscatter")
        p.step()


def build_l0_s3(p, ptable_my_ap, xlnfull_dram, gupT_ap, downT_ap, eo_out, tag):
    tblf = p.sb(tag + "_tblf", [32, 128], F32)
    tmp = p.sb(tag + "_tmp", [32, 128], F32)
    tmpi = p.sb(tag + "_tmpi", [32, 128], I32)
    identf = p.sb(tag + "_idf", [128, 128], F32)
    p.dma("sync", identf[:],
          p.const("identf128", np.eye(128, dtype=np.float32)).ap()[:])
    p.step()
    psT = p.ps(tag + "_psT", [128, 128], F32)
    p.v("memset", tblf[:], 0.0)
    p.step()
    for r in range(8):
        p.dma("sync", tmpi[:],
              ptable_my_ap[r:r + 1, :].rearrange("w (p a) -> (w p) a", p=32))
        p.step()
        p.v("tensor_copy", tmp[:], tmpi[:])
        p.step()
        p.v("tensor_add", tblf[:], tblf[:], tmp[:])
        p.step()
    p.v("tensor_scalar", tblf[:], tblf[:], -1.0, 0.0, ALU.add, ALU.max)
    p.step()
    p.v("tensor_scalar_min", tblf[:], tblf[:], float(8 * T - 1))
    p.step()
    p.t("transpose", psT[:, 0:32], tblf[:], identf[0:32, 0:32])
    p.step()
    idxf = p.sb(tag + "_idxf", [128, 32], F32)
    idxi = p.sb(tag + "_idxi", [128, 32], I32)
    p.v("tensor_copy", idxf[:], psT[:, 0:32])
    p.step()
    p.v("tensor_copy", idxi[:], idxf[:])
    p.step()

    xrows = p.sb(tag + "_xrows", [128, D], BF16)
    xstage = p.dram(tag + "_xstage", [128, D], BF16)
    xeT = p.sb(tag + "_xeT", [128, 8 * 128], BF16)
    wgup = p.sb(tag + "_wgup", [128, 8 * 1536], BF16)
    wdn = p.sb(tag + "_wdn", [128, 6 * 1024], BF16)
    wstg8 = p.sb(tag + "_wstg8", [128, 8 * 1536], F32)
    gs = p.sb(tag + "_gs", [128, 6 * 128], BF16)
    eob = p.sb(tag + "_eob", [128, 1024], BF16)
    ps0 = p.ps(tag + "_ps0", [128, 512], F32)
    ps1 = p.ps(tag + "_ps1", [128, 512], F32)
    for e in range(NE):
        p.g("indirect_dma_start", xrows[:], None, xlnfull_dram.ap()[:, :],
            IndirectOffsetOnAxis(idxi[:, e:e + 1], 0))
        p.step()
        p.dma("sync", xstage.ap()[:, :], xrows[:])
        p.step()
        for d8 in range(8):
            p.dma("sync", xeT[:, d8 * 128:(d8 + 1) * 128],
                  xstage.ap()[:, d8 * 128:(d8 + 1) * 128], transpose=True)
            p.step()
        dma_wload(p, wstg8.ap(), gupT_ap[e], 1024, 1536)
        p.v("tensor_copy", wgup[:], wstg8[:])
        for mt in range(12):
            bank = ps0 if mt % 2 == 0 else ps1
            for k in range(8):
                p.t("matmul", bank[:, 0:128],
                    wgup[:, k * 1536 + mt * 128: k * 1536 + mt * 128 + 128],
                    xeT[:, k * 128:(k + 1) * 128],
                    start=(k == 0), stop=(k == 7))
            p.step()
            if mt < 6:
                p.s("activation", gs[:, mt * 128:(mt + 1) * 128],
                    bank[:, 0:128], AF.Silu)
            else:
                f = mt - 6
                p.v("tensor_mul", gs[:, f * 128:(f + 1) * 128],
                    bank[:, 0:128], gs[:, f * 128:(f + 1) * 128])
            p.step()
        dma_wload(p, wstg8.ap()[:, 0:6 * 1024], downT_ap[e], 768, 1024)
        p.v("tensor_copy", wdn[:], wstg8[:, 0:6 * 1024])
        for n in range(2):
            bank = ps0 if n == 0 else ps1
            for k in range(6):
                p.t("matmul", bank[:],
                    gs[:, k * 128:(k + 1) * 128],
                    wdn[:, k * 1024 + n * 512: k * 1024 + n * 512 + 512],
                    start=(k == 0), stop=(k == 5))
            p.step()
            p.v("tensor_copy", eob[:, n * 512:(n + 1) * 512], bank[:])
            p.step()
        p.dma("sync", eo_out.ap()[e * 128:(e + 1) * 128, :], eob[:])
        p.step()


def build_l0_s4(p, x_in_dram, eofull_dram, gidx_ap, w_ap, x_out_dram, tag):
    acc = p.sb(tag + "_acc", [128, D], F32)
    grows = p.sb(tag + "_grows", [128, D], BF16)
    wt = p.sb(tag + "_wt", [128, 8], F32)
    gix = p.sb(tag + "_gix", [128, 8], I32)
    xres = p.sb(tag + "_xres", [128, D], F32)
    for t in range(NT):
        ts = slice(t * 128, (t + 1) * 128)
        p.dma("sync", wt[:], w_ap[ts, :])
        p.step()
        p.dma("sync", gix[:], gidx_ap[ts, :])
        p.step()
        p.v("memset", acc[:], 0.0)
        p.step()
        for k in range(8):
            p.g("indirect_dma_start", grows[:], None, eofull_dram.ap()[:, :],
                IndirectOffsetOnAxis(gix[:, k:k + 1], 0))
            p.step()
            p.v("scalar_tensor_tensor", acc[:], grows[:], wt[:, k:k + 1],
                acc[:], ALU.mult, ALU.add)
            p.step()
        p.dma("sync", xres[:], x_in_dram.ap()[ts, :])
        p.step()
        p.v("scalar_tensor_tensor", xres[:], acc[:], 0.5, xres[:],
            ALU.mult, ALU.add)
        p.step()
        p.dma("sync", x_out_dram.ap()[ts, :], xres[:])
        p.step()



def _runp(p, in_maps):
    return p.run(in_maps)


def _mamba_host_layout(pref, I):
    def vec_l(v, w_=16):
        return np.ascontiguousarray(v.reshape(w_, 128).T)
    return {
        "in_wT": np.ascontiguousarray(I[pref + "in_w"].T),
        "x_wT": np.ascontiguousarray(I[pref + "x_w"].T),
        "dt_wT": np.ascontiguousarray(I[pref + "dt_w"].T),
        "out_wT": np.ascontiguousarray(I[pref + "out_w"].T),
        "conv_w_l": np.ascontiguousarray(
            I[pref + "conv_w"].reshape(16, 128, 4).transpose(1, 0, 2)
            .reshape(128, 64)),
        "conv_b_l": vec_l(I[pref + "conv_b"]),
        "dt_b_l": vec_l(I[pref + "dt_b"]),
        "D_l": vec_l(I[pref + "D"]),
        "A_log_l": np.ascontiguousarray(
            I[pref + "A_log"].reshape(16, 128, 16).transpose(1, 0, 2)
            .reshape(128, 256)),
    }


def _pad_qk(W):
    WT = np.ascontiguousarray(W.T)
    out = np.zeros((1024, 512), np.float32)
    for h in range(16):
        out[:, 32 * h:32 * h + 16] = WT[:, 16 * h:16 * h + 16]
    return out


MAMBA_W_NAMES = ["in_wT", "x_wT", "dt_wT", "out_wT", "conv_w_l", "conv_b_l",
                 "dt_b_l", "D_l", "A_log_l"]
MAMBA_W_SHAPES = {"in_wT": [1024, 4096], "x_wT": [2048, 96],
                  "dt_wT": [64, 2048], "out_wT": [2048, 1024],
                  "conv_w_l": [128, 64], "conv_b_l": [128, 16],
                  "dt_b_l": [128, 16], "D_l": [128, 16],
                  "A_log_l": [128, 256]}
N_ALL = 8 * T


def _launch_s1(x_shards, gwT, tagname):
    p = Prog(tagname)
    x_in = p.dram("x", [T, D], F32, kind="ExternalInput")
    gw = p.dram("gwT", [1024, 32], F32, kind="ExternalInput")
    xln_o = p.dram("xln", [T, D], BF16, kind="ExternalOutput")
    hloc_o = p.dram("hloc", [T, 32], BF16, kind="ExternalOutput")
    expv_o = p.dram("expv", [T, 32], F32, kind="ExternalOutput")
    build_l0_router(p, x_in, gw.ap(), xln_o, hloc_o, expv_o, "r")
    return p.run([{"x": x_shards[c], "gwT": gwT} for c in range(8)])


def _launch_s2(r1, tagname):
    hfull = np.concatenate([r1[c]["hloc"] for c in range(8)], axis=0)
    hfullT = np.ascontiguousarray(hfull.T)
    p = Prog(tagname)
    hf_in = p.dram("hfull", [32, N_ALL], BF16, kind="ExternalInput")
    hl_in = p.dram("hloc", [T, 32], BF16, kind="ExternalInput")
    ev_in = p.dram("expv", [T, 32], F32, kind="ExternalInput")
    tk_in = p.dram("tokid", [T, 1], I32, kind="ExternalInput")
    rw_in = p.dram("rows", [T, 1], I32, kind="ExternalInput")
    pt_o = p.dram("ptable", [NSLOT + 128, 1], I32, kind="ExternalOutput")
    gi_o = p.dram("gidx", [T, 8], I32, kind="ExternalOutput")
    w_o = p.dram("w", [T, 8], F32, kind="ExternalOutput")
    build_l0_s2(p, hf_in.ap(), hl_in.ap(), ev_in.ap(), tk_in.ap(), rw_in.ap(),
                pt_o, gi_o, w_o, "s2")
    return p.run([{
        "hfull": hfullT, "hloc": r1[c]["hloc"], "expv": r1[c]["expv"],
        "tokid": (np.arange(T) + c * T + 1).reshape(-1, 1).astype(np.int32),
        "rows": (np.arange(T) + c * T).reshape(-1, 1).astype(np.int32),
    } for c in range(8)])


def _launch_s3(r1, r2, gupT, downT, tagname):
    ptables = np.stack([r2[c]["ptable"][:, 0] for c in range(8)])
    xlnfull = np.concatenate([r1[c]["xln"] for c in range(8)], axis=0)
    pt_my = []
    for c in range(8):
        sel = np.zeros((8, 32, 128), np.int32)
        for e in range(NE):
            sel[:, e, :] = ptables[:, e * CAP + c * 128: e * CAP + (c + 1) * 128]
        pt_my.append(sel.reshape(8, 4096))
    p = Prog(tagname)
    ptm_in = p.dram("ptm", [8, 4096], I32, kind="ExternalInput")
    xlf_in = p.dram("xlnfull", [N_ALL, D], BF16, kind="ExternalInput")
    gu_in = p.dram("gupT", [32, 1024, 1536], F32, kind="ExternalInput")
    dn_in = p.dram("downT", [32, 768, 1024], F32, kind="ExternalInput")
    eo_o = p.dram("eo", [4096, D], BF16, kind="ExternalOutput")
    build_l0_s3(p, ptm_in.ap(), xlf_in, gu_in.ap(), dn_in.ap(), eo_o, "s3")
    return p.run([{"ptm": pt_my[c], "xlnfull": xlnfull, "gupT": gupT,
                   "downT": downT} for c in range(8)])


def _eo_full(r3):
    eof = np.zeros((NSLOT, D), bf16)
    for c in range(8):
        part = r3[c]["eo"].reshape(NE, 128, D)
        for e in range(NE):
            eof[e * CAP + c * 128: e * CAP + (c + 1) * 128] = part[e]
    return eof


def kernel(**I):
    x = np.ascontiguousarray(I["L0_attn_x"]).astype(np.float32, copy=False)
    x_shards = [np.ascontiguousarray(x[c]) for c in range(8)]
    gwT = np.ascontiguousarray(I["L0_ffn_gate_weight"].T)
    gupT = np.ascontiguousarray(np.asarray(I["L0_ffn_gate_up_proj"])
                                .transpose(0, 2, 1))
    downT = np.ascontiguousarray(np.asarray(I["L0_ffn_down_proj"])
                                 .transpose(0, 2, 1))
    wf_h = _mamba_host_layout("L0_attn_fwd_", I)
    wb_h = _mamba_host_layout("L0_attn_bwd_", I)
    ff_h = {
        "w1T": np.ascontiguousarray(I["L0_attn_ff_w1"].T),
        "w2T": np.ascontiguousarray(I["L0_attn_ff_w2"].T),
        "b1_l": np.ascontiguousarray(I["L0_attn_ff_b1"].reshape(32, 128).T),
        "ln_g": np.ascontiguousarray(I["L0_attn_ff_ln_g"].reshape(1, -1)),
        "ln_b": np.ascontiguousarray(I["L0_attn_ff_ln_b"].reshape(1, -1)),
        "b2": np.ascontiguousarray(I["L0_attn_ff_b2"].reshape(1, -1)),
    }
    FF_SHAPES = {"w1T": [1024, 4096], "w2T": [4096, 1024], "b1_l": [128, 32],
                 "ln_g": [1, 1024], "ln_b": [1, 1024], "b2": [1, 1024]}

    r1 = _launch_s1(x_shards, gwT, "L1s1")
    r2 = _launch_s2(r1, "L2s2")
    r3 = _launch_s3(r1, r2, gupT, downT, "L3s3")
    eof1 = _eo_full(r3)

    p = Prog("L4big")
    x_in = p.dram("x", [T, D], F32, kind="ExternalInput")
    eo_in = p.dram("eofull", [NSLOT, D], BF16, kind="ExternalInput")
    gi_in = p.dram("gidx", [T, 8], I32, kind="ExternalInput")
    w_in = p.dram("w", [T, 8], F32, kind="ExternalInput")
    gw_in = p.dram("gwT", [1024, 32], F32, kind="ExternalInput")
    wf = {n: p.dram("f_" + n, MAMBA_W_SHAPES[n], F32,
                    kind="ExternalInput").ap() for n in MAMBA_W_NAMES}
    wb = {n: p.dram("b_" + n, MAMBA_W_SHAPES[n], F32,
                    kind="ExternalInput").ap() for n in MAMBA_W_NAMES}
    ff = {n: p.dram("ff_" + n, FF_SHAPES[n], F32,
                    kind="ExternalInput").ap() for n in FF_SHAPES}
    x2_o = p.dram("x2", [T, D], F32, kind="ExternalOutput")
    xln_o = p.dram("xln", [T, D], BF16, kind="ExternalOutput")
    hloc_o = p.dram("hloc", [T, 32], BF16, kind="ExternalOutput")
    expv_o = p.dram("expv", [T, 32], F32, kind="ExternalOutput")
    x1_d = p.dram("x1scr", [T, D], F32)
    with p.scope():
        build_l0_s4(p, x_in, eo_in, gi_in.ap(), w_in.ap(), x1_d, "s4")
    with p.scope():
        build_l0_attn_call(p, x1_d, x2_o, wf, wb, ff)
    with p.scope():
        build_l0_router(p, x2_o, gw_in.ap(), xln_o, hloc_o, expv_o, "r2")
    im4 = []
    for c in range(8):
        m = {"x": x_shards[c], "eofull": eof1, "gidx": r2[c]["gidx"],
             "w": r2[c]["w"], "gwT": gwT}
        for n in MAMBA_W_NAMES:
            m["f_" + n] = wf_h[n]
            m["b_" + n] = wb_h[n]
        for n in FF_SHAPES:
            m["ff_" + n] = ff_h[n]
        im4.append(m)
    r4 = _runp(p, im4)

    r5 = _launch_s2(r4, "L5s2")
    r6 = _launch_s3(r4, r5, gupT, downT, "L6s3")
    eof2 = _eo_full(r6)

    p = Prog("L7big")
    x_in = p.dram("x", [T, D], F32, kind="ExternalInput")
    eo_in = p.dram("eofull", [NSLOT, D], BF16, kind="ExternalInput")
    gi_in = p.dram("gidx", [T, 8], I32, kind="ExternalInput")
    w_in = p.dram("w", [T, 8], F32, kind="ExternalInput")
    rt_in = p.dram("routerDE", [1024, 8], F32, kind="ExternalInput")
    ga_in = p.dram("gating", [8, 1024, 2048], F32, kind="ExternalInput")
    up_in = p.dram("upw", [8, 1024, 2048], F32, kind="ExternalInput")
    dn_in = p.dram("downw", [8, 2048, 1024], F32, kind="ExternalInput")
    wq_in = p.dram("wqTp", [1024, 512], F32, kind="ExternalInput")
    wk_in = p.dram("wkTp", [1024, 512], F32, kind="ExternalInput")
    wv_in = p.dram("wvT", [1024, 1024], F32, kind="ExternalInput")
    wo_in = p.dram("woT", [1024, 1024], F32, kind="ExternalInput")
    out_o = p.dram("out", [T, D], F32, kind="ExternalOutput")
    x3_d = p.dram("x3scr", [T, D], F32)
    x4_d = p.dram("x4scr", [T, D], F32)
    x5_d = p.dram("x5scr", [T, D], F32)
    ga_bf = p.dram("ga_bf", [8, 1024, 2048], BF16)
    up_bf = p.dram("up_bf", [8, 1024, 2048], BF16)
    dn_bf = p.dram("dn_bf", [8, 2048, 1024], BF16)
    with p.scope():
        build_l0_s4(p, x_in, eo_in, gi_in.ap(), w_in.ap(), x3_d, "s4b")
    with p.scope():
        build_l1_ffn_call(p, x3_d, x4_d, rt_in.ap(), ga_bf, up_bf, dn_bf,
                          tag="l1fa", cast_srcs=(ga_in, up_in, dn_in))
    with p.scope():
        build_l1_attn_call(p, x4_d, x5_d, wq_in.ap(), wk_in.ap(), wv_in.ap(),
                           wo_in.ap(), tag="l1at")
    with p.scope():
        build_l1_ffn_call(p, x5_d, out_o, rt_in.ap(), ga_bf, up_bf, dn_bf,
                          tag="l1fb")
    im7 = []
    l1w = {
        "routerDE": np.ascontiguousarray(I["L1_ffn_kernel_router_DE"]),
        "gating": np.ascontiguousarray(I["L1_ffn_kernel_gating_EDF"]),
        "upw": np.ascontiguousarray(I["L1_ffn_kernel_up_EDF"]),
        "downw": np.ascontiguousarray(I["L1_ffn_kernel_down_EFD"]),
        "wqTp": _pad_qk(np.asarray(I["L1_attn_Wq"])),
        "wkTp": _pad_qk(np.asarray(I["L1_attn_Wk"])),
        "wvT": np.ascontiguousarray(I["L1_attn_Wv"].T),
        "woT": np.ascontiguousarray(I["L1_attn_Wo"].T),
    }
    for c in range(8):
        m = {"x": r4[c]["x2"], "eofull": eof2, "gidx": r5[c]["gidx"],
             "w": r5[c]["w"]}
        m.update(l1w)
        im7.append(m)
    r7 = _runp(p, im7)
    return np.stack([r7[c]["out"] for c in range(8)]).astype(np.float32)


# version 08 (final)
# speedup vs baseline: 1.1504x; 1.0135x over previous
import numpy as np
import ml_dtypes
from contextlib import ExitStack, contextmanager
import concourse.bass as bass
import concourse.mybir as mybir
from concourse.bass_utils import run_bass_kernel_spmd
from concourse.bass import IndirectOffsetOnAxis

F32 = mybir.dt.float32
BF16 = mybir.dt.bfloat16
I32 = mybir.dt.int32
AF = mybir.ActivationFunctionType
ALU = mybir.AluOpType
AX = mybir.AxisListType
bf16 = ml_dtypes.bfloat16

NCORES = 8
T = 2048
D = 1024
NT = T // 128


class Prog:

    ENGINES = ("sync", "vector", "scalar", "tensor", "gpsimd")
    NH = 8
    NG = 4
    SEMS = ("v", "s", "t", "p") + tuple(f"h{i}" for i in range(8)) \
        + tuple(f"g{i}" for i in range(4))
    ESEM = {"vector": "v", "scalar": "s", "tensor": "t", "gpsimd": "p"}
    SPECIAL = {"max", "max_index", "max_with_indices", "match_replace"}

    def __init__(self, name="prog"):
        self.nc = bass.Bass("TRN2", target_bir_lowering=False, name=name)
        self.ctx = ExitStack()
        self.root_ctx = self.ctx
        self.ops = []
        self.cum = {k: 0 for k in self.SEMS}
        self._consts = {}
        self._fconsts = {}
        self._names = {}
        self.finalized = False
        self._writes = {}
        self._reads = {}
        self._special_ops = set()

    def _uname(self, base):
        i = self._names.get(base, 0)
        self._names[base] = i + 1
        return f"{base}_{i}" if i else base

    def sb(self, name, shape, dtype):
        return self.ctx.enter_context(
            self.nc.sbuf_tensor(self._uname(name), shape, dtype))

    def ps(self, name, shape, dtype=F32):
        return self.ctx.enter_context(
            self.nc.psum_tensor(self._uname(name), shape, dtype))

    def dram(self, name, shape, dtype, kind="Internal"):
        return self.nc.dram_tensor(self._uname(name), shape, dtype, kind=kind)

    def const(self, key, arr):
        if key not in self._consts:
            self._consts[key] = self.nc.inline_tensor(
                np.ascontiguousarray(arr), name=self._uname("c_" + key))
        return self._consts[key]

    def fconst(self, value):
        v = float(value)
        if v not in self._fconsts:
            tile = self.root_ctx.enter_context(
                self.nc.sbuf_tensor(self._uname("fc"), [128, 1], F32,
                                    side="right"))
            self._fconsts[v] = tile
            self._rec("vector", (lambda e, t=tile, vv=v: e.memset(t[:], vv)),
                      [], [tile[:]], False)
        return self._fconsts[v]

    @contextmanager
    def scope(self):
        es = ExitStack()
        old = self.ctx
        self.ctx = es
        try:
            yield
        finally:
            self.ctx = old
            es.close()
            self._fence = dict(self.cum)

    @staticmethod
    def _interval(ap):
        t = ap.tensor
        pairs = [tuple(x) for x in ap.ap]
        off = ap.offset
        lo = hi = 0
        for (step, cnt) in pairs[1:] if len(pairs) > 1 else pairs:
            d = step * (cnt - 1)
            if d >= 0:
                hi += d
            else:
                lo += d
        tname = type(t).__name__
        if tname == "DRamTensorHandle":
            if len(pairs) > 1:
                d0 = pairs[0][0] * (pairs[0][1] - 1)
                if d0 >= 0:
                    hi += d0
                else:
                    lo += d0
            return (t.name, off + lo, off + hi + 1)
        pstep = pairs[0][0] if pairs else 1
        if pstep <= 0:
            return (t.name, 0, 1 << 60)
        foff = off % pstep if pstep > 0 else off
        return (t.name, foff + lo, foff + hi + 1)

    @classmethod
    def _collect_aps(cls, args, kw):
        aps = []
        for a in list(args) + list(kw.values()):
            if isinstance(a, bass.AP):
                aps.append(a)
            elif isinstance(a, IndirectOffsetOnAxis):
                aps.append(a.ap)
            elif hasattr(a, "ap") and callable(getattr(a, "ap", None)) \
                    and type(a).__name__ in ("SBTensorHandle",
                                             "PSumTensorHandle",
                                             "DRamTensorHandle"):
                aps.append(a.ap())
        return aps

    def _rec(self, engine, fn, in_aps, out_aps, is_dma, grp=None,
             special=False):
        if is_dma:
            if engine in ("sync", "scalar"):
                self._hrot = (self._hrot + 1) % self.NH
                semkey = f"h{self._hrot}"
            else:
                self._grot = (self._grot + 1) % self.NG
                semkey = f"g{self._grot}"
            inc = 16
        else:
            semkey = self.ESEM[engine]
            inc = 1
        idx = len(self.ops)
        waits = {k: v for k, v in self._fence.items() if v > 0}
        if is_dma and self.cum[semkey] > 0:
            if waits.get(semkey, -1) < self.cum[semkey]:
                waits[semkey] = self.cum[semkey]

        def add_dep(j):
            if j is None or j == idx:
                return
            (jeng, jsem, jinc, jcum, jdma, jspecial) = self._opinfo[j]
            if (jeng == engine == "tensor" and not jdma and not is_dma):
                return
            if waits.get(jsem, -1) < jcum:
                waits[jsem] = jcum

        ivs_in = [self._interval(a) for a in in_aps]
        ivs_out = [self._interval(a) for a in out_aps]
        for (nm, lo, hi) in ivs_in:
            for (wlo, whi, j, _g) in self._writes.get(nm, ()):
                if wlo < hi and lo < whi:
                    add_dep(j)
        for (nm, lo, hi) in ivs_out:
            for (wlo, whi, j, g) in self._writes.get(nm, ()):
                if wlo < hi and lo < whi and not (grp is not None and g == grp):
                    add_dep(j)
            for (rlo, rhi, j) in self._reads.get(nm, ()):
                if rlo < hi and lo < rhi:
                    add_dep(j)
        cum = self.cum[semkey] + inc
        self.cum[semkey] = cum
        self._opinfo.append((engine, semkey, inc, cum, is_dma, special))
        self.ops.append((engine, fn, semkey, inc, waits))
        for (nm, lo, hi) in ivs_in:
            self._reads.setdefault(nm, []).append((lo, hi, idx))
            if len(self._reads[nm]) > 192:
                self._reads[nm] = self._merge_tail(self._reads[nm])
        for (nm, lo, hi) in ivs_out:
            self._writes.setdefault(nm, []).append((lo, hi, idx, grp))
            if len(self._writes[nm]) > 192:
                self._writes[nm] = self._merge_tailw(self._writes[nm])
        return idx

    @staticmethod
    def _merge_tail(lst):
        old = lst[:-96]
        lo = min(x[0] for x in old)
        hi = max(x[1] for x in old)
        j = max(x[2] for x in old)
        return [(lo, hi, j)] + lst[-96:]

    @staticmethod
    def _merge_tailw(lst):
        old = lst[:-96]
        lo = min(x[0] for x in old)
        hi = max(x[1] for x in old)
        j = max(x[2] for x in old)
        return [(lo, hi, j, None)] + lst[-96:]

    OUT_KW = {"accum_out"}

    def _classify(self, meth, args, kw):
        n_out = 1
        outs, ins = [], []
        pos = list(args)
        if meth == "memset":
            if pos and isinstance(pos[0], bass.AP):
                outs.append(pos[0])
            return ins, outs
        if meth == "indirect_dma_start":
            o, ooff, i, ioff = pos[0], pos[1], pos[2], pos[3]
            outs.append(o)
            ins.append(i)
            if ooff is not None:
                ins.append(ooff.ap)
            if ioff is not None:
                ins.append(ioff.ap)
            return ins, outs
        if meth == "max":
            outs.append(kw.get("out", pos[0] if pos else None))
            iv = kw.get("in_", pos[1] if len(pos) > 1 else None)
            if iv is not None:
                ins.append(iv)
            return ins, outs
        if meth == "dma_start":
            o = kw.get("out", pos[0] if pos else None)
            i = kw.get("in_", pos[1] if len(pos) > 1 else None)
            outs.append(o)
            if i is not None:
                ins.append(i)
            return ins, outs
        first = True
        for a in pos:
            if isinstance(a, bass.AP):
                if first:
                    outs.append(a)
                    first = False
                else:
                    ins.append(a)
            elif isinstance(a, IndirectOffsetOnAxis):
                ins.append(a.ap)
        for k, v in kw.items():
            if isinstance(v, bass.AP):
                if k in self.OUT_KW:
                    outs.append(v)
                else:
                    ins.append(v)
        if meth == "matmul" and not kw.get("start", True):
            ins.append(outs[0])
        return ins, outs

    def _emit(self, engine, meth, args, kw, is_dma, grp=None):
        args = tuple(args)
        kw = dict(kw)
        ins, outs = self._classify(meth, args, kw)
        special = meth in self.SPECIAL

        def fn(e, meth=meth, args=args, kw=kw):
            return getattr(e, meth)(*args, **kw)
        self._rec(engine, fn, ins, outs, is_dma, grp=grp, special=special)

    def op(self, engine, fn, dma=False):
        raise RuntimeError("raw op() unsupported in dep scheduler")

    def dma(self, engine, out, in_, grp=None, **kw):
        self._emit(engine, "dma_start", (), {"out": out, "in_": in_, **kw},
                   True, grp=grp)

    def v(self, meth, *args, **kw):
        self._emit("vector", meth, args, kw, False)

    def s(self, meth, *args, **kw):
        self._emit("scalar", meth, args, kw, False)

    def g(self, meth, *args, **kw):
        grp = kw.pop("grp", None)
        self._emit("gpsimd", meth, args, kw, "dma" in meth, grp=grp)

    def t(self, meth, *args, **kw):
        self._emit("tensor", meth, args, kw, False)

    def step(self):
        pass

    _opinfo = None

    def finalize(self):
        assert not self.finalized
        self.finalized = True
        nc = self.nc
        referenced = [False] * len(self.ops)
        valmap = {}
        for i, (engine, fn, semkey, inc, waits) in enumerate(self.ops):
            valmap[(semkey, self._opinfo[i][3])] = i
        for (engine, fn, semkey, inc, waits) in self.ops:
            for k, vv in waits.items():
                j = valmap.get((k, vv))
                if j is not None:
                    referenced[j] = True
        newcum = {k: 0 for k in self.SEMS}
        remap = {}
        op_inc = [0] * len(self.ops)
        for i, (engine, fn, semkey, inc, waits) in enumerate(self.ops):
            is_dma = self._opinfo[i][4]
            if referenced[i] or is_dma:
                newcum[semkey] += inc
                op_inc[i] = inc
            remap[(semkey, self._opinfo[i][3])] = newcum[semkey]
        sems = {k: self.ctx.enter_context(nc.semaphore("SEM_" + k))
                for k in self.SEMS}
        streams = {e: [] for e in self.ENGINES}
        for i, (engine, fn, semkey, inc, waits) in enumerate(self.ops):
            w2 = {}
            for k, vv in waits.items():
                nv = remap[(k, vv)]
                if nv > 0 and w2.get(k, -1) < nv:
                    w2[k] = nv
            streams[engine].append((w2, fn, semkey, op_inc[i]))
        self.final_cum = newcum
        with nc.Block() as block:
            for ename in self.ENGINES:
                stream = streams[ename]
                if not stream and ename != "sync":
                    continue

                def make(stream=stream, ename=ename):
                    def body(eng):
                        last = {k: 0 for k in self.SEMS}
                        for waits, fn, semkey, inc in stream:
                            for k, vv in waits.items():
                                if vv > last[k]:
                                    eng.wait_ge(sems[k], vv)
                                    last[k] = vv
                            ins = fn(eng)
                            if inc:
                                ins.then_inc(sems[semkey], inc)
                        if ename == "sync":
                            for k in self.SEMS:
                                if self.final_cum[k] > last[k]:
                                    eng.wait_ge(sems[k], self.final_cum[k])
                    return body
                getattr(block, ename)(make())
        return nc

    def run(self, in_maps, trace=False):
        nc = self.finalize()
        res = run_bass_kernel_spmd(nc, in_maps,
                                   core_ids=list(range(len(in_maps))),
                                   trace=trace)
        self.ctx.close()
        return res.results


Prog._opinfo = None


def _progpatch_init(orig_init):
    def init(self, *a, **k):
        orig_init(self, *a, **k)
        self._opinfo = []
        self._fence = {}
        self._hrot = 0
        self._grot = 0
    return init


Prog.__init__ = _progpatch_init(Prog.__init__)



def ident_bf16(p):
    return p.const("ident128", np.eye(128, dtype=bf16))


def load_weight_bf16_T(p, src_ap, rows, cols, name):
    stage = p.sb("wprep_stage", [128, max(cols, rows)], F32)
    stageb = p.sb("wprep_stageb", [128, max(cols, rows)], BF16)
    out_dram = p.dram(name + "_T", [cols, rows], BF16)
    tmp_dram = p.dram(name + "_bf", [rows, cols], BF16)
    for r in range(rows // 128):
        p.dma("sync", stage[:, 0:cols], src_ap[r * 128:(r + 1) * 128, :])
        p.step()
        p.v("tensor_copy", stageb[:, 0:cols], stage[:, 0:cols])
        p.step()
        p.dma("sync", tmp_dram.ap()[r * 128:(r + 1) * 128, :], stageb[:, 0:cols])
        p.step()
    for c in range(cols // 128):
        p.dma("sync", stageb[:, 0:rows],
              tmp_dram.ap()[:, c * 128:(c + 1) * 128], transpose=True)
        p.step()
        p.dma("sync", out_dram.ap()[c * 128:(c + 1) * 128, :], stageb[:, 0:rows])
        p.step()
    return out_dram


def load_weight_bf16(p, src_ap, rows, cols, name):
    out_dram = p.dram(name + "_bf", [rows, cols], BF16)
    stage = p.sb("wcast_stage", [128, cols], F32)
    stageb = p.sb("wcast_stageb", [128, cols], BF16)
    assert rows % 128 == 0
    for r in range(rows // 128):
        p.dma("sync", stage[:], src_ap[r * 128:(r + 1) * 128, :])
        p.step()
        p.v("tensor_copy", stageb[:], stage[:])
        p.step()
        p.dma("sync", out_dram.ap()[r * 128:(r + 1) * 128, :], stageb[:])
        p.step()
    return out_dram


def layernorm_tile(p, out_bf, x_sb, scratch, eps=1e-5):
    m = scratch["m"]; sq = scratch["sq"]; var = scratch["var"]
    rstd = scratch["rstd"]; t1 = scratch["t1"]
    p.v("reduce_sum", m[:], x_sb[:], AX.X)
    p.step()
    p.v("tensor_scalar_mul", m[:], m[:], 1.0 / D)
    p.step()
    p.s("activation", t1[:], x_sb[:], AF.Square, accum_out=sq[:])
    p.step()
    p.v("tensor_scalar_mul", sq[:], sq[:], 1.0 / D)
    p.step()
    p.s("activation", var[:], m[:], AF.Square)
    p.step()
    p.v("tensor_sub", var[:], sq[:], var[:])
    p.step()
    p.s("activation", rstd[:], var[:], AF.Ln, bias=p.fconst(eps)[:])
    p.step()
    p.s("activation", rstd[:], rstd[:], AF.Exp, scale=p.fconst(-0.5)[:])
    p.step()
    p.v("tensor_scalar", out_bf[:], x_sb[:], m[:], rstd[:], ALU.subtract, ALU.mult)
    p.step()


def ln_scratch(p):
    return {
        "m": p.sb("ln_m", [128, 1], F32),
        "sq": p.sb("ln_sq", [128, 1], F32),
        "var": p.sb("ln_var", [128, 1], F32),
        "rstd": p.sb("ln_rstd", [128, 1], F32),
        "t1": p.sb("ln_t1", [128, D], F32),
    }


def transpose_via_dram(p, src_sb_tiles, n_tiles, cols, dst_sb, dram_scratch):
    rows = n_tiles * 128
    for r in range(n_tiles):
        p.dma("sync", dram_scratch.ap()[r * 128:(r + 1) * 128, :], src_sb_tiles[r])
        p.step()
    for c in range(cols // 128):
        p.dma("sync", dst_sb[c][:, 0:rows],
              dram_scratch.ap()[:, c * 128:(c + 1) * 128], transpose=True)
        p.step()


MAXN = 512


def emit_matmul(p, out_ps, lhsT_ap, rhs_ap, start, stop):
    p.t("matmul", out_ps, lhsT_ap, rhs_ap, start=start, stop=stop)





def dma_wload(p, dst_sb_ap, src2d_ap, R, C, queue="sync"):
    a = R // 128
    p.dma(queue,
          dst_sb_ap.rearrange("p (a c) -> p a c", a=a),
          src2d_ap.rearrange("(a p) c -> p a c", p=128))



def build_l1_router(p, xlnT_hi, xlnT_lo, router_ap, w_full):
    rstg = p.sb("r_stg", [128, 8], F32)
    rhi = p.sb("r_hi", [128, 8 * 8], BF16)
    rlo = p.sb("r_lo", [128, 8 * 8], BF16)
    rtmp = p.sb("r_tmp", [128, 8], F32)
    for k in range(8):
        ks = slice(k * 8, (k + 1) * 8)
        p.dma("sync", rstg[:], router_ap[k * 128:(k + 1) * 128, :])
        p.step()
        p.v("tensor_copy", rhi[:, ks], rstg[:])
        p.step()
        p.v("tensor_copy", rtmp[:], rhi[:, ks])
        p.step()
        p.v("tensor_sub", rtmp[:], rstg[:], rtmp[:])
        p.step()
        p.v("tensor_copy", rlo[:, ks], rtmp[:])
        p.step()
    lg_ps = p.ps("lg_ps", [128, 8], F32)
    lg = p.sb("lg", [128, 8], F32)
    ex = p.sb("ex", [128, 8], F32)
    m8 = p.sb("m8t", [128, 8], F32)
    Hgt = p.sb("HgtT", [128, 8], F32)
    Tt = p.sb("TtT", [128, 8], F32)
    scn = p.sb("scnT", [128, 8], F32)
    cnt = p.sb("cntc", [128, 1], F32)
    mx = p.sb("mxc", [128, 1], F32)
    Z = p.sb("Zc", [128, 1], F32)
    p1 = p.sb("p1c", [128, 1], F32)
    p2 = p.sb("p2c", [128, 1], F32)
    nw2 = p.sb("nw2c", [128, 1], F32)
    nw1 = p.sb("nw1c", [128, 1], F32)
    uu = p.sb("uuT", [128, 8], F32)
    oh = p.sb("ohT", [128, 8], F32)
    for t in range(NT):
        ts = slice(t * 128, (t + 1) * 128)
        ws = slice(t * 8, (t + 1) * 8)
        for k in range(8):
            ks = slice(k * 8, (k + 1) * 8)
            p.t("matmul", lg_ps[:], xlnT_hi[:, k * T + t * 128: k * T + t * 128 + 128],
                rhi[:, ks], start=(k == 0), stop=False)
            p.t("matmul", lg_ps[:], xlnT_hi[:, k * T + t * 128: k * T + t * 128 + 128],
                rlo[:, ks], start=False, stop=False)
            p.t("matmul", lg_ps[:], xlnT_lo[:, k * T + t * 128: k * T + t * 128 + 128],
                rhi[:, ks], start=False, stop=(k == 7))
        p.step()
        p.v("tensor_copy", lg[:], lg_ps[:])
        p.step()
        p.v("reduce_max", mx[:], lg[:], AX.X)
        p.step()
        p.v("tensor_scalar_mul", mx[:], mx[:], -1.0)
        p.step()
        p.s("activation", ex[:], lg[:], AF.Exp, bias=mx[:], accum_out=Z[:])
        p.step()
        p.v("reciprocal", Z[:], Z[:])
        p.step()
        p.v("tensor_scalar_mul", ex[:], ex[:], Z[:])
        p.step()
        p.v("max", out=m8[:], in_=lg[:])
        p.step()
        p.step()
        p.v("tensor_scalar", Hgt[:], lg[:], m8[:, 1:2], None, ALU.is_gt)
        p.step()
        p.v("tensor_scalar", Tt[:], lg[:], m8[:, 1:2], None, ALU.is_equal)
        p.step()
        p.v("tensor_reduce", cnt[:], Hgt[:], AX.X, ALU.add)
        p.step()
        p.v("tensor_tensor_scan", scn[:], Tt[:], Tt[:], 0.0, ALU.add, ALU.bypass)
        p.step()
        p.v("scalar_tensor_tensor", uu[:], scn[:], 1.0, Tt[:], ALU.mult, ALU.mult)
        p.step()
        p.v("tensor_scalar", uu[:], uu[:], cnt[:], None, ALU.add)
        p.step()
        p.v("tensor_mul", uu[:], uu[:], Tt[:])
        p.step()
        p.v("tensor_scalar", oh[:], uu[:], 0.5, 2.5, ALU.is_ge, ALU.bypass)
        p.step()
        p.v("tensor_scalar", scn[:], uu[:], 2.5, None, ALU.is_le)
        p.step()
        p.v("tensor_mul", oh[:], oh[:], scn[:])
        p.step()
        p.v("tensor_add", oh[:], oh[:], Hgt[:])
        p.step()
        p.v("tensor_mul", scn[:], ex[:], oh[:])
        p.step()
        p.v("reduce_max", p1[:], scn[:], AX.X)
        p.step()
        p.v("tensor_reduce", p2[:], scn[:], AX.X, ALU.add)
        p.step()
        p.v("tensor_sub", p2[:], p2[:], p1[:])
        p.step()
        p.v("tensor_sub", nw2[:], p2[:], p1[:])
        p.step()
        p.s("activation", nw2[:], nw2[:], AF.Sigmoid)
        p.step()
        p.v("tensor_scalar", nw1[:], nw2[:], -1.0, 1.0, ALU.mult, ALU.add)
        p.step()
        p.v("tensor_scalar", oh[:], uu[:], 1.0, None, ALU.is_equal)
        p.step()
        p.v("tensor_add", oh[:], oh[:], Hgt[:])
        p.step()
        p.v("tensor_scalar", scn[:], uu[:], 2.0, None, ALU.is_equal)
        p.step()
        p.v("tensor_scalar_mul", oh[:], oh[:], nw1[:])
        p.step()
        p.v("tensor_scalar_mul", scn[:], scn[:], nw2[:])
        p.step()
        p.v("tensor_add", w_full[:, ws], oh[:], scn[:])
        p.step()


def build_l1_ffn_experts(p, xlnT_hi, w_full, gate_bf, up_bf, down_bf,
                         x_in_dram, x_out_dram, cast_srcs=None):
    TH = 1024
    wg2 = [p.sb("wg", [128, 8 * 2048], BF16) for _ in range(2)]
    gs = p.sb("gs", [128, 16 * TH], BF16)
    wd = p.sb("wd", [128, 16 * 1024], BF16)
    wcst = [p.sb("wcst", [128, 1024], F32) for _ in range(2)] \
        if cast_srcs else None

    def wload(dst_sb, bf_dram, f32_dram, e, R, C, first):
        if not (cast_srcs and first):
            dma_wload(p, dst_sb.ap(), bf_dram.ap()[e], R, C)
            return
        per = (1024 // C) * 128 or 128
        ar = max(per // 128, 1)
        sub = C // 1024 if C > 1024 else 1
        cw = C // sub
        for hh in range(R // per):
            rs = slice(hh * per, (hh + 1) * per)
            for ss in range(sub):
                wb = wcst[(hh * sub + ss) % 2]
                cs = slice(hh * ar * C + ss * cw,
                           hh * ar * C + ss * cw + cw)
                p.dma("sync", wb[:, 0:cw],
                      f32_dram.ap()[e][rs, ss * cw:(ss + 1) * cw])
                p.s("copy", dst_sb[:, cs], wb[:, 0:cw])
        for hh in range(R // per):
            rs = slice(hh * per, (hh + 1) * per)
            cs = slice(hh * ar * C, (hh + 1) * ar * C)
            p.dma("sync",
                  bf_dram.ap()[e][rs, :].rearrange("(a p) c -> p a c", p=128),
                  dst_sb[:, cs].rearrange("p (a c) -> p a c", a=ar))
    acc = p.sb("accT", [128, 8 * 1024], F32)
    xres = p.sb("xres", [128, 1024], F32)
    ps = [p.ps("pmm", [128, 512], F32) for _ in range(2)]
    pbank = [0]

    def nextps():
        pbank[0] ^= 1
        return ps[pbank[0]]

    for h in range(2):
        for tt in range(8):
            p.v("memset", acc[:, tt * 1024:(tt + 1) * 1024], 0.0)
        p.step()
        for e in range(8):
            for which, wsrc, fsrc in (("g", gate_bf, 0), ("u", up_bf, 1)):
                wg = wg2[0 if which == "g" else 1]
                wload(wg, wsrc, cast_srcs[fsrc] if cast_srcs else None,
                      e, 1024, 2048, h == 0)
                for ft in range(16):
                    for nh in range(2):
                        bank = nextps()
                        for k in range(8):
                            p.t("matmul", bank[:],
                                wg[:, k * 2048 + ft * 128: k * 2048 + ft * 128 + 128],
                                xlnT_hi[:, k * T + h * TH + nh * 512:
                                        k * T + h * TH + nh * 512 + 512],
                                start=(k == 0), stop=(k == 7))
                        p.step()
                        dst = gs[:, ft * TH + nh * 512: ft * TH + nh * 512 + 512]
                        if which == "g":
                            p.s("activation", dst, bank[:], AF.Silu)
                        else:
                            p.v("tensor_mul", dst, bank[:], dst)
                        p.step()
            wload(wd, down_bf, cast_srcs[2] if cast_srcs else None,
                  e, 2048, 1024, h == 0)
            for tt in range(8):
                wcol = w_full[:, (h * 8 + tt) * 8 + e:(h * 8 + tt) * 8 + e + 1]
                for nh2 in range(2):
                    bank = nextps()
                    for k in range(16):
                        p.t("matmul", bank[:],
                            gs[:, k * TH + tt * 128: k * TH + tt * 128 + 128],
                            wd[:, k * 1024 + nh2 * 512: k * 1024 + nh2 * 512 + 512],
                            start=(k == 0), stop=(k == 15))
                    p.step()
                    a = acc[:, tt * 1024 + nh2 * 512: tt * 1024 + nh2 * 512 + 512]
                    p.v("scalar_tensor_tensor", a, bank[:], wcol, a,
                        ALU.mult, ALU.add)
                    p.step()
        for tt in range(8):
            row = (h * 8 + tt) * 128
            p.dma("sync", xres[:], x_in_dram.ap()[row:row + 128, :])
            p.step()
            p.v("scalar_tensor_tensor", xres[:],
                acc[:, tt * 1024:(tt + 1) * 1024], 0.5, xres[:],
                ALU.mult, ALU.add)
            p.step()
            p.dma("sync", x_out_dram.ap()[row:row + 128, :], xres[:])
            p.step()


def build_l1_ffn_call(p, x_in_dram, x_out_dram, router_ap, gate_bf, up_bf,
                      down_bf, tag="l1f", cast_srcs=None):
    xlnT_hi = p.sb(tag + "_Thi", [128, 8 * T], BF16)
    w_full = p.sb(tag + "_wfull", [128, NT * 8], F32)
    with p.scope():
        xlnT_lo = p.sb(tag + "_Tlo", [128, 8 * T], BF16)
        with p.scope():
            build_ln_and_transpose_into(p, x_in_dram, xlnT_hi, xlnT_lo, tag)
        build_l1_router(p, xlnT_hi, xlnT_lo, router_ap, w_full)
    build_l1_ffn_experts(p, xlnT_hi, w_full, gate_bf, up_bf, down_bf,
                         x_in_dram, x_out_dram, cast_srcs=cast_srcs)


def build_ln_and_transpose_into(p, x_src_dram, xlnT_hi, xlnT_lo, tag="ln",
                                tok_out_dram=None):
    xt = p.sb(tag + "_xt", [128, D], F32)
    lnf = p.sb(tag + "_f", [128, D], F32)
    hib = p.sb(tag + "_hib", [128, D], BF16)
    lob = p.sb(tag + "_lob", [128, D], BF16)
    sc = ln_scratch(p)
    tmp = sc["t1"]
    dsc = p.dram(tag + "_dsc", [T, D], BF16)
    dsclo = p.dram(tag + "_dsclo", [T, D], BF16) if xlnT_lo is not None else None
    for t in range(NT):
        p.dma("sync", xt[:], x_src_dram.ap()[t * 128:(t + 1) * 128, :])
        p.step()
        p.v("reduce_sum", sc["m"][:], xt[:], AX.X)
        p.step()
        p.v("tensor_scalar_mul", sc["m"][:], sc["m"][:], 1.0 / D)
        p.step()
        p.s("activation", tmp[:], xt[:], AF.Square, accum_out=sc["sq"][:])
        p.step()
        p.v("tensor_scalar_mul", sc["sq"][:], sc["sq"][:], 1.0 / D)
        p.step()
        p.s("activation", sc["var"][:], sc["m"][:], AF.Square)
        p.step()
        p.v("tensor_sub", sc["var"][:], sc["sq"][:], sc["var"][:])
        p.step()
        p.s("activation", sc["rstd"][:], sc["var"][:], AF.Ln, bias=p.fconst(1e-5)[:])
        p.step()
        p.s("activation", sc["rstd"][:], sc["rstd"][:], AF.Exp,
            scale=p.fconst(-0.5)[:])
        p.step()
        p.v("tensor_scalar", lnf[:], xt[:], sc["m"][:], sc["rstd"][:],
            ALU.subtract, ALU.mult)
        p.step()
        p.v("tensor_copy", hib[:], lnf[:])
        p.step()
        p.dma("sync", dsc.ap()[t * 128:(t + 1) * 128, :], hib[:])
        p.step()
        if tok_out_dram is not None:
            p.dma("sync", tok_out_dram.ap()[t * 128:(t + 1) * 128, :], hib[:])
            p.step()
        if xlnT_lo is not None:
            p.v("tensor_copy", tmp[:], hib[:])
            p.step()
            p.v("tensor_sub", tmp[:], lnf[:], tmp[:])
            p.step()
            p.v("tensor_copy", lob[:], tmp[:])
            p.step()
            p.dma("sync", dsclo.ap()[t * 128:(t + 1) * 128, :], lob[:])
            p.step()
    for src_d, dstT in ((dsc, xlnT_hi), (dsclo, xlnT_lo)):
        if dstT is None:
            continue
        for d8 in range(8):
            p.dma("sync", dstT[:, d8 * T:(d8 + 1) * T],
                  src_d.ap()[:, d8 * 128:(d8 + 1) * 128], transpose=True)
            p.step()



FD = 16
NH = 16
HD = 64
DF = 153
DFP = 160
RRD = 2.0
CHUNK = 128
NCH = T // CHUNK


def build_l1_attn_call(p, x_in_dram, x_out_dram, wqT_ap, wkT_ap, wvT_ap,
                       woT_ap, tag="l1a"):
    ident = p.sb(tag + "_ident", [128, 128], BF16)
    p.dma("sync", ident[:], p.const("ident128", np.eye(128, dtype=bf16)).ap()[:])
    p.step()
    mk = np.zeros((128, 128), np.float32)
    for i in range(128):
        mk[i, i:] = 1.0
    mask = p.sb(tag + "_mask", [128, 128], F32)
    p.dma("sync", mask[:], p.const("maskcausal", mk).ap()[:])
    p.step()
    onesb = p.sb(tag + "_ones", [128, 1], BF16)
    p.v("memset", onesb[:], 1.0)
    p.step()
    ones_row = p.sb(tag + "_onesr", [1, 64], BF16)
    p.v("memset", ones_row[:], 1.0)
    p.step()

    v_tok = p.sb(tag + "_vtok", [128, NT * 1024], BF16)
    q_tok = p.sb(tag + "_qtok", [128, NT * 512], BF16)
    k_tok = p.sb(tag + "_ktok", [128, NT * 512], BF16)
    y_all = p.sb(tag + "_yall", [128, 8 * T], BF16)

    ps0 = p.ps(tag + "_ps0", [128, 512], F32)
    ps1 = p.ps(tag + "_ps1", [128, 512], F32)
    ps2 = p.ps(tag + "_ps2", [128, 512], F32)
    ps3 = p.ps(tag + "_ps3", [128, 512], F32)
    psT = p.ps(tag + "_psT", [128, 256], BF16)

    with p.scope():
        xlnT = p.sb(tag + "_Thi", [128, 8 * T], BF16)
        with p.scope():
            build_ln_and_transpose_into(p, x_in_dram, xlnT, None, tag + "ln")
        wqk = p.sb(tag + "_wqk", [128, 8 * 512], BF16)
        wv = p.sb(tag + "_wv", [128, 8 * 1024], BF16)
        wstg = p.sb(tag + "_wstg", [128, 8 * 1024], F32)
        for wap, tdst in ((wqT_ap, q_tok), (wkT_ap, k_tok)):
            dma_wload(p, wstg.ap()[:, 0:8 * 512], wap, 1024, 512)
            p.v("tensor_copy", wqk[:], wstg[:, 0:8 * 512])
            for tt in range(NT):
                bank = ps0 if tt % 2 == 0 else ps1
                for k in range(8):
                    p.t("matmul", bank[:],
                        xlnT[:, k * T + tt * 128: k * T + tt * 128 + 128],
                        wqk[:, k * 512:(k + 1) * 512],
                        start=(k == 0), stop=(k == 7))
                p.step()
                p.v("tensor_copy", tdst[:, tt * 512:(tt + 1) * 512], bank[:])
                p.step()
        dma_wload(p, wstg.ap(), wvT_ap, 1024, 1024)
        p.v("tensor_copy", wv[:], wstg[:])
        for tt in range(NT):
            for n in range(2):
                bank = ps0 if n == 0 else ps1
                for k in range(8):
                    p.t("matmul", bank[:],
                        xlnT[:, k * T + tt * 128: k * T + tt * 128 + 128],
                        wv[:, k * 1024 + n * 512: k * 1024 + n * 512 + 512],
                        start=(k == 0), stop=(k == 7))
                p.step()
                p.v("tensor_copy",
                    v_tok[:, tt * 1024 + n * 512: tt * 1024 + n * 512 + 512],
                    bank[:])
                p.step()

    qf = p.sb(tag + "_qf", [128, NH * DFP], BF16)
    kf = p.sb(tag + "_kf", [128, NH * DFP], BF16)
    qfDa = p.sb(tag + "_qfDa", [128, 128], BF16)
    qfDb = p.sb(tag + "_qfDb", [32, 128], BF16)
    kfDa = p.sb(tag + "_kfDa", [128, 128], BF16)
    kfDb = p.sb(tag + "_kfDb", [32, 128], BF16)
    st_a = p.sb(tag + "_sta", [128, NH * 65], F32)
    st_b = p.sb(tag + "_stb", [32, NH * 65], F32)
    stba = p.sb(tag + "_stba", [128, 65], BF16)
    stbb = p.sb(tag + "_stbb", [32, 65], BF16)
    S_m = p.sb(tag + "_Sm", [128, 128], BF16)
    den = p.sb(tag + "_den", [1, 128], F32)
    denb = p.sb(tag + "_denb", [1, 128], BF16)
    rb = p.sb(tag + "_rb", [64, 128], F32)

    s2 = float(1.0 / np.sqrt(4.0 * np.sqrt(2.0)))
    p.v("memset", qf[:], 0.0)
    p.v("memset", kf[:], 0.0)
    p.v("memset", st_a[:], 0.0)
    p.v("memset", st_b[:], 0.0)
    p.step()
    for fbuf in (qf, kf):
        f3 = fbuf.ap().rearrange("p (h c) -> p h c", h=NH)
        p.v("memset", f3[:, :, 0:1], 1.0)
    p.step()

    def band_off(j):
        return 33 + 15 * j - (j * (j - 1)) // 2

    for c in range(NCH):
        for fbuf, tok in ((qf, q_tok), (kf, k_tok)):
            f3 = fbuf.ap().rearrange("p (h c) -> p h c", h=NH)
            t3 = tok.ap()[:, c * 512:(c + 1) * 512].rearrange(
                "p (h c) -> p h c", h=NH)[:, :, 0:16]
            p.v("tensor_scalar_mul", f3[:, :, 1:17], t3, 1.0 / RRD)
            p.step()
            p.s("activation", f3[:, :, 17:33], t3, AF.Square,
                scale=p.fconst(s2)[:])
            p.step()
            for j in range(15):
                w = 15 - j
                o = band_off(j)
                p.v("tensor_mul", f3[:, :, o:o + w],
                    f3[:, :, 1 + j:2 + j].to_broadcast([128, NH, w]),
                    f3[:, :, 2 + j:2 + j + w])
            p.step()
        for h in range(NH):
            B = h * DFP
            vsl = v_tok[:, c * 1024 + h * 64: c * 1024 + h * 64 + 64]
            sta = st_a[:, h * 65:(h + 1) * 65]
            stb = st_b[:, h * 65:(h + 1) * 65]
            ysl = y_all.ap()[64 * (h % 2):64 * (h % 2) + 64,
                             (h // 2) * T + c * 128:(h // 2) * T + c * 128 + 128]
            p.t("transpose", psT[:, 0:128], qf[:, B:B + 128], ident[:])
            p.step()
            p.v("tensor_copy", qfDa[:], psT[:, 0:128])
            p.step()
            p.t("transpose", psT[0:25, 0:128], qf[:, B + 128:B + 153], ident[:])
            p.step()
            p.v("tensor_copy", qfDb[0:25, :], psT[0:25, 0:128])
            p.step()
            p.t("transpose", psT[:, 0:128], kf[:, B:B + 128], ident[:])
            p.step()
            p.v("tensor_copy", kfDa[:], psT[:, 0:128])
            p.step()
            p.t("transpose", psT[0:25, 0:128], kf[:, B + 128:B + 153], ident[:])
            p.step()
            p.v("tensor_copy", kfDb[0:25, :], psT[0:25, 0:128])
            p.step()
            p.v("tensor_copy", stba[:], sta)
            p.v("tensor_copy", stbb[0:25, :], stb[0:25, :])
            p.step()
            p.t("matmul", ps0[:, 0:128], kfDa[:], qfDa[:], start=True, stop=False)
            p.t("matmul", ps0[:, 0:128], kfDb[0:25, :], qfDb[0:25, :],
                start=False, stop=True)
            p.step()
            p.v("tensor_mul", S_m[:], ps0[:, 0:128], mask[:])
            p.step()
            p.t("matmul", ps1[0:64, 0:128], vsl, S_m[:], start=True, stop=False)
            p.t("matmul", ps1[0:64, 0:128], stba[:, 0:64], qfDa[:],
                start=False, stop=False)
            p.t("matmul", ps1[0:64, 0:128], stbb[0:25, 0:64], qfDb[0:25, :],
                start=False, stop=True)
            p.step()
            p.t("matmul", ps2[0:1, 0:128], onesb[:], S_m[:], start=True, stop=False)
            p.t("matmul", ps2[0:1, 0:128], stba[:, 64:65], qfDa[:],
                start=False, stop=False)
            p.t("matmul", ps2[0:1, 0:128], stbb[0:25, 64:65], qfDb[0:25, :],
                start=False, stop=True)
            p.step()
            p.v("tensor_scalar", den[:], ps2[0:1, 0:128], 1e-12, None, ALU.add)
            p.step()
            p.v("reciprocal", den[:], den[:])
            p.step()
            p.v("tensor_copy", denb[:], den[:])
            p.step()
            p.t("matmul", ps3[0:64, 0:128], ones_row[:], denb[:],
                start=True, stop=True)
            p.step()
            p.v("tensor_copy", rb[:], ps3[0:64, 0:128])
            p.step()
            p.v("tensor_mul", ysl, ps1[0:64, 0:128], rb[:])
            p.step()
            p.t("matmul", ps0[:, 0:64], kf[:, B:B + 128], vsl, start=True, stop=True)
            p.t("matmul", ps0[:, 64:65], kf[:, B:B + 128], onesb[:],
                start=True, stop=True)
            p.step()
            p.v("tensor_add", sta, sta, ps0[:, 0:65])
            p.step()
            p.t("matmul", ps1[0:25, 0:64], kf[:, B + 128:B + 153], vsl,
                start=True, stop=True)
            p.t("matmul", ps1[0:25, 64:65], kf[:, B + 128:B + 153], onesb[:],
                start=True, stop=True)
            p.step()
            p.v("tensor_add", stb[0:25, :], stb[0:25, :], ps1[0:25, 0:65])
            p.step()

    with p.scope():
        wo = p.sb(tag + "_wo", [128, 8 * 1024], BF16)
        wstg2 = p.sb(tag + "_wstg2", [128, 8 * 1024], F32)
        xres = p.sb(tag + "_xres", [128, 1024], F32)
        dma_wload(p, wstg2.ap(), woT_ap, 1024, 1024)
        p.v("tensor_copy", wo[:], wstg2[:])
        for tt in range(NT):
            p.dma("sync", xres[:], x_in_dram.ap()[tt * 128:(tt + 1) * 128, :])
            p.step()
            for n in range(2):
                bank = ps0 if n == 0 else ps1
                for k in range(8):
                    p.t("matmul", bank[:],
                        y_all[:, k * T + tt * 128: k * T + tt * 128 + 128],
                        wo[:, k * 1024 + n * 512: k * 1024 + n * 512 + 512],
                        start=(k == 0), stop=(k == 7))
                p.step()
                p.v("tensor_add", xres[:, n * 512:(n + 1) * 512],
                    xres[:, n * 512:(n + 1) * 512], bank[:])
                p.step()
            p.dma("sync", x_out_dram.ap()[tt * 128:(tt + 1) * 128, :], xres[:])
            p.step()



DI = 2048
NDT = DI // 128
DSTATE = 16


def build_mamba_dir(p, src_dram, w, y_dram, tag, reverse=False):
    yraw_dram = p.dram(tag + "_yraw", [DI, T], BF16)
    z_dram = p.dram(tag + "_z", [DI, T], BF16)
    xc = p.sb(tag + "_xc", [128, NDT * T], BF16)
    ps0 = p.ps(tag + "_ps0", [128, 512], F32)
    ps1 = p.ps(tag + "_ps1", [128, 512], F32)

    with p.scope():
        xlnT = p.sb(tag + "_xlnT", [128, 8 * T], BF16)
        for d8 in range(8):
            p.dma("sync", xlnT[:, d8 * T:(d8 + 1) * T],
                  src_dram.ap()[:, d8 * 128:(d8 + 1) * 128], transpose=True)
            p.step()
        if reverse:
            rtmp = p.sb(tag + "_rtmp", [128, T], BF16)
            for d8 in range(8):
                p.v("tensor_copy", rtmp[:], xlnT[:, d8 * T:(d8 + 1) * T][:, ::-1])
                p.step()
                p.v("tensor_copy", xlnT[:, d8 * T:(d8 + 1) * T], rtmp[:])
                p.step()
        w_in = p.sb(tag + "_win", [128, 8 * 4096], BF16)
        wstg = p.sb(tag + "_wstg", [128, 2 * 4096], F32)
        zstg = p.sb(tag + "_zstg", [128, 512], BF16)
        for hh in range(4):
            dma_wload(p, wstg.ap(), w["in_wT"][hh * 256:(hh + 1) * 256, :],
                      256, 4096)
            p.v("tensor_copy", w_in[:, hh * 2 * 4096:(hh + 1) * 2 * 4096],
                wstg[:])
        for et in range(32):
            for n in range(4):
                bank = ps0 if (et * 4 + n) % 2 == 0 else ps1
                for k in range(8):
                    p.t("matmul", bank[:],
                        w_in[:, k * 4096 + et * 128: k * 4096 + et * 128 + 128],
                        xlnT[:, k * T + n * 512: k * T + n * 512 + 512],
                        start=(k == 0), stop=(k == 7))
                p.step()
                if et < 16:
                    p.v("tensor_copy",
                        xc[:, et * T + n * 512: et * T + n * 512 + 512], bank[:])
                    p.step()
                else:
                    p.v("tensor_copy", zstg[:], bank[:])
                    p.step()
                    p.dma("sync",
                          z_dram.ap()[(et - 16) * 128:(et - 15) * 128,
                                      n * 512:(n + 1) * 512], zstg[:])
                    p.step()

    with p.scope():
        convw = p.sb(tag + "_convw", [128, 64], F32)
        convb = p.sb(tag + "_convb", [128, 16], F32)
        dtb = p.sb(tag + "_dtb", [128, 16], F32)
        Dl = p.sb(tag + "_Dl", [128, 16], F32)
        Al = p.sb(tag + "_Al", [128, 256], F32)
        p.dma("sync", convw[:], w["conv_w_l"])
        p.dma("sync", convb[:], w["conv_b_l"])
        p.dma("sync", dtb[:], w["dt_b_l"])
        p.dma("sync", Dl[:], w["D_l"])
        p.dma("sync", Al[:], w["A_log_l"])
        p.step()
        p.s("activation", Al[:], Al[:], AF.Exp)
        p.step()
        p.v("tensor_scalar_mul", Al[:], Al[:], -1.0)
        p.step()

        dtt = p.sb(tag + "_dtt", [128, T], F32)
        du_bf = p.sb(tag + "_dubf", [128, T], BF16)
        at2 = [p.sb(tag + "_at", [128, T], BF16) for _ in range(2)]
        bt_bf = p.sb(tag + "_btbf", [128, T], BF16)
        ht_bf = p.sb(tag + "_htbf", [128, T], BF16)
        t1b2 = [p.sb(tag + "_t1b", [128, T], BF16) for _ in range(2)]
        Bsb2 = [p.sb(tag + "_Bsb", [128, T], BF16) for _ in range(2)]
        Csb2 = [p.sb(tag + "_Csb", [128, T], BF16) for _ in range(2)]
        psB2 = [p.ps(tag + "_psB2", [128, 1024], F32) for _ in range(2)]
        psC1 = p.ps(tag + "_psC1", [128, 1024], F32)
        yac = p.sb(tag + "_yac", [128, T], F32)
        t1 = p.sb(tag + "_t1", [128, T], F32)
        ybf = p.sb(tag + "_ybf", [128, T], BF16)
        xdbl_bf = p.sb(tag + "_xdbl", [96, T], BF16)
        dtw = p.sb(tag + "_dtw", [64, 2048], BF16)
        ones_t = p.sb(tag + "_onest", [128, 128], BF16)
        p.v("memset", ones_t[:], 1.0)
        p.step()

        for dt_ in range(NDT):
            xi_t = xc[:, dt_ * T:(dt_ + 1) * T]
            p.v("tensor_scalar_mul", t1[:], xi_t, convw[:, dt_ * 4 + 3:dt_ * 4 + 4])
            p.step()
            for j in range(1, 4):
                p.v("scalar_tensor_tensor", t1[:, j:T], xi_t[:, 0:T - j],
                    convw[:, dt_ * 4 + 3 - j:dt_ * 4 + 4 - j], t1[:, j:T],
                    ALU.mult, ALU.add)
                p.step()
            p.s("activation", xi_t, t1[:], AF.Silu, bias=convb[:, dt_:dt_ + 1])
            p.step()

        with p.scope():
            xw = p.sb(tag + "_xw", [128, 16 * 96], BF16)
            wstg2 = p.sb(tag + "_wstg2", [128, 96], F32)
            for k in range(16):
                p.dma("sync", wstg2[:], w["x_wT"][k * 128:(k + 1) * 128, :])
                p.step()
                p.v("tensor_copy", xw[:, k * 96:(k + 1) * 96], wstg2[:])
                p.step()
            for n in range(4):
                bank = ps0 if n % 2 == 0 else ps1
                for k in range(16):
                    p.t("matmul", bank[0:96, :],
                        xw[:, k * 96:(k + 1) * 96],
                        xc[:, k * T + n * 512: k * T + n * 512 + 512],
                        start=(k == 0), stop=(k == 15))
                p.step()
                p.v("tensor_copy", t1[0:96, n * 512:(n + 1) * 512], bank[0:96, :])
                p.step()
            p.v("tensor_copy", xdbl_bf[:], t1[0:96, :])
            p.step()
            wstg3 = p.sb(tag + "_wstg3", [64, 2048], F32)
            p.dma("sync", wstg3[:], w["dt_wT"][:, :])
            p.step()
            p.v("tensor_copy", dtw[:], wstg3[:])
            p.step()

        bct = [p.sb(tag + f"_bc{i}", [128, T], BF16) for i in range(11)]

        def bc_row(s):
            return bct[s // 3][32 * (s % 3):32 * (s % 3) + 1, :]

        for s in range(32):
            p.dma("gpsimd", bc_row(s), xdbl_bf.ap()[64 + s:65 + s, :])
        p.step()

        for dt_ in range(NDT):
            for n in range(4):
                bank = ps0 if n % 2 == 0 else ps1
                p.t("matmul", bank[:], dtw[:, dt_ * 128:(dt_ + 1) * 128],
                    xdbl_bf.ap()[0:64, n * 512:(n + 1) * 512],
                    start=True, stop=True)
                p.step()
                p.v("tensor_scalar", dtt[:, n * 512:(n + 1) * 512], bank[:],
                    dtb[:, dt_:dt_ + 1], None, ALU.add)
                p.step()
            p.s("activation", t1[:], dtt[:], AF.Abs)
            p.step()
            p.s("activation", t1[:], t1[:], AF.Exp, scale=p.fconst(-1.0)[:])
            p.step()
            p.s("activation", t1[:], t1[:], AF.Ln, bias=p.fconst(1.0)[:])
            p.step()
            p.v("tensor_scalar_max", dtt[:], dtt[:], 0.0)
            p.step()
            p.v("tensor_add", dtt[:], dtt[:], t1[:])
            p.step()
            p.v("tensor_mul", du_bf[:], dtt[:], xc[:, dt_ * T:(dt_ + 1) * T])
            p.step()
            p.v("memset", yac[:], 0.0)
            p.step()
            for s in range(DSTATE):
                par = s % 2
                atp = at2[par]
                psBp = psB2[par]
                Bsb, Csb, t1b = Bsb2[par], Csb2[par], t1b2[par]
                p.s("activation", atp[:], dtt[:], AF.Exp,
                    scale=Al[:, dt_ * 16 + s:dt_ * 16 + s + 1])
                bb = 32 * (s % 3)
                cb = 32 * ((16 + s) % 3)
                for half in range(2):
                    for n in range(2):
                        c0 = half * 1024 + n * 512
                        p.t("matmul", psBp[:, n * 512:(n + 1) * 512],
                            ones_t[bb:bb + 1, :],
                            bc_row(s)[:, c0:c0 + 512], start=True, stop=True)
                    p.s("copy", Bsb[:, half * 1024:(half + 1) * 1024], psBp[:])
                p.v("tensor_mul", bt_bf[:], du_bf[:], Bsb[:])
                p.v("tensor_tensor_scan", ht_bf[:], atp[:], bt_bf[:], 0.0,
                    ALU.mult, ALU.add)
                for half in range(2):
                    for n in range(2):
                        c0 = half * 1024 + n * 512
                        p.t("matmul", psC1[:, n * 512:(n + 1) * 512],
                            ones_t[cb:cb + 1, :],
                            bc_row(16 + s)[:, c0:c0 + 512], start=True, stop=True)
                    p.s("copy", Csb[:, half * 1024:(half + 1) * 1024], psC1[:])
                p.v("tensor_mul", t1b[:], ht_bf[:], Csb[:])
                if s % 2 == 0:
                    p.g("tensor_add", yac[:], yac[:], t1b[:])
                else:
                    p.v("tensor_add", yac[:], yac[:], t1b[:])
            p.v("tensor_copy", ybf[:], yac[:])
            p.step()
            p.dma("sync", yraw_dram.ap()[dt_ * 128:(dt_ + 1) * 128, :], ybf[:])
            p.step()

        for dt_ in range(NDT):
            p.dma("sync", ybf[:], z_dram.ap()[dt_ * 128:(dt_ + 1) * 128, :])
            p.step()
            p.s("activation", t1[:], ybf[:], AF.Silu)
            p.step()
            p.dma("sync", ybf[:], yraw_dram.ap()[dt_ * 128:(dt_ + 1) * 128, :])
            p.step()
            p.v("scalar_tensor_tensor", yac[:], xc[:, dt_ * T:(dt_ + 1) * T],
                Dl[:, dt_:dt_ + 1], ybf[:], ALU.mult, ALU.add)
            p.step()
            p.v("tensor_mul", yac[:], yac[:], t1[:])
            p.step()
            p.v("tensor_copy", ybf[:], yac[:])
            p.step()
            p.dma("sync", yraw_dram.ap()[dt_ * 128:(dt_ + 1) * 128, :], ybf[:])
            p.step()

    with p.scope():
        wo = p.sb(tag + "_wo", [128, 16 * 1024], BF16)
        wo_f = p.sb(tag + "_wo_f", [128, 16 * 1024], F32)
        ycols = p.sb(tag + "_ycols", [128, 16 * 128], BF16)
        outf = p.sb(tag + "_outf", [128, 1024], F32)
        dma_wload(p, wo_f.ap(), w["out_wT"], 2048, 1024)
        p.v("tensor_copy", wo[:], wo_f[:])
        ycols2 = p.sb(tag + "_ycols2", [128, 16 * 128], BF16)
        for tt in range(NT):
            srcv = yraw_dram.ap().rearrange("(a p) t -> p a t", p=128)
            c0 = (T - (tt + 1) * 128) if reverse else tt * 128
            p.dma("sync",
                  ycols.ap().rearrange("p (a t) -> p a t", a=16),
                  srcv[:, :, c0:c0 + 128])
            p.step()
            if reverse:
                for a in range(16):
                    p.v("tensor_copy", ycols2[:, a * 128:(a + 1) * 128],
                        ycols[:, a * 128:(a + 1) * 128][:, ::-1])
                p.step()
                ycols, ycols2 = ycols2, ycols
            for n in range(2):
                bank = ps0 if n == 0 else ps1
                for k in range(16):
                    p.t("matmul", bank[:],
                        ycols[:, k * 128:(k + 1) * 128],
                        wo[:, k * 1024 + n * 512: k * 1024 + n * 512 + 512],
                        start=(k == 0), stop=(k == 15))
                p.step()
                p.v("tensor_copy", outf[:, n * 512:(n + 1) * 512], bank[:])
                p.step()
            p.dma("sync", y_dram.ap()[tt * 128:(tt + 1) * 128, :], outf[:])
            p.step()


def bcast_row_f32(p, row_ap, dst_sb, ps, ones1f, cols=1024):
    stg = p.sb("bcr_stg", [1, cols], F32)
    p.dma("sync", stg[:], row_ap)
    p.step()
    for n in range((cols + 511) // 512):
        c0, c1 = n * 512, min((n + 1) * 512, cols)
        p.t("matmul", ps[:, 0:c1 - c0], ones1f[:], stg[:, c0:c1],
            start=True, stop=True)
        p.step()
        p.v("tensor_copy", dst_sb[:, c0:c1], ps[:, 0:c1 - c0])
        p.step()


def build_l0_attn_call(p, x_in_dram, x_out_dram, wf, wb, ff, tag="l0a"):
    xln_dram = p.dram(tag + "_xln", [T, D], F32)
    yf_dram = p.dram(tag + "_yf", [T, D], F32)
    yb_dram = p.dram(tag + "_yb", [T, D], F32)
    xblk_dram = p.dram(tag + "_xblk", [T, D], F32)

    dsc = p.dram(tag + "_dsc", [T, D], BF16)
    with p.scope():
        xt = p.sb(tag + "_xt", [128, D], F32)
        lnf = p.sb(tag + "_lnf", [128, D], F32)
        hib = p.sb(tag + "_hib", [128, D], BF16)
        sc = ln_scratch(p)
        for t in range(NT):
            p.dma("sync", xt[:], x_in_dram.ap()[t * 128:(t + 1) * 128, :])
            p.step()
            p.v("reduce_sum", sc["m"][:], xt[:], AX.X)
            p.step()
            p.v("tensor_scalar_mul", sc["m"][:], sc["m"][:], 1.0 / D)
            p.step()
            p.s("activation", sc["t1"][:], xt[:], AF.Square, accum_out=sc["sq"][:])
            p.step()
            p.v("tensor_scalar_mul", sc["sq"][:], sc["sq"][:], 1.0 / D)
            p.step()
            p.s("activation", sc["var"][:], sc["m"][:], AF.Square)
            p.step()
            p.v("tensor_sub", sc["var"][:], sc["sq"][:], sc["var"][:])
            p.step()
            p.s("activation", sc["rstd"][:], sc["var"][:], AF.Ln,
                bias=p.fconst(1e-5)[:])
            p.step()
            p.s("activation", sc["rstd"][:], sc["rstd"][:], AF.Exp,
                scale=p.fconst(-0.5)[:])
            p.step()
            p.v("tensor_scalar", lnf[:], xt[:], sc["m"][:], sc["rstd"][:],
                ALU.subtract, ALU.mult)
            p.step()
            p.dma("sync", xln_dram.ap()[t * 128:(t + 1) * 128, :], lnf[:])
            p.step()
            p.v("tensor_copy", hib[:], lnf[:])
            p.step()
            p.dma("sync", dsc.ap()[t * 128:(t + 1) * 128, :], hib[:])
            p.step()

    for dirname, rev, wdir, ydir in (("mf", False, wf, yf_dram),
                                     ("mb", True, wb, yb_dram)):
        with p.scope():
            build_mamba_dir(p, dsc, wdir, ydir, tag + dirname, reverse=rev)

    xffT = p.sb(tag + "_xffT", [128, 8 * T], BF16)
    ps0 = p.ps(tag + "_cps0", [128, 512], F32)
    ps1 = p.ps(tag + "_cps1", [128, 512], F32)
    ones1f = p.sb(tag + "_ones1f", [1, 128], F32)
    p.v("memset", ones1f[:], 1.0)
    p.step()
    g_bc = p.sb(tag + "_gbc", [128, D], F32)
    b_bc = p.sb(tag + "_bbc", [128, D], F32)
    b2_bc = p.sb(tag + "_b2bc", [128, D], F32)
    with p.scope():
        bcast_row_f32(p, ff["ln_g"], g_bc, ps0, ones1f)
        bcast_row_f32(p, ff["ln_b"], b_bc, ps0, ones1f)
        bcast_row_f32(p, ff["b2"], b2_bc, ps0, ones1f)
    with p.scope():
        xt = p.sb(tag + "_cxt", [128, D], F32)
        t2 = p.sb(tag + "_ct2", [128, D], F32)
        hib = p.sb(tag + "_chib", [128, D], BF16)
        sc = ln_scratch(p)
        dsc2 = p.dram(tag + "_dsc2", [T, D], BF16)
        for t in range(NT):
            r0, r1 = t * 128, (t + 1) * 128
            p.dma("sync", xt[:], yf_dram.ap()[r0:r1, :])
            p.step()
            p.dma("sync", t2[:], yb_dram.ap()[r0:r1, :])
            p.step()
            p.v("tensor_add", xt[:], xt[:], t2[:])
            p.step()
            p.dma("sync", t2[:], xln_dram.ap()[r0:r1, :])
            p.step()
            p.v("scalar_tensor_tensor", xt[:], xt[:], 0.5, t2[:],
                ALU.mult, ALU.add)
            p.step()
            p.dma("sync", xblk_dram.ap()[r0:r1, :], xt[:])
            p.step()
            p.v("reduce_sum", sc["m"][:], xt[:], AX.X)
            p.step()
            p.v("tensor_scalar_mul", sc["m"][:], sc["m"][:], 1.0 / D)
            p.step()
            p.s("activation", sc["t1"][:], xt[:], AF.Square, accum_out=sc["sq"][:])
            p.step()
            p.v("tensor_scalar_mul", sc["sq"][:], sc["sq"][:], 1.0 / D)
            p.step()
            p.s("activation", sc["var"][:], sc["m"][:], AF.Square)
            p.step()
            p.v("tensor_sub", sc["var"][:], sc["sq"][:], sc["var"][:])
            p.step()
            p.s("activation", sc["rstd"][:], sc["var"][:], AF.Ln,
                bias=p.fconst(1e-5)[:])
            p.step()
            p.s("activation", sc["rstd"][:], sc["rstd"][:], AF.Exp,
                scale=p.fconst(-0.5)[:])
            p.step()
            p.v("tensor_scalar", t2[:], xt[:], sc["m"][:], sc["rstd"][:],
                ALU.subtract, ALU.mult)
            p.step()
            p.v("tensor_mul", t2[:], t2[:], g_bc[:])
            p.step()
            p.v("tensor_add", t2[:], t2[:], b_bc[:])
            p.step()
            p.v("tensor_copy", hib[:], t2[:])
            p.step()
            p.dma("sync", dsc2.ap()[t * 128:(t + 1) * 128, :], hib[:])
            p.step()
        for d8 in range(8):
            p.dma("sync", xffT[:, d8 * T:(d8 + 1) * T],
                  dsc2.ap()[:, d8 * 128:(d8 + 1) * 128], transpose=True)
            p.step()

    TH = 1024
    h1 = p.sb(tag + "_h1", [128, 32 * TH], BF16)
    b1l = p.sb(tag + "_b1l", [128, 32], F32)
    p.dma("sync", b1l[:], ff["b1_l"])
    p.step()
    with p.scope():
        w1c = p.sb(tag + "_w1c", [128, 8 * 128], BF16)
        wstg1k = p.sb(tag + "_wstg5", [128, 8 * 128], F32)
        w2h = p.sb(tag + "_w2h", [128, 32 * 512], BF16)
        w2hf = p.sb(tag + "_w2hf", [128, 16 * 512], F32)
        wstg6 = p.sb(tag + "_wstg6", [128, 512], F32)
        xres = p.sb(tag + "_xres", [128, 512], F32)
        for h in range(2):
            tsl = slice(h * TH, (h + 1) * TH)
            for ft in range(32):
                dma_wload(p, wstg1k.ap(), ff["w1T"][:, ft * 128:(ft + 1) * 128],
                          1024, 128)
                p.v("tensor_copy", w1c[:], wstg1k[:])
                for n in range(2):
                    bank = ps0 if n == 0 else ps1
                    for k in range(8):
                        p.t("matmul", bank[:],
                            w1c[:, k * 128:(k + 1) * 128],
                            xffT[:, k * T + h * TH + n * 512:
                                 k * T + h * TH + n * 512 + 512],
                            start=(k == 0), stop=(k == 7))
                    p.step()
                    p.s("activation",
                        h1[:, ft * TH + n * 512: ft * TH + n * 512 + 512],
                        bank[:], AF.Silu, bias=b1l[:, ft:ft + 1])
                    p.step()
            for nh in range(2):
                for hh in range(2):
                    dma_wload(p, w2hf.ap(),
                              ff["w2T"][hh * 2048:(hh + 1) * 2048,
                                        nh * 512:(nh + 1) * 512], 2048, 512)
                    p.v("tensor_copy",
                        w2h[:, hh * 16 * 512:(hh + 1) * 16 * 512], w2hf[:])
                for tt in range(8):
                    r0 = (h * 8 + tt) * 128
                    bank = ps0 if tt % 2 == 0 else ps1
                    for k in range(32):
                        p.t("matmul", bank[:],
                            h1[:, k * TH + tt * 128: k * TH + tt * 128 + 128],
                            w2h[:, k * 512:(k + 1) * 512],
                            start=(k == 0), stop=(k == 31))
                    p.step()
                    p.dma("sync", xres[:],
                          x_in_dram.ap()[r0:r0 + 128, nh * 512:(nh + 1) * 512])
                    p.step()
                    p.v("tensor_add", xres[:], xres[:], bank[:])
                    p.step()
                    p.v("tensor_add", xres[:], xres[:],
                        b2_bc[:, nh * 512:(nh + 1) * 512])
                    p.step()
                    p.dma("sync", wstg6[:],
                          xblk_dram.ap()[r0:r0 + 128, nh * 512:(nh + 1) * 512])
                    p.step()
                    p.v("tensor_add", xres[:], xres[:], wstg6[:])
                    p.step()
                    p.dma("sync",
                          x_out_dram.ap()[r0:r0 + 128, nh * 512:(nh + 1) * 512],
                          xres[:])
                    p.step()



NE = 32
CAP = 1024
NSLOT = NE * CAP
TRASH = NSLOT


def build_l0_router(p, x_in_dram, gwT_ap, xln_out, hloc_out,
                    expv_out, tag):
    xlnT_hi = p.sb(tag + "_Thi", [128, 8 * T], BF16)
    xlnT_lo = p.sb(tag + "_Tlo", [128, 8 * T], BF16)
    with p.scope():
        build_ln_and_transpose_into(p, x_in_dram, xlnT_hi, xlnT_lo, tag + "ln",
                                    tok_out_dram=xln_out)
    rhi = p.sb(tag + "_rhi", [128, 8 * 32], BF16)
    rlo = p.sb(tag + "_rlo", [128, 8 * 32], BF16)
    with p.scope():
        rstg = p.sb(tag + "_rstg", [128, 32], F32)
        rtmp = p.sb(tag + "_rtmp", [128, 32], F32)
        for k in range(8):
            ks = slice(k * 32, (k + 1) * 32)
            p.dma("sync", rstg[:], gwT_ap[k * 128:(k + 1) * 128, :])
            p.step()
            p.v("tensor_copy", rhi[:, ks], rstg[:])
            p.step()
            p.v("tensor_copy", rtmp[:], rhi[:, ks])
            p.step()
            p.v("tensor_sub", rtmp[:], rstg[:], rtmp[:])
            p.step()
            p.v("tensor_copy", rlo[:, ks], rtmp[:])
            p.step()
    lg_ps = p.ps(tag + "_lgps", [128, 32], F32)
    lg = p.sb(tag + "_lg", [128, 32], F32)
    ex = p.sb(tag + "_ex", [128, 32], F32)
    m8 = p.sb(tag + "_m8", [128, 8], F32)
    Hgt = p.sb(tag + "_Hgt", [128, 32], F32)
    Tt = p.sb(tag + "_Tt", [128, 32], F32)
    scn = p.sb(tag + "_scn", [128, 32], F32)
    cnt = p.sb(tag + "_cnt", [128, 1], F32)
    mx = p.sb(tag + "_mx", [128, 1], F32)
    lim = p.sb(tag + "_lim", [128, 1], F32)
    Hb = p.sb(tag + "_Hb", [128, 32], BF16)
    for t in range(NT):
        ts = slice(t * 128, (t + 1) * 128)
        for k in range(8):
            ks = slice(k * 32, (k + 1) * 32)
            xs_ = slice(k * T + t * 128, k * T + t * 128 + 128)
            p.t("matmul", lg_ps[:], xlnT_hi[:, xs_], rhi[:, ks],
                start=(k == 0), stop=False)
            p.t("matmul", lg_ps[:], xlnT_hi[:, xs_], rlo[:, ks],
                start=False, stop=False)
            p.t("matmul", lg_ps[:], xlnT_lo[:, xs_], rhi[:, ks],
                start=False, stop=(k == 7))
        p.step()
        p.v("tensor_copy", lg[:], lg_ps[:])
        p.step()
        p.v("reduce_max", mx[:], lg[:], AX.X)
        p.step()
        p.v("tensor_scalar_mul", mx[:], mx[:], -1.0)
        p.step()
        p.s("activation", ex[:], lg[:], AF.Exp, bias=mx[:])
        p.step()
        p.dma("sync", expv_out.ap()[ts, :], ex[:])
        p.step()
        p.v("max", out=m8[:], in_=lg[:])
        p.step()
        p.v("tensor_scalar", Hgt[:], lg[:], m8[:, 7:8], None, ALU.is_gt)
        p.step()
        p.v("tensor_scalar", Tt[:], lg[:], m8[:, 7:8], None, ALU.is_equal)
        p.step()
        p.v("tensor_reduce", cnt[:], Hgt[:], AX.X, ALU.add)
        p.step()
        p.v("tensor_tensor_scan", scn[:], Tt[:], Tt[:], 0.0, ALU.add, ALU.bypass)
        p.step()
        p.v("tensor_scalar", lim[:], cnt[:], -1.0, 8.0, ALU.mult, ALU.add)
        p.step()
        p.v("tensor_scalar", scn[:], scn[:], lim[:], None, ALU.is_le)
        p.step()
        p.v("tensor_mul", scn[:], scn[:], Tt[:])
        p.step()
        p.v("tensor_add", Hgt[:], Hgt[:], scn[:])
        p.step()
        p.v("tensor_copy", Hb[:], Hgt[:])
        p.step()
        p.dma("sync", hloc_out.ap()[ts, :], Hb[:])
        p.step()


def build_l0_s2(p, hfull_ap, hloc_ap, expv_ap, tokid_ap, rows_ap,
                ptable_out, gidx_out, w_out, tag):
    N = 8 * T
    with p.scope():
        zt = p.sb(tag + "_zt", [128, 257], I32)
        p.v("memset", zt[:], 0)
        p.step()
        p.dma("sync",
              ptable_out.ap().rearrange("(p a) w -> p (a w)", p=128), zt[:])
        p.step()
    pos_all = p.dram(tag + "_posall", [N, NE], F32)
    with p.scope():
        incl = p.sb(tag + "_incl", [32, N], F32)
        hf = p.sb(tag + "_hf", [32, N], BF16)
        p.dma("sync", hf[:], hfull_ap)
        p.step()
        p.v("tensor_tensor_scan", incl[:], hf[:], hf[:], 0.0, ALU.add,
            ALU.bypass)
        p.step()
        p.v("tensor_sub", incl[:], incl[:], hf[:])
        p.step()
        identf = p.sb(tag + "_identf", [128, 128], F32)
        p.dma("sync", identf[:],
              p.const("identf128", np.eye(128, dtype=np.float32)).ap()[:])
        p.step()
        psT = p.ps(tag + "_psT", [128, 128], F32)
        ptk = p.sb(tag + "_ptk", [128, 32], F32)
        for t in range(N // 128):
            p.t("transpose", psT[:, 0:32], incl[:, t * 128:(t + 1) * 128],
                identf[0:32, 0:32])
            p.step()
            p.v("tensor_copy", ptk[:], psT[:, 0:32])
            p.step()
            p.dma("sync", pos_all.ap()[t * 128:(t + 1) * 128, :], ptk[:])
            p.step()
    pos_tok = p.sb(tag + "_postok", [128, 32], F32)
    htok = p.sb(tag + "_htok", [128, 32], F32)
    htokb = p.sb(tag + "_htokb", [128, 32], BF16)
    expv = p.sb(tag + "_expv", [128, 32], F32)
    hkept = p.sb(tag + "_hkept", [128, 32], F32)
    r2 = p.sb(tag + "_r2", [128, 32], F32)
    A = p.sb(tag + "_A", [128, 32], F32)
    big_t = p.sb(tag + "_big", [128, 32], F32)
    e1024 = p.sb(tag + "_e1024", [128, 32], F32)
    trash32 = p.sb(tag + "_trash", [128, 32], F32)
    p.v("memset", big_t[:], 2.0e8)
    p.step()
    p.dma("sync", e1024[:],
          p.const("e1024", np.broadcast_to(
              (np.arange(32, dtype=np.float32) * CAP)[None, :],
              (128, 32)).copy()).ap()[:])
    p.step()
    gidx_f = p.sb(tag + "_gidxf", [128, 8], F32)
    praw = p.sb(tag + "_praw", [128, 8], F32)
    keptk = p.sb(tag + "_keptk", [128, 8], F32)
    wsum = p.sb(tag + "_wsum", [128, 1], F32)
    wt = p.sb(tag + "_wt", [128, 8], F32)
    gidx_i = p.sb(tag + "_gidxi", [128, 8], I32)
    sidx_f = p.sb(tag + "_sidxf", [128, 8], F32)
    sidx_i = p.sb(tag + "_sidxi", [128, 8], I32)
    tki = p.sb(tag + "_tki", [128, 1], I32)
    rws = p.sb(tag + "_rws", [128, 1], I32)
    for t in range(NT):
        ts = slice(t * 128, (t + 1) * 128)
        p.dma("sync", rws[:], rows_ap[ts, :])
        p.step()
        p.g("indirect_dma_start", pos_tok[:], None, pos_all.ap()[:, :],
            IndirectOffsetOnAxis(rws.ap() if False else rws[:], 0))
        p.step()
        p.dma("sync", htokb[:], hloc_ap[ts, :])
        p.step()
        p.v("tensor_copy", htok[:], htokb[:])
        p.step()
        p.dma("sync", expv[:], expv_ap[ts, :])
        p.step()
        p.dma("sync", tki[:], tokid_ap[ts, :])
        p.step()
        p.v("tensor_scalar", hkept[:], pos_tok[:], float(CAP) - 0.5, None,
            ALU.is_lt)
        p.step()
        p.v("tensor_mul", hkept[:], hkept[:], htok[:])
        p.step()
        p.v("tensor_tensor_scan", r2[:], htok[:], htok[:], 0.0, ALU.add,
            ALU.bypass)
        p.step()
        p.v("tensor_mul", r2[:], r2[:], htok[:])
        p.step()
        p.v("scalar_tensor_tensor", A[:], pos_tok[:], 1.0, e1024[:],
            ALU.add, ALU.add)
        p.step()
        p.v("tensor_mul", A[:], A[:], hkept[:])
        p.step()
        p.v("tensor_scalar", trash32[:], hkept[:], -2.0e8, 2.0e8,
            ALU.mult, ALU.add)
        p.step()
        p.v("tensor_add", A[:], A[:], trash32[:])
        p.step()
        for k in range(8):
            p.v("scalar_tensor_tensor", trash32[:], r2[:],
                float(k + 1), A[:], ALU.is_equal, ALU.mult,
                accum_out=gidx_f[:, k:k + 1])
            p.step()
            p.v("scalar_tensor_tensor", trash32[:], r2[:],
                float(k + 1), expv[:], ALU.is_equal, ALU.mult,
                accum_out=praw[:, k:k + 1])
            p.step()
            p.v("scalar_tensor_tensor", trash32[:], r2[:],
                float(k + 1), hkept[:], ALU.is_equal, ALU.mult,
                accum_out=keptk[:, k:k + 1])
            p.step()
        p.v("tensor_reduce", wsum[:], praw[:], AX.X, ALU.add)
        p.step()
        p.v("reciprocal", wsum[:], wsum[:])
        p.step()
        p.v("tensor_scalar_mul", wt[:], praw[:], wsum[:])
        p.step()
        p.v("tensor_mul", wt[:], wt[:], keptk[:])
        p.step()
        p.dma("sync", w_out.ap()[ts, :], wt[:])
        p.step()
        p.v("tensor_scalar", sidx_f[:], gidx_f[:], -1.0, float(TRASH),
            ALU.add, ALU.min)
        p.step()
        p.v("tensor_copy", sidx_i[:], sidx_f[:])
        p.step()
        p.v("tensor_scalar", gidx_f[:], gidx_f[:], -1.0, 0.0, ALU.add, ALU.max)
        p.step()
        p.v("tensor_scalar_min", gidx_f[:], gidx_f[:], float(NSLOT - 1))
        p.step()
        p.v("tensor_copy", gidx_i[:], gidx_f[:])
        p.step()
        p.dma("sync", gidx_out.ap()[ts, :], gidx_i[:])
        p.step()
        for k in range(8):
            p.g("indirect_dma_start", ptable_out.ap()[:, :],
                IndirectOffsetOnAxis(sidx_i[:, k:k + 1], 0), tki[:], None,
                grp="ptscatter")
        p.step()


def build_l0_s3(p, ptable_my_ap, xlnfull_dram, gupT_ap, downT_ap, eo_out, tag):
    tblf = p.sb(tag + "_tblf", [32, 128], F32)
    tmp = p.sb(tag + "_tmp", [32, 128], F32)
    tmpi = p.sb(tag + "_tmpi", [32, 128], I32)
    identf = p.sb(tag + "_idf", [128, 128], F32)
    p.dma("sync", identf[:],
          p.const("identf128", np.eye(128, dtype=np.float32)).ap()[:])
    p.step()
    psT = p.ps(tag + "_psT", [128, 128], F32)
    p.v("memset", tblf[:], 0.0)
    p.step()
    for r in range(8):
        p.dma("sync", tmpi[:],
              ptable_my_ap[r:r + 1, :].rearrange("w (p a) -> (w p) a", p=32))
        p.step()
        p.v("tensor_copy", tmp[:], tmpi[:])
        p.step()
        p.v("tensor_add", tblf[:], tblf[:], tmp[:])
        p.step()
    p.v("tensor_scalar", tblf[:], tblf[:], -1.0, 0.0, ALU.add, ALU.max)
    p.step()
    p.v("tensor_scalar_min", tblf[:], tblf[:], float(8 * T - 1))
    p.step()
    p.t("transpose", psT[:, 0:32], tblf[:], identf[0:32, 0:32])
    p.step()
    idxf = p.sb(tag + "_idxf", [128, 32], F32)
    idxi = p.sb(tag + "_idxi", [128, 32], I32)
    p.v("tensor_copy", idxf[:], psT[:, 0:32])
    p.step()
    p.v("tensor_copy", idxi[:], idxf[:])
    p.step()

    xrows = p.sb(tag + "_xrows", [128, D], BF16)
    xstage = p.dram(tag + "_xstage", [128, D], BF16)
    xeT = p.sb(tag + "_xeT", [128, 8 * 128], BF16)
    wgup = p.sb(tag + "_wgup", [128, 8 * 1536], BF16)
    wdn = p.sb(tag + "_wdn", [128, 6 * 1024], BF16)
    wstg8 = p.sb(tag + "_wstg8", [128, 8 * 1536], F32)
    gs = p.sb(tag + "_gs", [128, 6 * 128], BF16)
    eob = p.sb(tag + "_eob", [128, 1024], BF16)
    ps0 = p.ps(tag + "_ps0", [128, 512], F32)
    ps1 = p.ps(tag + "_ps1", [128, 512], F32)
    for e in range(NE):
        p.g("indirect_dma_start", xrows[:], None, xlnfull_dram.ap()[:, :],
            IndirectOffsetOnAxis(idxi[:, e:e + 1], 0))
        p.step()
        p.dma("sync", xstage.ap()[:, :], xrows[:])
        p.step()
        for d8 in range(8):
            p.dma("sync", xeT[:, d8 * 128:(d8 + 1) * 128],
                  xstage.ap()[:, d8 * 128:(d8 + 1) * 128], transpose=True)
            p.step()
        dma_wload(p, wstg8.ap(), gupT_ap[e], 1024, 1536)
        p.v("tensor_copy", wgup[:], wstg8[:])
        for mt in range(12):
            bank = ps0 if mt % 2 == 0 else ps1
            for k in range(8):
                p.t("matmul", bank[:, 0:128],
                    wgup[:, k * 1536 + mt * 128: k * 1536 + mt * 128 + 128],
                    xeT[:, k * 128:(k + 1) * 128],
                    start=(k == 0), stop=(k == 7))
            p.step()
            if mt < 6:
                p.s("activation", gs[:, mt * 128:(mt + 1) * 128],
                    bank[:, 0:128], AF.Silu)
            else:
                f = mt - 6
                p.v("tensor_mul", gs[:, f * 128:(f + 1) * 128],
                    bank[:, 0:128], gs[:, f * 128:(f + 1) * 128])
            p.step()
        dma_wload(p, wstg8.ap()[:, 0:6 * 1024], downT_ap[e], 768, 1024)
        p.v("tensor_copy", wdn[:], wstg8[:, 0:6 * 1024])
        for n in range(2):
            bank = ps0 if n == 0 else ps1
            for k in range(6):
                p.t("matmul", bank[:],
                    gs[:, k * 128:(k + 1) * 128],
                    wdn[:, k * 1024 + n * 512: k * 1024 + n * 512 + 512],
                    start=(k == 0), stop=(k == 5))
            p.step()
            p.v("tensor_copy", eob[:, n * 512:(n + 1) * 512], bank[:])
            p.step()
        p.dma("sync", eo_out.ap()[e * 128:(e + 1) * 128, :], eob[:])
        p.step()


def build_l0_s4(p, x_in_dram, eofull_dram, gidx_ap, w_ap, x_out_dram, tag):
    acc = p.sb(tag + "_acc", [128, D], F32)
    grows = p.sb(tag + "_grows", [128, D], BF16)
    wt = p.sb(tag + "_wt", [128, 8], F32)
    gix = p.sb(tag + "_gix", [128, 8], I32)
    xres = p.sb(tag + "_xres", [128, D], F32)
    for t in range(NT):
        ts = slice(t * 128, (t + 1) * 128)
        p.dma("sync", wt[:], w_ap[ts, :])
        p.step()
        p.dma("sync", gix[:], gidx_ap[ts, :])
        p.step()
        p.v("memset", acc[:], 0.0)
        p.step()
        for k in range(8):
            p.g("indirect_dma_start", grows[:], None, eofull_dram.ap()[:, :],
                IndirectOffsetOnAxis(gix[:, k:k + 1], 0))
            p.step()
            p.v("scalar_tensor_tensor", acc[:], grows[:], wt[:, k:k + 1],
                acc[:], ALU.mult, ALU.add)
            p.step()
        p.dma("sync", xres[:], x_in_dram.ap()[ts, :])
        p.step()
        p.v("scalar_tensor_tensor", xres[:], acc[:], 0.5, xres[:],
            ALU.mult, ALU.add)
        p.step()
        p.dma("sync", x_out_dram.ap()[ts, :], xres[:])
        p.step()



def _runp(p, in_maps):
    return p.run(in_maps)


def _mamba_host_layout(pref, I):
    def vec_l(v, w_=16):
        return np.ascontiguousarray(v.reshape(w_, 128).T)
    return {
        "in_wT": np.ascontiguousarray(I[pref + "in_w"].T),
        "x_wT": np.ascontiguousarray(I[pref + "x_w"].T),
        "dt_wT": np.ascontiguousarray(I[pref + "dt_w"].T),
        "out_wT": np.ascontiguousarray(I[pref + "out_w"].T),
        "conv_w_l": np.ascontiguousarray(
            I[pref + "conv_w"].reshape(16, 128, 4).transpose(1, 0, 2)
            .reshape(128, 64)),
        "conv_b_l": vec_l(I[pref + "conv_b"]),
        "dt_b_l": vec_l(I[pref + "dt_b"]),
        "D_l": vec_l(I[pref + "D"]),
        "A_log_l": np.ascontiguousarray(
            I[pref + "A_log"].reshape(16, 128, 16).transpose(1, 0, 2)
            .reshape(128, 256)),
    }


def _pad_qk(W):
    WT = np.ascontiguousarray(W.T)
    out = np.zeros((1024, 512), np.float32)
    for h in range(16):
        out[:, 32 * h:32 * h + 16] = WT[:, 16 * h:16 * h + 16]
    return out


MAMBA_W_NAMES = ["in_wT", "x_wT", "dt_wT", "out_wT", "conv_w_l", "conv_b_l",
                 "dt_b_l", "D_l", "A_log_l"]
MAMBA_W_SHAPES = {"in_wT": [1024, 4096], "x_wT": [2048, 96],
                  "dt_wT": [64, 2048], "out_wT": [2048, 1024],
                  "conv_w_l": [128, 64], "conv_b_l": [128, 16],
                  "dt_b_l": [128, 16], "D_l": [128, 16],
                  "A_log_l": [128, 256]}
N_ALL = 8 * T


def _launch_s1(x_shards, gwT, tagname):
    p = Prog(tagname)
    x_in = p.dram("x", [T, D], F32, kind="ExternalInput")
    gw = p.dram("gwT", [1024, 32], F32, kind="ExternalInput")
    xln_o = p.dram("xln", [T, D], BF16, kind="ExternalOutput")
    hloc_o = p.dram("hloc", [T, 32], BF16, kind="ExternalOutput")
    expv_o = p.dram("expv", [T, 32], F32, kind="ExternalOutput")
    build_l0_router(p, x_in, gw.ap(), xln_o, hloc_o, expv_o, "r")
    return p.run([{"x": x_shards[c], "gwT": gwT} for c in range(8)])


def _launch_s2(r1, tagname):
    hfull = np.concatenate([r1[c]["hloc"] for c in range(8)], axis=0)
    hfullT = np.ascontiguousarray(hfull.T)
    p = Prog(tagname)
    hf_in = p.dram("hfull", [32, N_ALL], BF16, kind="ExternalInput")
    hl_in = p.dram("hloc", [T, 32], BF16, kind="ExternalInput")
    ev_in = p.dram("expv", [T, 32], F32, kind="ExternalInput")
    tk_in = p.dram("tokid", [T, 1], I32, kind="ExternalInput")
    rw_in = p.dram("rows", [T, 1], I32, kind="ExternalInput")
    pt_o = p.dram("ptable", [NSLOT + 128, 1], I32, kind="ExternalOutput")
    gi_o = p.dram("gidx", [T, 8], I32, kind="ExternalOutput")
    w_o = p.dram("w", [T, 8], F32, kind="ExternalOutput")
    build_l0_s2(p, hf_in.ap(), hl_in.ap(), ev_in.ap(), tk_in.ap(), rw_in.ap(),
                pt_o, gi_o, w_o, "s2")
    return p.run([{
        "hfull": hfullT, "hloc": r1[c]["hloc"], "expv": r1[c]["expv"],
        "tokid": (np.arange(T) + c * T + 1).reshape(-1, 1).astype(np.int32),
        "rows": (np.arange(T) + c * T).reshape(-1, 1).astype(np.int32),
    } for c in range(8)])


def _launch_s3(r1, r2, gupT, downT, tagname):
    ptables = np.stack([r2[c]["ptable"][:, 0] for c in range(8)])
    xlnfull = np.concatenate([r1[c]["xln"] for c in range(8)], axis=0)
    pt_my = []
    for c in range(8):
        sel = np.zeros((8, 32, 128), np.int32)
        for e in range(NE):
            sel[:, e, :] = ptables[:, e * CAP + c * 128: e * CAP + (c + 1) * 128]
        pt_my.append(sel.reshape(8, 4096))
    p = Prog(tagname)
    ptm_in = p.dram("ptm", [8, 4096], I32, kind="ExternalInput")
    xlf_in = p.dram("xlnfull", [N_ALL, D], BF16, kind="ExternalInput")
    gu_in = p.dram("gupT", [32, 1024, 1536], F32, kind="ExternalInput")
    dn_in = p.dram("downT", [32, 768, 1024], F32, kind="ExternalInput")
    eo_o = p.dram("eo", [4096, D], BF16, kind="ExternalOutput")
    build_l0_s3(p, ptm_in.ap(), xlf_in, gu_in.ap(), dn_in.ap(), eo_o, "s3")
    return p.run([{"ptm": pt_my[c], "xlnfull": xlnfull, "gupT": gupT,
                   "downT": downT} for c in range(8)])


def _eo_full(r3):
    eof = np.zeros((NSLOT, D), bf16)
    for c in range(8):
        part = r3[c]["eo"].reshape(NE, 128, D)
        for e in range(NE):
            eof[e * CAP + c * 128: e * CAP + (c + 1) * 128] = part[e]
    return eof


def kernel(**I):
    x = np.ascontiguousarray(I["L0_attn_x"]).astype(np.float32, copy=False)
    x_shards = [np.ascontiguousarray(x[c]) for c in range(8)]
    gwT = np.ascontiguousarray(I["L0_ffn_gate_weight"].T)
    gupT = np.ascontiguousarray(np.asarray(I["L0_ffn_gate_up_proj"])
                                .transpose(0, 2, 1))
    downT = np.ascontiguousarray(np.asarray(I["L0_ffn_down_proj"])
                                 .transpose(0, 2, 1))
    wf_h = _mamba_host_layout("L0_attn_fwd_", I)
    wb_h = _mamba_host_layout("L0_attn_bwd_", I)
    ff_h = {
        "w1T": np.ascontiguousarray(I["L0_attn_ff_w1"].T),
        "w2T": np.ascontiguousarray(I["L0_attn_ff_w2"].T),
        "b1_l": np.ascontiguousarray(I["L0_attn_ff_b1"].reshape(32, 128).T),
        "ln_g": np.ascontiguousarray(I["L0_attn_ff_ln_g"].reshape(1, -1)),
        "ln_b": np.ascontiguousarray(I["L0_attn_ff_ln_b"].reshape(1, -1)),
        "b2": np.ascontiguousarray(I["L0_attn_ff_b2"].reshape(1, -1)),
    }
    FF_SHAPES = {"w1T": [1024, 4096], "w2T": [4096, 1024], "b1_l": [128, 32],
                 "ln_g": [1, 1024], "ln_b": [1, 1024], "b2": [1, 1024]}

    r1 = _launch_s1(x_shards, gwT, "L1s1")
    r2 = _launch_s2(r1, "L2s2")
    r3 = _launch_s3(r1, r2, gupT, downT, "L3s3")
    eof1 = _eo_full(r3)

    p = Prog("L4big")
    x_in = p.dram("x", [T, D], F32, kind="ExternalInput")
    eo_in = p.dram("eofull", [NSLOT, D], BF16, kind="ExternalInput")
    gi_in = p.dram("gidx", [T, 8], I32, kind="ExternalInput")
    w_in = p.dram("w", [T, 8], F32, kind="ExternalInput")
    gw_in = p.dram("gwT", [1024, 32], F32, kind="ExternalInput")
    wf = {n: p.dram("f_" + n, MAMBA_W_SHAPES[n], F32,
                    kind="ExternalInput").ap() for n in MAMBA_W_NAMES}
    wb = {n: p.dram("b_" + n, MAMBA_W_SHAPES[n], F32,
                    kind="ExternalInput").ap() for n in MAMBA_W_NAMES}
    ff = {n: p.dram("ff_" + n, FF_SHAPES[n], F32,
                    kind="ExternalInput").ap() for n in FF_SHAPES}
    x2_o = p.dram("x2", [T, D], F32, kind="ExternalOutput")
    xln_o = p.dram("xln", [T, D], BF16, kind="ExternalOutput")
    hloc_o = p.dram("hloc", [T, 32], BF16, kind="ExternalOutput")
    expv_o = p.dram("expv", [T, 32], F32, kind="ExternalOutput")
    x1_d = p.dram("x1scr", [T, D], F32)
    with p.scope():
        build_l0_s4(p, x_in, eo_in, gi_in.ap(), w_in.ap(), x1_d, "s4")
    with p.scope():
        build_l0_attn_call(p, x1_d, x2_o, wf, wb, ff)
    with p.scope():
        build_l0_router(p, x2_o, gw_in.ap(), xln_o, hloc_o, expv_o, "r2")
    im4 = []
    for c in range(8):
        m = {"x": x_shards[c], "eofull": eof1, "gidx": r2[c]["gidx"],
             "w": r2[c]["w"], "gwT": gwT}
        for n in MAMBA_W_NAMES:
            m["f_" + n] = wf_h[n]
            m["b_" + n] = wb_h[n]
        for n in FF_SHAPES:
            m["ff_" + n] = ff_h[n]
        im4.append(m)
    r4 = _runp(p, im4)

    r5 = _launch_s2(r4, "L5s2")
    r6 = _launch_s3(r4, r5, gupT, downT, "L6s3")
    eof2 = _eo_full(r6)

    p = Prog("L7big")
    x_in = p.dram("x", [T, D], F32, kind="ExternalInput")
    eo_in = p.dram("eofull", [NSLOT, D], BF16, kind="ExternalInput")
    gi_in = p.dram("gidx", [T, 8], I32, kind="ExternalInput")
    w_in = p.dram("w", [T, 8], F32, kind="ExternalInput")
    rt_in = p.dram("routerDE", [1024, 8], F32, kind="ExternalInput")
    ga_in = p.dram("gating", [8, 1024, 2048], F32, kind="ExternalInput")
    up_in = p.dram("upw", [8, 1024, 2048], F32, kind="ExternalInput")
    dn_in = p.dram("downw", [8, 2048, 1024], F32, kind="ExternalInput")
    wq_in = p.dram("wqTp", [1024, 512], F32, kind="ExternalInput")
    wk_in = p.dram("wkTp", [1024, 512], F32, kind="ExternalInput")
    wv_in = p.dram("wvT", [1024, 1024], F32, kind="ExternalInput")
    wo_in = p.dram("woT", [1024, 1024], F32, kind="ExternalInput")
    out_o = p.dram("out", [T, D], F32, kind="ExternalOutput")
    x3_d = p.dram("x3scr", [T, D], F32)
    x4_d = p.dram("x4scr", [T, D], F32)
    x5_d = p.dram("x5scr", [T, D], F32)
    ga_bf = p.dram("ga_bf", [8, 1024, 2048], BF16)
    up_bf = p.dram("up_bf", [8, 1024, 2048], BF16)
    dn_bf = p.dram("dn_bf", [8, 2048, 1024], BF16)
    with p.scope():
        build_l0_s4(p, x_in, eo_in, gi_in.ap(), w_in.ap(), x3_d, "s4b")
    with p.scope():
        build_l1_ffn_call(p, x3_d, x4_d, rt_in.ap(), ga_bf, up_bf, dn_bf,
                          tag="l1fa", cast_srcs=(ga_in, up_in, dn_in))
    with p.scope():
        build_l1_attn_call(p, x4_d, x5_d, wq_in.ap(), wk_in.ap(), wv_in.ap(),
                           wo_in.ap(), tag="l1at")
    with p.scope():
        build_l1_ffn_call(p, x5_d, out_o, rt_in.ap(), ga_bf, up_bf, dn_bf,
                          tag="l1fb")
    im7 = []
    l1w = {
        "routerDE": np.ascontiguousarray(I["L1_ffn_kernel_router_DE"]),
        "gating": np.ascontiguousarray(I["L1_ffn_kernel_gating_EDF"]),
        "upw": np.ascontiguousarray(I["L1_ffn_kernel_up_EDF"]),
        "downw": np.ascontiguousarray(I["L1_ffn_kernel_down_EFD"]),
        "wqTp": _pad_qk(np.asarray(I["L1_attn_Wq"])),
        "wkTp": _pad_qk(np.asarray(I["L1_attn_Wk"])),
        "wvT": np.ascontiguousarray(I["L1_attn_Wv"].T),
        "woT": np.ascontiguousarray(I["L1_attn_Wo"].T),
    }
    for c in range(8):
        m = {"x": r4[c]["x2"], "eofull": eof2, "gidx": r5[c]["gidx"],
             "w": r5[c]["w"]}
        m.update(l1w)
        im7.append(m)
    r7 = _runp(p, im7)
    return np.stack([r7[c]["out"] for c in range(8)]).astype(np.float32)
